# version 17 (final)
# speedup vs baseline: 1.0230x; 1.0033x over previous
import numpy as np
from contextlib import ExitStack

import os as _os
import bass_rust
import concourse.bass as bass
import concourse.mybir as mybir
from concourse import library_config
from concourse.bass_utils import run_bass_kernel_spmd

F32 = mybir.dt.float32
BF16 = mybir.dt.bfloat16
I32 = mybir.dt.int32
ALU = mybir.AluOpType
ACT = mybir.ActivationFunctionType
AXX = mybir.AxisListType.X
RED = bass_rust.ReduceOp

NEG_BIG = -1.0e30
RND_C = 12582912.0

FULL_CFG = dict(T=2048, DM=1024)


_ELIDE_DRAIN_ENGINES = set()
_FULL_ELIDE = False


class _LazyDrain:

    __slots__ = ("_px",)

    def __init__(self, px):
        self._px = px

    def then_inc(self, sem, value=1, **k):
        px = self._px
        P = px._P
        if _FULL_ELIDE:
            li = P.last_inst.get(px._en)
            if li is not None and id(li) not in P._inced:
                li.then_inc(sem, value)
                P._inced.add(id(li))
                return li
        d = px._e.drain()
        d.then_inc(sem, value)
        return d


class _EngProxy:

    def __init__(self, e, P, en):
        self._e = e
        self._P = P
        self._en = en

    def drain(self):
        if self._en in _ELIDE_DRAIN_ENGINES:
            return _LazyDrain(self)
        return self._e.drain()

    def wait_ge(self, *a, **k):
        self._P.last_inst[self._en] = None
        return self._e.wait_ge(*a, **k)

    def __getattr__(self, name):
        attr = getattr(self._e, name)
        if not callable(attr):
            return attr
        P, en = self._P, self._en

        def call(*a, **k):
            r = attr(*a, **k)
            if r is not None and hasattr(r, "then_inc"):
                P.last_inst[en] = r
            return r
        return call


class Prog:

    def __init__(self, cfg):
        self.cfg = cfg
        self.T = cfg["T"]
        self.DM = cfg["DM"]
        self.NTT = self.T // 128
        self.NDC = self.DM // 128
        self.nc = bass.Bass("TRN2", target_bir_lowering=False, debug=False,
                            detect_race_conditions=False)
        self.es = ExitStack()
        self.block = None
        self.sems = {}
        self.inputs = {}
        self.outputs = {}
        self._uid = 0
        self.last_inst = {}
        self._inced = set()

    def begin(self):
        self.block = self.es.enter_context(self.nc.Block())
        for s in ("sT", "sV", "sA", "sG", "sD", "sD2", "sGD", "sR", "sL"):
            self.sem(s)

    def finish(self):
        self.es.close()
        return self.nc

    def uid(self, p):
        self._uid += 1
        return f"{p}{self._uid}"

    def dram_in(self, name, shape, dt=F32):
        ap = self.nc.dram_tensor(name, list(shape), dt, kind="ExternalInput").ap()
        self.inputs[name] = ap
        return ap

    def dram_out(self, name, shape, dt=F32):
        ap = self.nc.dram_tensor(name, list(shape), dt, kind="ExternalOutput").ap()
        self.outputs[name] = ap
        return ap

    def dram_tmp(self, name, shape, dt=F32):
        return self.nc.dram_tensor(name, list(shape), dt, kind="Internal").ap()

    def sem(self, name):
        h = self.es.enter_context(self.nc.semaphore(name))
        self.sems[name] = [h, 0]
        return h

    def sb(self, name, shape, dt=F32, es=None):
        return (es or self.es).enter_context(self.nc.sbuf_tensor(name, list(shape), dt))

    def psum(self, name, shape, dt=F32):
        return self.es.enter_context(self.nc.psum_tensor(name, list(shape), dt))

    def inc(self, inst, name, amt=1):
        inst.then_inc(self.sems[name][0], amt)
        self._inced.add(id(inst))
        self.sems[name][1] += amt
        return self.sems[name][1]

    def incl(self, eng, name, amt=1, barrier=True):
        li = self.last_inst.get(eng._en)
        if li is not None and id(li) not in self._inced:
            li.then_inc(self.sems[name][0], amt)
            self._inced.add(id(li))
            self.sems[name][1] += amt
            v = self.sems[name][1]
            if barrier:
                eng._e.drain()
            return v
        return self.inc(eng._e.drain(), name, amt)

    def val(self, name):
        return self.sems[name][1]

    def wait(self, eng, name, val=None):
        eng.wait_ge(self.sems[name][0], self.sems[name][1] if val is None else val)

    def sec(self, engine_name, fn):
        def wrapped(e):
            fn(_EngProxy(e, self, engine_name))
        getattr(self.block, engine_name)(wrapped)

    def dma(self, dst, src, queue="sync", sem="sD", wait_engines=False):
        def f(e):
            self.wait(e, sem)
            if wait_engines:
                for s in ("sT", "sV", "sA"):
                    self.wait(e, s)
            self.inc(e.dma_start(out=dst, in_=src), sem, 16)
        self.sec(queue, f)

    def dma_wait(self, eng_name, sem="sD"):
        def f(e):
            self.wait(e, sem)
        self.sec(eng_name, f)



def emit_const_setup(P):
    nc = P.nc
    P.ident_f = P.sb("ident_f", [128, 128], F32)
    P.ident_b = P.sb("ident_b", [128, 128], BF16)
    P.iota_col = P.sb("iota_col", [128, 128], I32)
    P.ones_col = P.sb("ones_col", [1, 128], F32)
    P.ones_col_b = P.sb("ones_col_b", [1, 128], BF16)
    P.ones128 = P.sb("ones128", [128, 1], F32)
    P.ones128_b = P.sb("ones128_b", [128, 1], BF16)
    P.tri_b = P.sb("tri_b", [128, 128], BF16)
    P.tri_f = P.sb("tri_f", [128, 128], F32)
    P.tramp128 = P.sb("tramp128", [128, P.T], I32)
    P.prow = P.sb("prow", [128, 1], I32)
    P.pj8 = P.sb("pj8", [128, 8], I32)
    P.jp8 = P.sb("jp8", [8, 128], I32)

    P.tpd = P.sb("tpd", [128, 512], F32)

    def g(gp):
        gp.iota(out=P.iota_col[:], pattern=[[1, 128]], base=0, channel_multiplier=-1)
        gp.iota(out=P.tramp128[:], pattern=[[1, P.T]], base=0, channel_multiplier=0)
        gp.iota(out=P.prow[:], pattern=[[1, 1]], base=0, channel_multiplier=1)
        gp.iota(out=P.pj8[:], pattern=[[-32, 8]], base=0, channel_multiplier=1)
        gp.iota(out=P.jp8[:], pattern=[[1, 128]], base=0, channel_multiplier=-32)
        P.incl(gp, "sG")
    P.sec("gpsimd", g)

    P.triu_b = P.sb("triu_b", [128, 128], BF16)
    P.dtri = P.sb("dtri", [128, 4, 512], BF16)
    P.prow_f = P.sb("prow_f", [128, 1], F32)

    def v(v):
        P.wait(v, "sG")
        v.tensor_scalar(out=P.triu_b[:], in0=P.iota_col[:], scalar1=0, scalar2=None,
                        op0=ALU.is_ge)
        v.tensor_copy(out=P.prow_f[:], in_=P.prow[:])
        v.drain()
        v.tensor_scalar(out=P.ident_f[:], in0=P.iota_col[:], scalar1=0, scalar2=None,
                        op0=ALU.is_equal)
        v.tensor_scalar(out=P.tri_f[:], in0=P.iota_col[:], scalar1=0, scalar2=None,
                        op0=ALU.is_le)
        v.memset(P.ones_col[:], 1.0)
        v.memset(P.ones_col_b[:], 1.0)
        v.memset(P.ones128[:], 1.0)
        v.memset(P.ones128_b[:], 1.0)
        v.drain()
        v.tensor_copy(out=P.ident_b[:], in_=P.ident_f[:])
        v.tensor_copy(out=P.tri_b[:], in_=P.tri_f[:])
        P.incl(v, "sV")
    P.sec("vector", v)


def emit_ln(P, X, LNX, es=None, eps=1e-5):
    NTT, DM = P.NTT, P.DM
    cs = min(512, DM)
    nch = DM // cs
    st = P.sb(P.uid("ln_st"), [128, NTT, 6 * nch], F32, es=es)
    mv = P.sb(P.uid("ln_mv"), [128, NTT, 2], F32, es=es)
    sd = P.sb(P.uid("ln_sd"), [128, NTT], F32, es=es)
    rs = P.sb(P.uid("ln_rs"), [128, NTT], F32, es=es)

    def v1(v):
        v.drain()
        for i in range(NTT):
            for c in range(nch):
                v.bn_stats(out=st[:, i, 6 * c:6 * c + 6], in_=X[:, i, cs * c:cs * (c + 1)])
        v.drain()
        for i in range(NTT):
            v.bn_aggr(out=mv[:, i, :], in_=st[:, i, :])
        P.incl(v, "sV")
    P.sec("vector", v1)

    def v1b(v):
        v.tensor_scalar(out=sd[:, :], in0=mv[:, :, 1], scalar1=float(eps),
                        scalar2=None, op0=ALU.add)
        P.incl(v, "sV")
    P.sec("vector", v1b)

    def s1(s):
        P.wait(s, "sV")
        s.activation(out=sd[:, :], in_=sd[:, :], func=ACT.Sqrt)
        P.incl(s, "sA")
    P.sec("scalar", s1)

    def v2(v):
        P.wait(v, "sA")
        v.reciprocal(out=rs[:, :], in_=sd[:, :])
        v.drain()
        for i in range(NTT):
            v.tensor_scalar(out=LNX[:, i, :], in0=X[:, i, :],
                            scalar1=mv[:, i, 0:1], scalar2=rs[:, i:i + 1],
                            op0=ALU.subtract, op1=ALU.mult)
        P.incl(v, "sV")
    P.sec("vector", v2)


def emit_ln_t(P, banks, X, LNXT, es=None, eps=1e-5, dst2=None):
    NTT, DM, NDC = P.NTT, P.DM, P.NDC
    with ExitStack() as les:
        cs = min(512, DM)
        nch = DM // cs
        st = P.sb(P.uid("lnt_st"), [128, 6 * nch], F32, es=les)
        mv = P.sb(P.uid("lnt_mv"), [128, 2], F32, es=les)
        sd = P.sb(P.uid("lnt_sd"), [128, 1], F32, es=les)
        rs = P.sb(P.uid("lnt_rs"), [128, 1], F32, es=les)
        lnt = P.sb(P.uid("lnt_buf"), [128, DM], F32, es=les)
        for i in range(NTT):
            def v1(v, i=i):
                v.drain()
                for c in range(nch):
                    v.bn_stats(out=st[:, 6 * c:6 * c + 6], in_=X[:, i, cs * c:cs * (c + 1)])
                v.drain()
                v.bn_aggr(out=mv[:], in_=st[:])
                v.drain()
                v.tensor_scalar(out=sd[:], in0=mv[:, 1:2], scalar1=float(eps),
                                scalar2=None, op0=ALU.add)
                P.incl(v, "sV")
            P.sec("vector", v1)

            def s1(s):
                P.wait(s, "sV")
                s.activation(out=sd[:], in_=sd[:], func=ACT.Sqrt)
                P.incl(s, "sA")
            P.sec("scalar", s1)

            def v2(v, i=i):
                P.wait(v, "sA")
                v.reciprocal(out=rs[:], in_=sd[:])
                v.drain()
                v.tensor_scalar(out=lnt[:], in0=X[:, i, :],
                                scalar1=mv[:, 0:1], scalar2=rs[:, 0:1],
                                op0=ALU.subtract, op1=ALU.mult)
                P.incl(v, "sV")
            P.sec("vector", v2)

            for c0 in range(0, NDC, 4):
                grp = list(range(c0, min(c0 + 4, NDC)))
                taken = banks.take(len(grp))

                def t(t, grp=grp, taken=taken):
                    banks.tensor_wait_free(t, [b for b, _ in taken])
                    P.wait(t, "sV")
                    inst = None
                    for (bi, bap), c in zip(taken, grp):
                        inst = t.transpose(out=bap[:, 0:128], in_=lnt[:, 128 * c:128 * (c + 1)],
                                           identity=P.ident_f[:])
                    P.inc(inst, "sT")
                P.sec("tensor", t)

                def s(s, grp=grp, taken=taken, i=i):
                    P.wait(s, "sT")
                    for (bi, bap), c in zip(taken, grp):
                        s.activation(out=LNXT[:, c, 128 * i:128 * (i + 1)], in_=bap[:, 0:128],
                                     func=ACT.Copy)
                        if dst2 is not None:
                            s.activation(out=dst2[:, c, 128 * i:128 * (i + 1)],
                                         in_=bap[:, 0:128], func=ACT.Copy)
                    vv = P.incl(s, "sA")
                    banks.set_free_at([bi for bi, _ in taken], "sA", vv)
                P.sec("scalar", s)
            def v3(v):
                P.wait(v, "sT")
                P.incl(v, "sV")
            P.sec("vector", v3)


class Banks:

    def __init__(self, P):
        self.P = P
        self.b = [P.psum(f"bank{i}", [128, 512], F32) for i in range(6)]
        self.bb = [P.psum(f"bankb{i}", [128, 1024], BF16) for i in range(2)]
        self.free_at = [None] * 8
        self.rr = 0
        self.rrb = 0
        self.held = set()

    def take(self, n=1, hold=False):
        out = []
        for _ in range(n):
            for _try in range(8):
                i = self.rr % 6
                self.rr += 1
                if i not in self.held:
                    break
            assert i not in self.held, "all psum banks held"
            out.append((i, self.b[i]))
        if hold:
            for i, _ in out:
                self.held.add(i)
        return out

    def release(self, taken):
        for i, _ in taken:
            self.held.discard(i)

    def take_bf(self, n=1):
        out = []
        for _ in range(n):
            i = self.rrb % 2
            self.rrb += 1
            out.append((6 + i, self.bb[i]))
        return out

    def tensor_wait_free(self, t, idxs):
        seen = {}
        for i in idxs:
            fa = self.free_at[i]
            if fa is not None:
                s, v = fa
                seen[s] = max(seen.get(s, 0), v)
        for s, v in seen.items():
            t.wait_ge(self.P.sems[s][0], v)

    def set_free_at(self, idxs, sem_name, val):
        for i in idxs:
            self.free_at[i] = (sem_name, val)


def emit_weight_load_cast(P, hbm_ap, name, din, dout, to_bf16=True,
                          row_scale=None, es=None):
    nch = din // 128
    if not to_bf16:
        wf = P.sb(P.uid(name + "_f"), [128, nch, dout], F32, es=es)
        P.dma(wf[:, :, :], hbm_ap.rearrange("(c p) o -> p c o", p=128), queue="sync",
              wait_engines=True)
        P.dma_wait("vector")

        def v(v):
            P.incl(v, "sV")
        P.sec("vector", v)
        return wf
    wb = P.sb(P.uid(name + "_b"), [128, nch, dout], BF16, es=es)
    hv = hbm_ap.rearrange("(c p) o -> p c o", p=128)
    with ExitStack() as ses:
        CS = min(512, dout)
        stg = P.sb(P.uid(name + "_s"), [128, nch, CS], F32, es=ses)
        for o0 in range(0, dout, CS):
            P.dma(stg[:, :, :], hv[:, :, o0:o0 + CS], queue="sync",
                  wait_engines=(o0 == 0))
            P.dma_wait("vector")

            def v(v, o0=o0):
                for c in range(nch):
                    if row_scale is not None:
                        v.tensor_scalar(out=wb[:, c, o0:o0 + CS], in0=stg[:, c, :],
                                        scalar1=row_scale[:, c:c + 1], scalar2=None,
                                        op0=ALU.mult)
                    else:
                        v.tensor_copy(out=wb[:, c, o0:o0 + CS], in_=stg[:, c, :])
                P.incl(v, "sV")
            P.sec("vector", v)
            def g(e):
                P.wait(e, "sV")
            P.sec("sync", g)
    return wb


def emit_transpose_fm(P, banks, src_tm, dst_fm, ncol_chunks, ntt, dst_dt=BF16,
                      src_ti=None):
    tis = src_ti if src_ti is not None else range(ntt)
    jobs = [(i, c) for i in tis for c in range(ncol_chunks)]
    for g0 in range(0, len(jobs), 4):
        grp = jobs[g0:g0 + 4]
        taken = banks.take(len(grp))

        def t(t, grp=grp, taken=taken):
            banks.tensor_wait_free(t, [i for i, _ in taken])
            P.wait(t, "sV")
            inst = None
            for (bi, bap), (i, c) in zip(taken, grp):
                inst = t.transpose(out=bap[:, 0:128], in_=src_tm[:, i, 128 * c:128 * (c + 1)],
                                   identity=P.ident_f[:])
            P.inc(inst, "sT")
        P.sec("tensor", t)

        def s(s, grp=grp, taken=taken):
            P.wait(s, "sT")
            for (bi, bap), (i, c) in zip(taken, grp):
                ii = list(tis).index(i) if src_ti is not None else i
                s.activation(out=dst_fm[:, c, 128 * ii:128 * (ii + 1)], in_=bap[:, 0:128],
                             func=ACT.Copy)
            v = P.incl(s, "sA")
            banks.set_free_at([bi for bi, _ in taken], "sA", v)
        P.sec("scalar", s)

    def s2(s):
        P.incl(s, "sA")
    P.sec("scalar", s2)


def emit_mm_fm(P, banks, lhsT, rhs_fm, out_fm, din, dout, T=None,
               out_dt=BF16, evac=None, extra_k1=None, wait_sems=()):
    T = T or P.T
    nch = din // 128
    nmt = dout // 128
    NSEG = 512
    for m in range(nmt):
        for q0 in range(0, T, NSEG * 2):
            take = banks.take(2)
            segs = [(q0, min(q0 + NSEG, T)), (q0 + NSEG, min(q0 + 2 * NSEG, T))]
            segs = [s_ for s_ in segs if s_[0] < s_[1]]

            def t(t, m=m, segs=segs, take=take):
                banks.tensor_wait_free(t, [i for i, _ in take])
                for sm in wait_sems:
                    P.wait(t, sm)
                for c in range(nch):
                    for (bi, bap), (a, b) in zip(take, segs):
                        t.matmul(out=bap[:, 0:b - a], lhsT=lhsT[:, c, 128 * m:128 * (m + 1)],
                                 rhs=rhs_fm[:, c, a:b], start=(c == 0), stop=(c == nch - 1))
                P.incl(t, "sT")
            P.sec("tensor", t)

            def s(s, m=m, segs=segs, take=take):
                P.wait(s, "sT")
                for (bi, bap), (a, b) in zip(take, segs):
                    if evac is not None:
                        evac(s, bap[:, 0:b - a], m, a, b)
                    else:
                        s.activation(out=out_fm[:, m, a:b], in_=bap[:, 0:b - a], func=ACT.Copy)
                v = P.incl(s, "sA")
                banks.set_free_at([bi for bi, _ in take], "sA", v)
            P.sec("scalar", s)



def emit_rope_tables(P, mdim, coef, name, es=None):
    nc = P.nc
    COS = P.sb(P.uid(name + "_cos"), [128, P.T], F32, es=es)
    SIN = P.sb(P.uid(name + "_sin"), [128, P.T], F32, es=es)
    ses = ExitStack()
    invf = P.sb(P.uid(name + "_invf"), [128, 1], F32, es=ses)
    tf = P.sb(P.uid(name + "_tf"), [128, P.T], F32, es=ses)
    ang = P.sb(P.uid(name + "_ang"), [128, P.T], F32, es=ses)
    mi = P.sb(P.uid(name + "_mi"), [128, P.T], F32, es=ses)
    ti = P.tramp128
    TWO_PI = float(2.0 * np.pi)

    def s(s):
        P.wait(s, "sG")
        s.activation(out=invf[:], in_=P.prow[:], func=ACT.Exp, scale=-float(coef))
        P.incl(s, "sA")
    P.sec("scalar", s)

    def sync(e):
        P.wait(e, "sA")
        for k in range(1, 128 // mdim):
            P.wait(e, "sD")
            P.inc(e.dma_start(out=invf[k * mdim:(k + 1) * mdim, :], in_=invf[0:mdim, :]), "sD", 16)
    P.sec("sync", sync)
    P.dma_wait("vector", "sD")

    for which, off in (("cos", np.pi / 2), ("sin", 0.0)):
        dst = COS if which == "cos" else SIN

        def v(v, off=off, dst=dst):
            P.wait(v, "sG")
            v.tensor_copy(out=tf[:], in_=ti[:])
            v.drain()
            v.tensor_scalar(out=ang[:], in0=tf[:], scalar1=invf[:, 0:1],
                            scalar2=float(off), op0=ALU.mult, op1=ALU.add)
            v.drain()
            v.tensor_scalar(out=mi[:], in0=ang[:], scalar1=1.0 / TWO_PI,
                            scalar2=RND_C, op0=ALU.mult, op1=ALU.add)
            v.drain()
            v.tensor_scalar(out=mi[:], in0=mi[:], scalar1=RND_C,
                            scalar2=None, op0=ALU.subtract)
            v.drain()
            v.scalar_tensor_tensor(out=dst[:], in0=mi[:], scalar=-TWO_PI,
                                   in1=ang[:], op0=ALU.mult, op1=ALU.add)
            P.incl(v, "sV")
        P.sec("vector", v)

        def s2(s, dst=dst):
            P.wait(s, "sV")
            s.activation(out=dst[:], in_=dst[:], func=ACT.Sin)
            P.incl(s, "sA")
        P.sec("scalar", s2)
    def vfin(v):
        P.wait(v, "sA")
        P.incl(v, "sV")
    P.sec("vector", vfin)
    ses.close()
    return COS, SIN



def emit_tm_proj_spill(P, banks, LNXT, w_hbm, dst_h, wname, dout=None,
                       act=None):
    T, DM, NTT, NDC = P.T, P.DM, P.NTT, P.NDC
    dout = dout or DM
    with ExitStack() as ves:
        wv = emit_weight_load_cast(P, w_hbm, wname, DM, dout, es=ves)
        vstg = ves.enter_context(P.nc.sbuf_tensor(P.uid(wname + "_stg"), [128, 4, dout], BF16))
        for i in range(NTT):
            ii = i % 4
            for seg in range(0, dout, 512):
                sn = min(512, dout - seg)
                take = banks.take(1)

                def t(t, i=i, seg=seg, sn=sn, take=take):
                    banks.tensor_wait_free(t, [b for b, _ in take])
                    P.wait(t, "sV")
                    if i >= 4:
                        P.wait(t, "sD")
                    inst = None
                    for c in range(NDC):
                        inst = t.matmul(out=take[0][1][:, 0:sn],
                                        lhsT=LNXT[:, c, 128 * i:128 * (i + 1)],
                                        rhs=wv[:, c, seg:seg + sn],
                                        start=(c == 0), stop=(c == NDC - 1))
                    P.inc(inst, "sT")
                P.sec("tensor", t)

                def s(s, ii=ii, seg=seg, sn=sn, take=take):
                    P.wait(s, "sT")
                    s.activation(out=vstg[:, ii, seg:seg + sn],
                                 in_=take[0][1][:, 0:sn],
                                 func=(act or ACT.Copy))
                    vv = P.incl(s, "sA")
                    banks.set_free_at([take[0][0]], "sA", vv)
                P.sec("scalar", s)
            if ii == 3 or i == NTT - 1:
                def so(e, i=i, ii=ii):
                    P.wait(e, "sD")
                    P.wait(e, "sA")
                    P.inc(e.dma_start(out=dst_h[:, i - ii:i + 1, :], in_=vstg[:, 0:ii + 1, :]),
                          "sD", 16)
                P.sec("sync", so)
    P.dma_wait("vector", "sD")


def emit_l0_attn(P, banks, X, W):
    T, DM, NTT, NDC = P.T, P.DM, P.NTT, P.NDC
    H = DM // 64
    NG = H // 4
    TSEG = min(1024, T)
    q_h = P.dram_tmp(P.uid("l0q"), [128, NDC, T], BF16)
    k_h = P.dram_tmp(P.uid("l0k"), [128, NDC, T], BF16)
    v_h = P.dram_tmp(P.uid("l0v"), [128, NTT, DM], BF16)

    es = ExitStack()
    sbl = lambda n, sh, dt=F32: es.enter_context(P.nc.sbuf_tensor(P.uid(n), sh, dt))

    bd4 = sbl("l0_bd4", [128, 4], F32)
    bd4T = sbl("l0_bd4T", [4, 128], F32)
    tmpi = sbl("l0_tmpi", [128, 4], F32)
    tmpj = sbl("l0_tmpj", [4, 128], F32)

    def v0(v):
        P.wait(v, "sG")
        v.tensor_scalar(out=bd4[:], in0=P.pj8[:, 0:4], scalar1=0, scalar2=None, op0=ALU.is_ge)
        v.tensor_scalar(out=bd4T[:], in0=P.jp8[0:4, :], scalar1=0, scalar2=None, op0=ALU.is_ge)
        v.drain()
        v.tensor_scalar(out=tmpi[:], in0=P.pj8[:, 0:4], scalar1=32, scalar2=None, op0=ALU.is_lt)
        v.tensor_scalar(out=tmpj[:], in0=P.jp8[0:4, :], scalar1=32, scalar2=None, op0=ALU.is_lt)
        v.drain()
        v.tensor_tensor(out=bd4[:], in0=bd4[:], in1=tmpi[:], op=ALU.mult)
        v.tensor_tensor(out=bd4T[:], in0=bd4T[:], in1=tmpj[:], op=ALU.mult)
        P.incl(v, "sV")
    P.sec("vector", v0)

    with ExitStack() as pes:
        psl = lambda n, sh, dt=F32: pes.enter_context(P.nc.sbuf_tensor(P.uid(n), sh, dt))
        COS, SIN = emit_rope_tables(P, 32, np.log(10000.0) / 32.0, "l0rope", es=pes)
        LNXT = psl("l0_lnxt", [128, NDC, T], BF16)
        emit_ln_t(P, banks, X, LNXT)

        emit_tm_proj_spill(P, banks, LNXT, W["l0_wv"], v_h, "l0wv")

        wq = emit_weight_load_cast(P, W["l0_wq"], "l0wq", DM, DM, es=pes)
        wk = emit_weight_load_cast(P, W["l0_wk"], "l0wk", DM, DM, es=pes)
        raw = psl("l0_qraw", [128, 2, TSEG])
        rot = psl("l0_rot", [128, 2, TSEG])
        ss = psl("l0_ss", [4, TSEG])
        rstd = psl("l0_rstd", [4, TSEG])
        gout = psl("l0_gout", [128, 2, TSEG], BF16)

        for (wmat, dsth) in ((wq, q_h), (wk, k_h)):
            for g2 in range(NG):
                for t0 in range(0, T, TSEG):
                    for half in range(2):
                        m = 2 * g2 + half
                        for q0 in range(0, TSEG, 512):
                            take = banks.take(1)

                            def t(t, m=m, q0=q0, t0=t0, take=take, wmat=wmat):
                                banks.tensor_wait_free(t, [b for b, _ in take])
                                P.wait(t, "sV")
                                P.wait(t, "sD")
                                for c in range(NDC):
                                    t.matmul(out=take[0][1][:, 0:512],
                                             lhsT=wmat[:, c, 128 * m:128 * (m + 1)],
                                             rhs=LNXT[:, c, t0 + q0:t0 + q0 + 512],
                                             start=(c == 0), stop=(c == NDC - 1))
                                P.incl(t, "sT")
                            P.sec("tensor", t)

                            def s(s, half=half, q0=q0, take=take):
                                P.wait(s, "sT")
                                s.activation(out=raw[:, half, q0:q0 + 512],
                                             in_=take[0][1][:, 0:512], func=ACT.Copy)
                                vv = P.incl(s, "sA")
                                banks.set_free_at([take[0][0]], "sA", vv)
                            P.sec("scalar", s)

                    def v(v, t0=t0):
                        P.wait(v, "sA")
                        cs_ = COS[:, t0:t0 + TSEG]
                        sn_ = SIN[:, t0:t0 + TSEG]
                        v.tensor_tensor(out=rot[:, 0, :], in0=raw[:, 0, :], in1=cs_, op=ALU.mult)
                        v.tensor_tensor(out=rot[:, 1, :], in0=raw[:, 0, :], in1=sn_, op=ALU.mult)
                        v.drain()
                        v.tensor_tensor(out=raw[:, 0, :], in0=raw[:, 1, :], in1=sn_, op=ALU.mult)
                        v.tensor_tensor(out=raw[:, 1, :], in0=raw[:, 1, :], in1=cs_, op=ALU.mult)
                        v.drain()
                        v.tensor_tensor(out=rot[:, 0, :], in0=rot[:, 0, :], in1=raw[:, 0, :], op=ALU.subtract)
                        v.tensor_tensor(out=rot[:, 1, :], in0=rot[:, 1, :], in1=raw[:, 1, :], op=ALU.add)
                        P.incl(v, "sV")
                    P.sec("vector", v)

                    def s2(s):
                        P.wait(s, "sV")
                        s.activation(out=raw[:, 0, :], in_=rot[:, 0, :], func=ACT.Square)
                        s.activation(out=raw[:, 1, :], in_=rot[:, 1, :], func=ACT.Square)
                        P.incl(s, "sA")
                    P.sec("scalar", s2)

                    for q0 in range(0, TSEG, 512):
                        take = banks.take(1)

                        def t(t, q0=q0, take=take):
                            banks.tensor_wait_free(t, [b for b, _ in take])
                            P.wait(t, "sA")
                            t.matmul(out=take[0][1][0:4, 0:512], lhsT=bd4[:],
                                     rhs=raw[:, 0, q0:q0 + 512], start=True, stop=False)
                            t.matmul(out=take[0][1][0:4, 0:512], lhsT=bd4[:],
                                     rhs=raw[:, 1, q0:q0 + 512], start=False, stop=True)
                            P.incl(t, "sT")
                        P.sec("tensor", t)

                        def v2(v, q0=q0, take=take):
                            P.wait(v, "sT")
                            v.tensor_copy(out=ss[:, q0:q0 + 512], in_=take[0][1][0:4, 0:512])
                            vv = P.incl(v, "sV")
                            banks.set_free_at([take[0][0]], "sV", vv)
                        P.sec("vector", v2)

                    def v2b(v):
                        v.tensor_scalar(out=ss[:], in0=ss[:], scalar1=1e-6, scalar2=None, op0=ALU.add)
                        P.incl(v, "sV")
                    P.sec("vector", v2b)

                    def s3(s):
                        P.wait(s, "sV")
                        s.activation(out=ss[:], in_=ss[:], func=ACT.Sqrt)
                        P.incl(s, "sA")
                    P.sec("scalar", s3)

                    def v3(v):
                        P.wait(v, "sA")
                        v.reciprocal(out=rstd[:], in_=ss[:])
                        P.incl(v, "sV")
                    P.sec("vector", v3)

                    for half in range(2):
                        for q0 in range(0, TSEG, 512):
                            take = banks.take(1)

                            def t(t, q0=q0, take=take):
                                banks.tensor_wait_free(t, [b for b, _ in take])
                                P.wait(t, "sV")
                                t.matmul(out=take[0][1][:, 0:512], lhsT=bd4T[:],
                                         rhs=rstd[:, q0:q0 + 512], start=True, stop=True)
                                P.incl(t, "sT")
                            P.sec("tensor", t)

                            def v4(v, half=half, q0=q0, take=take):
                                P.wait(v, "sT")
                                v.tensor_tensor(out=gout[:, half, q0:q0 + 512],
                                                in0=rot[:, half, q0:q0 + 512],
                                                in1=take[0][1][:, 0:512], op=ALU.mult)
                                vv = P.incl(v, "sV")
                                banks.set_free_at([take[0][0]], "sV", vv)
                            P.sec("vector", v4)

                    def so2(e, g2=g2, t0=t0, dsth=dsth):
                        P.wait(e, "sD")
                        P.wait(e, "sV")
                        P.inc(e.dma_start(out=dsth[:, 2 * g2:2 * g2 + 2, t0:t0 + TSEG],
                                          in_=gout[:, :, :]), "sD", 16)
                    P.sec("sync", so2)

    of = sbl("l0_of", [128, NDC, T], BF16)
    qg = sbl("l0_qg", [128, 2, T], BF16)
    kg = sbl("l0_kg", [128, 2, T], BF16)
    vext = sbl("l0_vext", [128, NTT, 65], BF16)
    Pb = sbl("l0_p", [128, NTT, 512], BF16)
    rsum = sbl("l0_rsum", [1, 512], F32)
    pvs = sbl("l0_pvs", [65, 512])
    inv8 = 1.0 / np.sqrt(64.0)

    for g2 in range(NG):
        def li(e, g2=g2):
            P.wait(e, "sD")
            for s in ("sT", "sV", "sA"):
                P.wait(e, s)
            P.inc(e.dma_start(out=qg[:, :, :], in_=q_h[:, 2 * g2:2 * g2 + 2, :]), "sD", 16)
            P.wait(e, "sD")
            P.inc(e.dma_start(out=kg[:, :, :], in_=k_h[:, 2 * g2:2 * g2 + 2, :]), "sD", 16)
        P.sec("sync", li)
        P.dma_wait("tensor", "sD")
        P.dma_wait("vector", "sD")

        for j in range(4):
            h = 4 * g2 + j
            r0 = 32 * j

            def li2(e, h=h):
                P.wait(e, "sD")
                P.wait(e, "sT")
                P.inc(e.dma_start(out=vext[:, :, 0:64], in_=v_h[:, :, 64 * h:64 * (h + 1)]), "sD", 16)
            P.sec("sync", li2)

            def vv0(v):
                P.wait(v, "sD")
                for i in range(NTT):
                    v.tensor_copy(out=vext[:, i, 64:65], in_=P.ones128_b[:])
                P.incl(v, "sV")
            P.sec("vector", vv0)

            for q0 in range(0, T, 512):
                for kt in range(NTT):
                    take = banks.take(1)

                    def t(t, q0=q0, kt=kt, take=take, r0=r0):
                        banks.tensor_wait_free(t, [b for b, _ in take])
                        P.wait(t, "sV")
                        tp = (r0, 0)
                        t.matmul(out=take[0][1][:, 0:512],
                                 lhsT=kg[r0:r0 + 32, 0, 128 * kt:128 * (kt + 1)],
                                 rhs=qg[r0:r0 + 32, 0, q0:q0 + 512],
                                 start=True, stop=False, tile_position=tp)
                        inst = t.matmul(out=take[0][1][:, 0:512],
                                        lhsT=kg[r0:r0 + 32, 1, 128 * kt:128 * (kt + 1)],
                                        rhs=qg[r0:r0 + 32, 1, q0:q0 + 512],
                                        start=False, stop=True, tile_position=tp)
                        P.inc(inst, "sT")
                    P.sec("tensor", t)

                    def s(s, q0=q0, kt=kt, take=take):
                        P.wait(s, "sT")
                        s.activation(out=Pb[:, kt, :], in_=take[0][1][:, 0:512],
                                     func=ACT.Exp, scale=inv8)
                        vv = P.incl(s, "sA", barrier=False)
                        banks.set_free_at([take[0][0]], "sA", vv)
                    P.sec("scalar", s)

                take = banks.take(1)

                def t2(t, take=take):
                    banks.tensor_wait_free(t, [b for b, _ in take])
                    P.wait(t, "sA")
                    inst = None
                    for kt in range(NTT):
                        inst = t.matmul(out=take[0][1][0:65, 0:512], lhsT=vext[:, kt, :],
                                        rhs=Pb[:, kt, :], start=(kt == 0), stop=(kt == NTT - 1))
                    P.inc(inst, "sT")
                P.sec("tensor", t2)

                def v5(v, take=take):
                    P.wait(v, "sT")
                    v.tensor_copy(out=pvs[:], in_=take[0][1][0:65, 0:512])
                    v.drain()
                    v.reciprocal(out=rsum[:], in_=pvs[64:65, :])
                    vv = P.incl(v, "sV")
                    banks.set_free_at([take[0][0]], "sV", vv)
                P.sec("vector", v5)

                take2 = banks.take(1)

                def t3(t, take2=take2):
                    banks.tensor_wait_free(t, [b for b, _ in take2])
                    P.wait(t, "sV")
                    inst = t.matmul(out=take2[0][1][0:64, 0:512], lhsT=P.ones_col[:, 0:64],
                                    rhs=rsum[:], start=True, stop=True)
                    P.inc(inst, "sT")
                P.sec("tensor", t3)

                def v6(v, h=h, q0=q0, take2=take2):
                    P.wait(v, "sT")
                    mc, mr = (64 * h) // 128, (64 * h) % 128
                    v.tensor_tensor(out=of[mr:mr + 64, mc, q0:q0 + 512],
                                    in0=pvs[0:64, :], in1=take2[0][1][0:64, 0:512],
                                    op=ALU.mult)
                    vv = P.incl(v, "sV")
                    banks.set_free_at([take2[0][0]], "sV", vv)
                P.sec("vector", v6)

    with ExitStack() as wes2:
        wo = emit_weight_load_cast(P, W["l0_wo"], "l0wo", DM, DM, es=wes2)
        bo = wes2.enter_context(P.nc.sbuf_tensor(P.uid("l0bo"), [1, DM], F32))
        P.dma(bo[:], W["l0_bo"], queue="sync")
        P.dma_wait("tensor", "sD")
        for i in range(NTT):
            for seg in range(DM // 512):
                take = banks.take(1)

                def t(t, i=i, seg=seg, take=take):
                    banks.tensor_wait_free(t, [b for b, _ in take])
                    P.wait(t, "sV")
                    for c in range(NDC):
                        t.matmul(out=take[0][1][:, 0:512],
                                 lhsT=of[:, c, 128 * i:128 * (i + 1)],
                                 rhs=wo[:, c, 512 * seg:512 * (seg + 1)],
                                 start=(c == 0), stop=False)
                    inst = t.matmul(out=take[0][1][:, 0:512], lhsT=P.ones_col[:],
                                    rhs=bo[:, 512 * seg:512 * (seg + 1)], start=False, stop=True)
                    P.inc(inst, "sT")
                P.sec("tensor", t)

                def v7(v, i=i, seg=seg, take=take):
                    P.wait(v, "sT")
                    v.tensor_tensor(out=X[:, i, 512 * seg:512 * (seg + 1)],
                                    in0=X[:, i, 512 * seg:512 * (seg + 1)],
                                    in1=take[0][1][:, 0:512], op=ALU.add)
                    vv = P.incl(v, "sV")
                    banks.set_free_at([take[0][0]], "sV", vv)
                P.sec("vector", v7)
    es.close()



def emit_l0_ffn(P, banks, X, W):
    T, DM, NTT, NDC = P.T, P.DM, P.NTT, P.NDC
    es = ExitStack()
    sbl = lambda n, sh, dt=F32: es.enter_context(P.nc.sbuf_tensor(P.uid(n), sh, dt))
    SEG = min(512, T)
    NST = SEG // 128

    r_h = P.dram_tmp(P.uid("l0fr"), [128, NTT, DM], BF16)
    LNXT = sbl("l0f_lnxt", [128, NDC, T], BF16)
    emit_ln_t(P, banks, X, LNXT)
    mur = sbl("l0f_mur", [128, NDC])
    muk = sbl("l0f_muk", [128, NDC])
    P.dma(mur[:, :], W["l0f_mur"], queue="sync", wait_engines=True)
    P.dma(muk[:, :], W["l0f_muk"], queue="sync", wait_engines=True)
    P.dma_wait("vector", "sD")

    r_tm = sbl("l0f_r", [128, 4, DM], BF16)
    kk = sbl("l0f_kk", [128, NDC, T], BF16)
    xm = sbl("l0f_xm", [128, NDC, SEG], BF16)
    t1 = sbl("l0f_t1", [128, SEG])

    def emit_mix(t0, msb):
        def v(v, t0=t0, msb=msb):
            P.wait(v, "sA")
            P.wait(v, "sT")
            for c in range(NDC):
                if t0 == 0:
                    v.memset(t1[:, 0:1], 0.0)
                    sl_prev = LNXT[:, c, 0:SEG - 1]
                    sl_t1 = t1[:, 1:SEG]
                else:
                    sl_prev = LNXT[:, c, t0 - 1:t0 + SEG - 1]
                    sl_t1 = t1[:, 0:SEG]
                v.tensor_scalar(out=sl_t1, in0=sl_prev,
                                scalar1=msb[:, c:c + 1], scalar2=None, op0=ALU.mult)
                v.drain()
                v.tensor_tensor(out=sl_t1, in0=sl_prev, in1=sl_t1, op=ALU.subtract)
                v.drain()
                v.scalar_tensor_tensor(out=xm[:, c, :], in0=LNXT[:, c, t0:t0 + SEG],
                                       scalar=msb[:, c:c + 1], in1=t1[:],
                                       op0=ALU.mult, op1=ALU.add)
                v.drain()
            P.incl(v, "sV")
        P.sec("vector", v)

    with ExitStack() as ws:
        wr = emit_weight_load_cast(P, W["l0f_wr"], "l0fwr", DM, DM, es=ws)
        for t0 in range(0, T, SEG):
            emit_mix(t0, mur)
            for ii in range(NST):
                i = t0 // 128 + ii
                for seg in range(DM // 512):
                    take = banks.take(1)

                    def t(t, ii=ii, seg=seg, take=take):
                        banks.tensor_wait_free(t, [b for b, _ in take])
                        P.wait(t, "sV")
                        inst = None
                        for c in range(NDC):
                            inst = t.matmul(out=take[0][1][:, 0:512],
                                            lhsT=xm[:, c, 128 * ii:128 * (ii + 1)],
                                            rhs=wr[:, c, 512 * seg:512 * (seg + 1)],
                                            start=(c == 0), stop=(c == NDC - 1))
                        P.inc(inst, "sT")
                    P.sec("tensor", t)

                    def s(s, i=i, seg=seg, take=take):
                        P.wait(s, "sT")
                        s.activation(out=r_tm[:, i % 4, 512 * seg:512 * (seg + 1)],
                                     in_=take[0][1][:, 0:512], func=ACT.Sigmoid)
                        vv = P.incl(s, "sA")
                        banks.set_free_at([take[0][0]], "sA", vv)
                    P.sec("scalar", s)

                if i % 4 == 3 or i == NTT - 1:
                    def so(e, i=i):
                        P.wait(e, "sD")
                        P.wait(e, "sA")
                        P.inc(e.dma_start(out=r_h[:, i - (i % 4):i + 1, :],
                                          in_=r_tm[:, 0:(i % 4) + 1, :]), "sD", 16)
                    P.sec("sync", so)

    with ExitStack() as ws:
        wkm = emit_weight_load_cast(P, W["l0f_wk"], "l0fwk", DM, DM, es=ws)
        ktmp = ws.enter_context(P.nc.sbuf_tensor(P.uid("l0f_ktmp"), [128, 512], F32))
        for t0 in range(0, T, SEG):
            emit_mix(t0, muk)
            for m in range(NDC):
                for q0 in range(0, SEG, 512):
                    qn = min(512, SEG - q0)
                    take = banks.take(1)

                    def t(t, m=m, q0=q0, qn=qn, take=take):
                        banks.tensor_wait_free(t, [b for b, _ in take])
                        P.wait(t, "sV")
                        inst = None
                        for c in range(NDC):
                            inst = t.matmul(out=take[0][1][:, 0:qn],
                                            lhsT=wkm[:, c, 128 * m:128 * (m + 1)],
                                            rhs=xm[:, c, q0:q0 + qn],
                                            start=(c == 0), stop=(c == NDC - 1))
                        P.inc(inst, "sT")
                    P.sec("tensor", t)

                    def s(s, m=m, t0=t0, q0=q0, qn=qn, take=take):
                        P.wait(s, "sT")
                        s.activation(out=ktmp[:, 0:qn], in_=take[0][1][:, 0:qn], func=ACT.Relu)
                        s.drain()
                        s.activation(out=kk[:, m, t0 + q0:t0 + q0 + qn], in_=ktmp[:, 0:qn],
                                     func=ACT.Square)
                        vv = P.incl(s, "sA")
                        banks.set_free_at([take[0][0]], "sA", vv)
                    P.sec("scalar", s)

    with ExitStack() as ws:
        wvm = emit_weight_load_cast(P, W["l0f_wv"], "l0fwv", DM, DM, es=ws)
        for i in range(NTT):
            if i % 4 == 0:
                def sl2(e, i=i):
                    P.wait(e, "sD")
                    P.wait(e, "sV")
                    P.inc(e.dma_start(out=r_tm[:, :, :], in_=r_h[:, i:i + 4, :]), "sD", 16)
                P.sec("sync", sl2)
                P.dma_wait("vector", "sD")
            for seg in range(DM // 512):
                take = banks.take(1)

                def t(t, i=i, seg=seg, take=take):
                    banks.tensor_wait_free(t, [b for b, _ in take])
                    P.wait(t, "sA")
                    P.wait(t, "sV")
                    inst = None
                    for c in range(NDC):
                        inst = t.matmul(out=take[0][1][:, 0:512],
                                        lhsT=kk[:, c, 128 * i:128 * (i + 1)],
                                        rhs=wvm[:, c, 512 * seg:512 * (seg + 1)],
                                        start=(c == 0), stop=(c == NDC - 1))
                    P.inc(inst, "sT")
                P.sec("tensor", t)

                def v2(v, i=i, seg=seg, take=take):
                    P.wait(v, "sT")
                    sl = slice(512 * seg, 512 * (seg + 1))
                    v.tensor_tensor(out=t1[:, 0:512], in0=take[0][1][:, 0:512],
                                    in1=r_tm[:, i % 4, sl], op=ALU.mult)
                    v.drain()
                    v.tensor_tensor(out=X[:, i, sl], in0=X[:, i, sl], in1=t1[:, 0:512],
                                    op=ALU.add)
                    vv = P.incl(v, "sV")
                    banks.set_free_at([take[0][0]], "sV", vv)
                P.sec("vector", v2)
    es.close()


def bch(ap2d, nh, rep):
    return ap2d.rearrange("p (h o) -> p h o", o=1).to_broadcast([128, nh, rep])


def r3(ap2d, nh):
    return ap2d.rearrange("p (h w) -> p h w", h=nh)



def emit_l3_ffn(P, banks, X, W):
    T, DM, NTT, NDC = P.T, P.DM, P.NTT, P.NDC
    TH = T // 2
    es = ExitStack()
    sbl = lambda n, sh, dt=F32: es.enter_context(P.nc.sbuf_tensor(P.uid(n), sh, dt))

    LNXT = sbl("l3f_lnxt", [128, NDC, T], BF16)
    emit_ln_t(P, banks, X, LNXT)

    win = emit_weight_load_cast(P, W["l3f_win"], "l3fwin", DM, 64, es=es)
    wout = sbl("l3f_woutf", [128, DM])
    wout_b = sbl("l3f_wout", [128, DM], BF16)
    binp = sbl("l3f_bin", [128, 1])
    w1 = sbl("l3f_w1", [128, 16])
    b1 = sbl("l3f_b1", [128, 16])
    w2 = sbl("l3f_w2", [128, 16])
    b2 = sbl("l3f_b2", [128, 1])
    bout = sbl("l3f_bout", [1, DM])
    for (dst, wn) in ((wout, "l3f_wout"), (binp, "l3f_bin"), (w1, "l3f_w1"),
                      (b1, "l3f_b1"), (w2, "l3f_w2"), (b2, "l3f_b2"), (bout, "l3f_bout")):
        P.dma(dst[:], W[wn], queue="sync")
    P.dma_wait("vector", "sD")

    def vw(v):
        v.tensor_copy(out=wout_b[:], in_=wout[:])
        P.incl(v, "sV")
    P.sec("vector", vw)

    s2 = sbl("l3f_s", [128, TH])
    gacc = sbl("l3f_gacc", [128, TH])
    gb = sbl("l3f_gb", [128, TH], BF16)
    ht = sbl("l3f_h", [128, TH])

    for q0 in range(0, TH, 512):
        qn = min(512, TH - q0)
        take = banks.take(1)

        def t(t, q0=q0, qn=qn, take=take):
            banks.tensor_wait_free(t, [b for b, _ in take])
            P.wait(t, "sV")
            inst = None
            for half in range(2):
                for c in range(NDC):
                    inst = t.matmul(out=take[0][1][64 * half:64 * half + 64, 0:qn],
                                    lhsT=win[:, c, 0:64],
                                    rhs=LNXT[:, c, half * TH + q0:half * TH + q0 + qn],
                                    start=(c == 0), stop=(c == NDC - 1),
                                    tile_position=(0, 64 * half))
            P.inc(inst, "sT")
        P.sec("tensor", t)

        def v1(v, q0=q0, qn=qn, take=take):
            P.wait(v, "sT")
            v.tensor_scalar(out=s2[:, q0:q0 + qn], in0=take[0][1][:, 0:qn],
                            scalar1=binp[:, 0:1], scalar2=None, op0=ALU.add)
            vv = P.incl(v, "sV")
            banks.set_free_at([take[0][0]], "sV", vv)
        P.sec("vector", v1)

    def v2(v):
        v.memset(gacc[:], 0.0)
        v.drain()
        for j in range(16):
            v.tensor_scalar(out=ht[:], in0=s2[:], scalar1=w1[:, j:j + 1],
                            scalar2=b1[:, j:j + 1], op0=ALU.mult, op1=ALU.add)
            v.drain()
            v.tensor_scalar(out=ht[:], in0=ht[:], scalar1=0.0, scalar2=None, op0=ALU.max)
            v.drain()
            v.scalar_tensor_tensor(out=gacc[:], in0=ht[:], scalar=w2[:, j:j + 1],
                                   in1=gacc[:], op0=ALU.mult, op1=ALU.add)
            v.drain()
        v.tensor_scalar(out=gb[:], in0=gacc[:], scalar1=b2[:, 0:1], scalar2=None,
                        op0=ALU.add)
        P.incl(v, "sV")
    P.sec("vector", v2)

    for i in range(NTT):
        half, ii = (0, i) if i < NTT // 2 else (1, i - NTT // 2)
        for seg in range(DM // 512):
            take = banks.take(1)

            def t(t, half=half, ii=ii, seg=seg, take=take):
                banks.tensor_wait_free(t, [b for b, _ in take])
                P.wait(t, "sV")
                t.matmul(out=take[0][1][:, 0:512],
                         lhsT=gb[64 * half:64 * half + 64, 128 * ii:128 * (ii + 1)],
                         rhs=wout_b[64 * half:64 * half + 64, 512 * seg:512 * (seg + 1)],
                         start=True, stop=False, tile_position=(64 * half, 0))
                inst = t.matmul(out=take[0][1][:, 0:512], lhsT=P.ones_col[:],
                                rhs=bout[:, 512 * seg:512 * (seg + 1)], start=False, stop=True)
                P.inc(inst, "sT")
            P.sec("tensor", t)

            def v3(v, i=i, seg=seg, take=take):
                P.wait(v, "sT")
                sl = slice(512 * seg, 512 * (seg + 1))
                v.tensor_tensor(out=X[:, i, sl], in0=X[:, i, sl],
                                in1=take[0][1][:, 0:512], op=ALU.add)
                vv = P.incl(v, "sV")
                banks.set_free_at([take[0][0]], "sV", vv)
            P.sec("vector", v3)
    es.close()



def emit_l2_attn(P, banks, X, W):
    T, DM, NTT, NDC = P.T, P.DM, P.NTT, P.NDC
    H = DM // 64
    es = ExitStack()
    sbl = lambda n, sh, dt=F32: es.enter_context(P.nc.sbuf_tensor(P.uid(n), sh, dt))
    v_h = P.dram_tmp(P.uid("l2v"), [128, NTT, DM], BF16)

    of_h = P.dram_tmp(P.uid("l2of"), [128, NDC, T], BF16)
    LNXT = sbl("l2_lnxt", [128, NDC, T], BF16)
    emit_ln_t(P, banks, X, LNXT)
    emit_tm_proj_spill(P, banks, LNXT, W["l2_wv"], v_h, "l2wv")

    aes = ExitStack()
    sbl = lambda n, sh, dt=F32: aes.enter_context(P.nc.sbuf_tensor(P.uid(n), sh, dt))
    wq = emit_weight_load_cast(P, W["l2_wq"], "l2wq", DM, DM, es=aes)
    wk = emit_weight_load_cast(P, W["l2_wk"], "l2wk", DM, DM, es=aes)
    projf = sbl("l2_projf", [64, 256])
    projb = sbl("l2_projb", [64, 256], BF16)
    P.dma(projf[:], W["l2_proj"], queue="sync")
    P.dma_wait("vector", "sD")

    def vp(v):
        v.tensor_copy(out=projb[:], in_=projf[:])
        P.incl(v, "sV")
    P.sec("vector", vp)

    ofc = sbl("l2_ofc", [128, T], BF16)
    qf = sbl("l2_qf", [64, T], BF16)
    kf = sbl("l2_kf", [64, T], BF16)
    kpt = sbl("l2_kpt", [128, NTT, 256], BF16)
    qpf = sbl("l2_qpf", [128, 2, T], BF16)
    vext = sbl("l2_vext", [128, NTT, 65], BF16)
    kvs = sbl("l2_kvs", [128, 2, 65], BF16)
    ano = sbl("l2_ano", [65, 512])
    rno = sbl("l2_rno", [1, 512], F32)

    for h in range(H):
        for (wmat, dst) in ((wq, qf), (wk, kf)):
            for q0 in range(0, T, 512):
                qn = min(512, T - q0)
                take = banks.take(1)

                def t(t, q0=q0, qn=qn, take=take, wmat=wmat, h=h):
                    banks.tensor_wait_free(t, [b for b, _ in take])
                    P.wait(t, "sV")
                    P.wait(t, "sA")
                    inst = None
                    for c in range(NDC):
                        inst = t.matmul(out=take[0][1][0:64, 0:qn],
                                        lhsT=wmat[:, c, 64 * h:64 * h + 64],
                                        rhs=LNXT[:, c, q0:q0 + qn],
                                        start=(c == 0), stop=(c == NDC - 1))
                    P.inc(inst, "sT")
                P.sec("tensor", t)

                def s(s, q0=q0, qn=qn, take=take, dst=dst):
                    P.wait(s, "sT")
                    s.activation(out=dst[:, q0:q0 + qn], in_=take[0][1][0:64, 0:qn],
                                 func=ACT.Copy)
                    vv = P.incl(s, "sA")
                    banks.set_free_at([take[0][0]], "sA", vv)
                P.sec("scalar", s)

        for i in range(NTT):
            take = banks.take(1)

            def t(t, i=i, take=take):
                banks.tensor_wait_free(t, [b for b, _ in take])
                P.wait(t, "sA")
                inst = t.matmul(out=take[0][1][:, 0:256],
                                lhsT=kf[0:64, 128 * i:128 * (i + 1)],
                                rhs=projb[:], start=True, stop=True)
                P.inc(inst, "sT")
            P.sec("tensor", t)

            def s(s, i=i, take=take):
                P.wait(s, "sT")
                s.activation(out=kpt[:, i, :], in_=take[0][1][:, 0:256], func=ACT.Relu)
                vv = P.incl(s, "sA")
                banks.set_free_at([take[0][0]], "sA", vv)
            P.sec("scalar", s)

        def li2(e, h=h):
            P.wait(e, "sD")
            P.wait(e, "sT")
            P.inc(e.dma_start(out=vext[:, :, 0:64], in_=v_h[:, :, 64 * h:64 * (h + 1)]), "sD", 16)
        P.sec("sync", li2)

        def vv0(v):
            P.wait(v, "sD")
            for i in range(NTT):
                v.tensor_copy(out=vext[:, i, 64:65], in_=P.ones128_b[:])
            P.incl(v, "sV")
        P.sec("vector", vv0)

        for fi in range(2):
            take = banks.take(1)

            def t(t, fi=fi, take=take):
                banks.tensor_wait_free(t, [b for b, _ in take])
                P.wait(t, "sV")
                P.wait(t, "sA")
                inst = None
                for kt in range(NTT):
                    inst = t.matmul(out=take[0][1][:, 0:65],
                                    lhsT=kpt[:, kt, 128 * fi:128 * (fi + 1)],
                                    rhs=vext[:, kt, :], start=(kt == 0), stop=(kt == NTT - 1))
                P.inc(inst, "sT")
            P.sec("tensor", t)

            def v1(v, fi=fi, take=take):
                P.wait(v, "sT")
                v.tensor_copy(out=kvs[:, fi, :], in_=take[0][1][:, 0:65])
                vv = P.incl(v, "sV")
                banks.set_free_at([take[0][0]], "sV", vv)
            P.sec("vector", v1)

        for fi in range(2):
            for q0 in range(0, T, 512):
                qn = min(512, T - q0)
                take = banks.take(1)

                def t(t, fi=fi, q0=q0, qn=qn, take=take):
                    banks.tensor_wait_free(t, [b for b, _ in take])
                    P.wait(t, "sA")
                    inst = t.matmul(out=take[0][1][:, 0:qn],
                                    lhsT=projb[0:64, 128 * fi:128 * (fi + 1)],
                                    rhs=qf[:, q0:q0 + qn], start=True, stop=True)
                    P.inc(inst, "sT")
                P.sec("tensor", t)

                def s(s, fi=fi, q0=q0, qn=qn, take=take):
                    P.wait(s, "sT")
                    s.activation(out=qpf[:, fi, q0:q0 + qn], in_=take[0][1][:, 0:qn],
                                 func=ACT.Relu)
                    vv = P.incl(s, "sA")
                    banks.set_free_at([take[0][0]], "sA", vv)
                P.sec("scalar", s)

        for q0 in range(0, T, 512):
            qn = min(512, T - q0)
            take = banks.take(1)

            def t(t, q0=q0, qn=qn, take=take):
                banks.tensor_wait_free(t, [b for b, _ in take])
                P.wait(t, "sA")
                P.wait(t, "sV")
                inst = None
                for fi in range(2):
                    inst = t.matmul(out=take[0][1][0:65, 0:qn],
                                    lhsT=kvs[:, fi, :],
                                    rhs=qpf[:, fi, q0:q0 + qn],
                                    start=(fi == 0), stop=(fi == 1))
                P.inc(inst, "sT")
            P.sec("tensor", t)

            def v2(v, q0=q0, qn=qn, take=take):
                P.wait(v, "sT")
                v.tensor_copy(out=ano[:, 0:qn], in_=take[0][1][0:65, 0:qn])
                v.drain()
                v.tensor_scalar(out=rno[:, 0:qn], in0=ano[64:65, 0:qn], scalar1=1e-8,
                                scalar2=None, op0=ALU.add)
                v.drain()
                v.reciprocal(out=rno[:, 0:qn], in_=rno[:, 0:qn])
                vv = P.incl(v, "sV")
                banks.set_free_at([take[0][0]], "sV", vv)
            P.sec("vector", v2)

            take2 = banks.take(1)

            def t3(t, q0=q0, qn=qn, take2=take2):
                banks.tensor_wait_free(t, [b for b, _ in take2])
                P.wait(t, "sV")
                inst = t.matmul(out=take2[0][1][0:64, 0:qn], lhsT=P.ones_col[:, 0:64],
                                rhs=rno[:, 0:qn], start=True, stop=True)
                P.inc(inst, "sT")
            P.sec("tensor", t3)

            def v3(v, h=h, q0=q0, qn=qn, take2=take2):
                P.wait(v, "sT")
                mr = (64 * h) % 128
                v.tensor_tensor(out=ofc[mr:mr + 64, q0:q0 + qn],
                                in0=ano[0:64, 0:qn], in1=take2[0][1][0:64, 0:qn],
                                op=ALU.mult)
                vv = P.incl(v, "sV")
                banks.set_free_at([take2[0][0]], "sV", vv)
            P.sec("vector", v3)

        if h % 2 == 1:
            def sof(e, h=h):
                P.wait(e, "sD")
                P.wait(e, "sV")
                P.inc(e.dma_start(out=of_h[:, h // 2, :], in_=ofc[:, :]), "sD", 16)
            P.sec("sync", sof)
            P.dma_wait("vector", "sD")

    aes.close()

    with ExitStack() as wes2:
        wo = emit_weight_load_cast(P, W["l2_wo"], "l2wo", DM, DM, es=wes2)
        bo = wes2.enter_context(P.nc.sbuf_tensor(P.uid("l2bo"), [1, DM], F32))
        ofs = wes2.enter_context(P.nc.sbuf_tensor(P.uid("l2_ofs"), [128, NDC, T], BF16))
        P.dma(bo[:], W["l2_bo"], queue="sync")
        P.dma(ofs[:, :, :], of_h[:, :, :], queue="sync", wait_engines=True)
        P.dma_wait("tensor", "sD")
        P.dma_wait("vector", "sD")
        for i in range(NTT):
            for seg in range(DM // 512):
                take = banks.take(1)

                def t(t, i=i, seg=seg, take=take):
                    banks.tensor_wait_free(t, [b for b, _ in take])
                    P.wait(t, "sV")
                    for c in range(NDC):
                        t.matmul(out=take[0][1][:, 0:512],
                                 lhsT=ofs[:, c, 128 * i:128 * (i + 1)],
                                 rhs=wo[:, c, 512 * seg:512 * (seg + 1)],
                                 start=(c == 0), stop=False)
                    inst = t.matmul(out=take[0][1][:, 0:512], lhsT=P.ones_col[:],
                                    rhs=bo[:, 512 * seg:512 * (seg + 1)], start=False, stop=True)
                    P.inc(inst, "sT")
                P.sec("tensor", t)

                def v7(v, i=i, seg=seg, take=take):
                    P.wait(v, "sT")
                    v.tensor_tensor(out=X[:, i, 512 * seg:512 * (seg + 1)],
                                    in0=X[:, i, 512 * seg:512 * (seg + 1)],
                                    in1=take[0][1][:, 0:512], op=ALU.add)
                    vv = P.incl(v, "sV")
                    banks.set_free_at([take[0][0]], "sV", vv)
                P.sec("vector", v7)
    es.close()



def emit_l2_ffn(P, banks, X, W):
    T, DM, NTT, NDC = P.T, P.DM, P.NTT, P.NDC
    DH = 4 * DM
    NHC = DH // 128
    es = ExitStack()
    sbl = lambda n, sh, dt=F32: es.enter_context(P.nc.sbuf_tensor(P.uid(n), sh, dt))
    h2_h = P.dram_tmp(P.uid("l2fh2"), [128, NHC, T], BF16)
    h2q_h = P.dram_tmp(P.uid("l2fh2q"), [128, NHC, T], BF16)
    row_h = P.dram_tmp(P.uid("l2frow"), [1, T], F32)

    dsc = sbl("l2f_dsc", [128, NTT])
    rmax = sbl("l2f_rmax", [128, T])
    h2b = sbl("l2f_h2b", [128, T], BF16)
    w1si = sbl("l2f_w1si", [128, 1])
    w2si = sbl("l2f_w2si", [128, 1])
    xs2 = ExitStack()
    x2l = lambda n, sh, dt=F32: xs2.enter_context(P.nc.sbuf_tensor(P.uid(n), sh, dt))
    XQT = x2l("l2f_xqt", [128, NDC, T], BF16)
    with ExitStack() as les:
        cs = min(512, DM)
        nch = DM // cs
        st = P.sb(P.uid("lq_st"), [128, 6 * nch], F32, es=les)
        mv = P.sb(P.uid("lq_mv"), [128, 2], F32, es=les)
        sd = P.sb(P.uid("lq_sd"), [128, 1], F32, es=les)
        rs = P.sb(P.uid("lq_rs"), [128, 1], F32, es=les)
        lnt = P.sb(P.uid("lq_buf"), [128, DM], F32, es=les)
        lni = P.sb(P.uid("lq_i"), [128, DM], F32, es=les)
        mx = P.sb(P.uid("lq_mx"), [128, 1], F32, es=les)
        for i in range(NTT):
            def v1(v, i=i):
                v.drain()
                for c in range(nch):
                    v.bn_stats(out=st[:, 6 * c:6 * c + 6], in_=X[:, i, cs * c:cs * (c + 1)])
                v.drain()
                v.bn_aggr(out=mv[:], in_=st[:])
                v.drain()
                v.tensor_scalar(out=sd[:], in0=mv[:, 1:2], scalar1=1e-5,
                                scalar2=None, op0=ALU.add)
                P.incl(v, "sV")
            P.sec("vector", v1)

            def s1(s):
                P.wait(s, "sV")
                s.activation(out=sd[:], in_=sd[:], func=ACT.Sqrt)
                P.incl(s, "sA")
            P.sec("scalar", s1)

            def v2(v, i=i):
                P.wait(v, "sA")
                v.reciprocal(out=rs[:], in_=sd[:])
                v.drain()
                v.tensor_scalar(out=lnt[:], in0=X[:, i, :],
                                scalar1=mv[:, 0:1], scalar2=rs[:, 0:1],
                                op0=ALU.subtract, op1=ALU.mult)
                v.drain()
                v.tensor_reduce(out=mx[:], in_=lnt[:], axis=AXX, op=ALU.max,
                                apply_absolute_value=True)
                v.drain()
                v.tensor_scalar(out=mx[:], in0=mx[:], scalar1=1e-5, scalar2=None,
                                op0=ALU.max)
                v.drain()
                v.tensor_scalar(out=dsc[:, i:i + 1], in0=mx[:], scalar1=1.0 / 127.0,
                                scalar2=None, op0=ALU.mult)
                v.drain()
                v.reciprocal(out=mx[:], in_=mx[:])
                v.drain()
                v.tensor_scalar(out=mx[:], in0=mx[:], scalar1=127.0, scalar2=None,
                                op0=ALU.mult)
                v.drain()
                v.tensor_scalar(out=lni[:], in0=lnt[:], scalar1=mx[:, 0:1],
                                scalar2=RND_C, op0=ALU.mult, op1=ALU.add)
                v.drain()
                v.tensor_scalar(out=lnt[:], in0=lni[:], scalar1=RND_C, scalar2=None,
                                op0=ALU.subtract)
                P.incl(v, "sV")
            P.sec("vector", v2)

            for c0 in range(0, NDC, 4):
                grp = list(range(c0, min(c0 + 4, NDC)))
                taken = banks.take(len(grp))

                def t(t, grp=grp, taken=taken):
                    banks.tensor_wait_free(t, [b for b, _ in taken])
                    P.wait(t, "sV")
                    inst = None
                    for (bi, bap), c in zip(taken, grp):
                        inst = t.transpose(out=bap[:, 0:128], in_=lnt[:, 128 * c:128 * (c + 1)],
                                           identity=P.ident_f[:])
                    P.inc(inst, "sT")
                P.sec("tensor", t)

                def s(s, grp=grp, taken=taken, i=i):
                    P.wait(s, "sT")
                    for (bi, bap), c in zip(taken, grp):
                        s.activation(out=XQT[:, c, 128 * i:128 * (i + 1)], in_=bap[:, 0:128],
                                     func=ACT.Copy)
                    vv = P.incl(s, "sA")
                    banks.set_free_at([bi for bi, _ in taken], "sA", vv)
                P.sec("scalar", s)

            def v3(v):
                P.wait(v, "sT")
                P.incl(v, "sV")
            P.sec("vector", v3)

    def quantize_weight2(hbm, name, din, dout, sinv):
        nchw = din // 128
        wq_h = P.dram_tmp(P.uid(name + "_qh"), [128, nchw, dout], BF16)
        with ExitStack() as ses:
            wrow = P.sb(P.uid(name + "_s"), [128, dout], F32, es=ses)
            wi = P.sb(P.uid(name + "_si"), [128, dout], F32, es=ses)
            wb = P.sb(P.uid(name + "_sb"), [128, dout], BF16, es=ses)
            acc = P.sb(P.uid(name + "_a"), [128, nchw], F32, es=ses)
            asl = P.sb(P.uid(name + "_l"), [128, 1], F32, es=ses)
            scl = P.sb(P.uid(name + "_c"), [128, 1], F32, es=ses)
            hv = hbm.rearrange("(c p) o -> p c o", p=128)
            for c in range(nchw):
                P.dma(wrow[:, :], hv[:, c, :], queue="sync", wait_engines=(c == 0))
                P.dma_wait("vector")

                def v(v, c=c):
                    v.drain()
                    v.tensor_reduce(out=acc[:, c:c + 1], in_=wrow[:, :], axis=AXX,
                                    op=ALU.add, apply_absolute_value=True)
                    P.incl(v, "sV")
                P.sec("vector", v)

                def g2(e):
                    P.wait(e, "sV")
                P.sec("sync", g2)

            def vr(v):
                v.tensor_reduce(out=asl[:], in_=acc[:, :], axis=AXX, op=ALU.add)
                P.incl(v, "sV")
            P.sec("vector", vr)

            take = banks.take(2)

            def tg(t, take=take):
                banks.tensor_wait_free(t, [b for b, _ in take])
                P.wait(t, "sV")
                t.matmul(out=take[0][1][0:1, 0:1], lhsT=asl[:], rhs=P.ones128[:],
                         start=True, stop=True)
                P.incl(t, "sT")
            P.sec("tensor", tg)

            def vg(v, take=take):
                P.wait(v, "sT")
                v.tensor_copy(out=scl[0:1, :], in_=take[0][1][0:1, 0:1])
                vv = P.incl(v, "sV")
                banks.set_free_at([take[0][0]], "sV", vv)
            P.sec("vector", vg)

            def tg2(t, take=take):
                banks.tensor_wait_free(t, [take[1][0]])
                P.wait(t, "sV")
                t.matmul(out=take[1][1][:, 0:1], lhsT=P.ones_col[:], rhs=scl[0:1, 0:1],
                         start=True, stop=True)
                P.incl(t, "sT")
            P.sec("tensor", tg2)

            def vg2(v, take=take):
                P.wait(v, "sT")
                v.tensor_copy(out=scl[:, :], in_=take[1][1][:, 0:1])
                vv = P.incl(v, "sV")
                banks.set_free_at([take[1][0]], "sV", vv)
            P.sec("vector", vg2)

            def vs(v, din=din, dout=dout):
                P.wait(v, "sV")
                v.tensor_scalar(out=scl[:], in0=scl[:], scalar1=1.0 / (din * dout),
                                scalar2=1e-5, op0=ALU.mult, op1=ALU.max)
                v.drain()
                v.tensor_copy(out=sinv[:], in_=scl[:])
                v.drain()
                v.reciprocal(out=scl[:], in_=scl[:])
                P.incl(v, "sV")
            P.sec("vector", vs)

            for c in range(nchw):
                P.dma(wrow[:, :], hv[:, c, :], queue="sync")
                P.dma_wait("vector")

                def v(v):
                    P.wait(v, "sD")
                    v.tensor_scalar(out=wi[:, :], in0=wrow[:, :],
                                    scalar1=scl[:, 0:1], scalar2=RND_C,
                                    op0=ALU.mult, op1=ALU.add)
                    v.drain()
                    v.tensor_scalar(out=wrow[:, :], in0=wi[:, :],
                                    scalar1=RND_C, scalar2=1.0,
                                    op0=ALU.subtract, op1=ALU.min)
                    v.drain()
                    v.tensor_scalar(out=wb[:, :], in0=wrow[:, :],
                                    scalar1=-1.0, scalar2=None, op0=ALU.max)
                    P.incl(v, "sV")
                P.sec("vector", v)

                def g3(e, c=c):
                    P.wait(e, "sD")
                    P.wait(e, "sV")
                    P.inc(e.dma_start(out=wq_h[:, c, :], in_=wb[:, :]), "sD", 16)
                P.sec("sync", g3)
            P.dma_wait("vector", "sD")
        return wq_h

    w1q = quantize_weight2(W["l2f_w1"], "l2fw1", DM, DH, w1si)
    cw = x2l("l2f_cw", [128, NHC, 3])
    cb = x2l("l2f_cb", [128, NHC])
    P.dma(cw[:, :, :], W["l2f_cw"].rearrange("p (c k) -> p c k", k=3), queue="sync")
    P.dma(cb[:, :], W["l2f_cb"], queue="sync")
    P.dma_wait("vector", "sD")

    dsc1 = x2l("l2f_dsc1", [128, NTT])

    def vd(v):
        v.tensor_scalar(out=dsc1[:], in0=dsc[:], scalar1=w1si[:, 0:1], scalar2=None,
                        op0=ALU.mult)
        P.incl(v, "sV")
    P.sec("vector", vd)

    DSCB = x2l("l2f_dscb", [128, T])
    dscr = x2l("l2f_dscr", [1, T])

    def sr(e):
        P.wait(e, "sD")
        P.wait(e, "sV")
        P.inc(e.dma_start(out=row_h.rearrange("o (i p) -> p (i o)", p=128), in_=dsc1[:, :],
                          allow_slow_non_contiguous=True), "sD", 16)
        P.wait(e, "sD")
        P.inc(e.dma_start(out=dscr[:], in_=row_h[:]), "sD", 16)
    P.sec("sync", sr)
    P.dma_wait("tensor", "sD")
    take = banks.take(T // 512)

    def tb(t, take=take):
        banks.tensor_wait_free(t, [b for b, _ in take])
        P.wait(t, "sD")
        inst = None
        for k, (bi, bap) in enumerate(take):
            inst = t.matmul(out=bap[:, 0:512], lhsT=P.ones_col[:],
                            rhs=dscr[:, 512 * k:512 * (k + 1)], start=True, stop=True)
        P.inc(inst, "sT")
    P.sec("tensor", tb)

    def vb(v, take=take):
        P.wait(v, "sT")
        for k, (bi, bap) in enumerate(take):
            v.tensor_copy(out=DSCB[:, 512 * k:512 * (k + 1)], in_=bap[:, 0:512])
        vv = P.incl(v, "sV")
        banks.set_free_at([b for b, _ in take], "sV", vv)
    P.sec("vector", vb)

    h1c = x2l("l2f_h1c", [128, T])
    ycv = x2l("l2f_ycv", [128, T])
    w1s = x2l("l2f_w1s", [128, NDC, 128], BF16)

    for m in range(NHC):
        def swm(e, m=m):
            P.wait(e, "sD2")
            P.wait(e, "sT")
            P.inc(e.dma_start(out=w1s[:, :, :], in_=w1q[:, :, 128 * m:128 * (m + 1)]),
                  "sD2", 16)
        P.sec("sync", swm)
        P.dma_wait("tensor", "sD2")
        for q0 in range(0, T, 512):
            qn = min(512, T - q0)
            take = banks.take(1)

            def t(t, m=m, q0=q0, qn=qn, take=take):
                banks.tensor_wait_free(t, [b for b, _ in take])
                P.wait(t, "sV")
                inst = None
                for c in range(NDC):
                    inst = t.matmul(out=take[0][1][:, 0:qn],
                                    lhsT=w1s[:, c, :],
                                    rhs=XQT[:, c, q0:q0 + qn],
                                    start=(c == 0), stop=(c == NDC - 1))
                P.inc(inst, "sT")
            P.sec("tensor", t)

            def v4(v, m=m, q0=q0, qn=qn, take=take):
                P.wait(v, "sT")
                v.tensor_tensor(out=h1c[:, q0:q0 + qn], in0=take[0][1][:, 0:qn],
                                in1=DSCB[:, q0:q0 + qn], op=ALU.mult)
                vv = P.incl(v, "sV")
                banks.set_free_at([take[0][0]], "sV", vv)
            P.sec("vector", v4)

        def v6(v, m=m):
            v.tensor_scalar(out=ycv[:], in0=h1c[:], scalar1=cw[:, m, 1:2],
                            scalar2=cb[:, m:m + 1], op0=ALU.mult, op1=ALU.add)
            v.drain()
            v.scalar_tensor_tensor(out=ycv[:, 1:T], in0=h1c[:, 0:T - 1],
                                   scalar=cw[:, m, 0:1], in1=ycv[:, 1:T],
                                   op0=ALU.mult, op1=ALU.add)
            v.drain()
            v.scalar_tensor_tensor(out=ycv[:, 0:T - 1], in0=h1c[:, 1:T],
                                   scalar=cw[:, m, 2:3], in1=ycv[:, 0:T - 1],
                                   op0=ALU.mult, op1=ALU.add)
            P.incl(v, "sV")
        P.sec("vector", v6)

        def s2(s):
            P.wait(s, "sV")
            P.wait(s, "sD")
            s.activation(out=h2b[:], in_=ycv[:], func=ACT.Silu)
            s.drain()
            s.activation(out=ycv[:], in_=ycv[:], func=ACT.Silu)
            P.incl(s, "sA")
        P.sec("scalar", s2)

        def v7(v, m=m):
            P.wait(v, "sA")
            if m == 0:
                v.scalar_tensor_tensor(out=rmax[:], in0=ycv[:], scalar=-1.0,
                                       in1=ycv[:], op0=ALU.mult, op1=ALU.max)
            else:
                v.tensor_tensor(out=rmax[:], in0=rmax[:], in1=ycv[:], op=ALU.max)
                v.drain()
                v.scalar_tensor_tensor(out=rmax[:], in0=ycv[:], scalar=-1.0,
                                       in1=rmax[:], op0=ALU.mult, op1=ALU.max)
            P.incl(v, "sV")
        P.sec("vector", v7)

        def so(e, m=m):
            P.wait(e, "sD")
            P.wait(e, "sA")
            P.inc(e.dma_start(out=h2_h[:, m, :], in_=h2b[:, :]), "sD", 16)
        P.sec("sync", so)

    def xcl(e):
        P.wait(e, "sD")
        for sname in ("sT", "sV", "sA"):
            P.wait(e, sname)
    P.sec("sync", xcl)
    xs2.close()

    w2q = quantize_weight2(W["l2f_w2"], "l2fw2", DH, DM, w2si)

    SCB2 = sbl("l2f_scb2", [128, T])
    dsc2 = sbl("l2f_dsc2", [128, NTT])
    mtm = sbl("l2f_mtm", [128, NTT])

    for i in range(NTT):
        take = banks.take(1)

        def tt(t, i=i, take=take):
            banks.tensor_wait_free(t, [b for b, _ in take])
            P.wait(t, "sV")
            inst = t.transpose(out=take[0][1][:, 0:128], in_=rmax[:, 128 * i:128 * (i + 1)],
                               identity=P.ident_f[:])
            P.inc(inst, "sT")
        P.sec("tensor", tt)

        def vm(v, i=i, take=take):
            P.wait(v, "sT")
            v.tensor_reduce(out=mtm[:, i:i + 1], in_=take[0][1][:, 0:128], axis=AXX,
                            op=ALU.max)
            vv = P.incl(v, "sV")
            banks.set_free_at([take[0][0]], "sV", vv)
        P.sec("vector", vm)

    def v8(v):
        v.tensor_scalar(out=mtm[:], in0=mtm[:], scalar1=1e-5, scalar2=None,
                        op0=ALU.max)
        v.drain()
        v.tensor_scalar(out=dsc2[:], in0=mtm[:], scalar1=w2si[:, 0:1],
                        scalar2=1.0 / 127.0, op0=ALU.mult, op1=ALU.mult)
        v.drain()
        v.reciprocal(out=mtm[:], in_=mtm[:])
        v.drain()
        v.tensor_scalar(out=mtm[:], in0=mtm[:], scalar1=127.0,
                        scalar2=None, op0=ALU.mult)
        P.incl(v, "sV")
    P.sec("vector", v8)

    def sr2(e):
        P.wait(e, "sD")
        P.wait(e, "sV")
        P.inc(e.dma_start(out=row_h.rearrange("o (i p) -> p (i o)", p=128), in_=mtm[:, :],
                          allow_slow_non_contiguous=True), "sD", 16)
        P.wait(e, "sD")
        P.inc(e.dma_start(out=SCB2[0:1, :], in_=row_h[:]), "sD", 16)
    P.sec("sync", sr2)
    P.dma_wait("tensor", "sD")
    take = banks.take(T // 512)

    def tb2(t, take=take):
        banks.tensor_wait_free(t, [b for b, _ in take])
        P.wait(t, "sD")
        inst = None
        for k, (bi, bap) in enumerate(take):
            inst = t.matmul(out=bap[:, 0:512], lhsT=P.ones_col[:],
                            rhs=SCB2[0:1, 512 * k:512 * (k + 1)], start=True, stop=True)
        P.inc(inst, "sT")
    P.sec("tensor", tb2)

    def vb2(v, take=take):
        P.wait(v, "sT")
        for k, (bi, bap) in enumerate(take):
            v.tensor_copy(out=SCB2[:, 512 * k:512 * (k + 1)], in_=bap[:, 0:512])
        vv = P.incl(v, "sV")
        banks.set_free_at([b for b, _ in take], "sV", vv)
    P.sec("vector", vb2)

    h2i = sbl("l2f_h2i", [128, T], F32)
    for m in range(NHC):
        def si(e, m=m):
            P.wait(e, "sD")
            P.wait(e, "sV")
            P.inc(e.dma_start(out=h2b[:, :], in_=h2_h[:, m, :]), "sD", 16)
        P.sec("sync", si)

        def v9(v, m=m):
            P.wait(v, "sD")
            v.tensor_tensor(out=h1c[:], in0=h2b[:], in1=SCB2[:], op=ALU.mult)
            v.drain()
            v.tensor_scalar(out=h2i[:], in0=h1c[:], scalar1=RND_C, scalar2=None,
                            op0=ALU.add)
            v.drain()
            v.tensor_scalar(out=h1c[:], in0=h2i[:], scalar1=RND_C, scalar2=127.0,
                            op0=ALU.subtract, op1=ALU.min)
            v.drain()
            v.tensor_scalar(out=h2b[:], in0=h1c[:], scalar1=-128.0, scalar2=None,
                            op0=ALU.max)
            P.incl(v, "sV")
        P.sec("vector", v9)

        def so2(e, m=m):
            P.wait(e, "sD")
            P.wait(e, "sV")
            P.inc(e.dma_start(out=h2q_h[:, m, :], in_=h2b[:, :]), "sD", 16)
        P.sec("sync", so2)

    h2qs = sbl("l2f_h2qs", [128, NHC, 512], BF16)
    w2s = sbl("l2f_w2s", [128, NHC, 512], BF16)
    tmp = sbl("l2f_tmp", [128, 512])
    for g4 in range(T // 512):
        def sl(e, g4=g4):
            P.wait(e, "sD")
            P.wait(e, "sT")
            P.inc(e.dma_start(out=h2qs[:, :, :], in_=h2q_h[:, :, 512 * g4:512 * (g4 + 1)]),
                  "sD", 16)
        P.sec("sync", sl)
        P.dma_wait("tensor", "sD")
        for seg in range(DM // 512):
            def sw2(e, seg=seg):
                P.wait(e, "sD2")
                P.wait(e, "sT")
                P.inc(e.dma_start(out=w2s[:, :, :], in_=w2q[:, :, 512 * seg:512 * (seg + 1)]),
                      "sD2", 16)
            P.sec("sync", sw2)
            P.dma_wait("tensor", "sD2")
            for ii in range(4):
                i = 4 * g4 + ii
                take = banks.take(1)

                def t(t, ii=ii, seg=seg, take=take):
                    banks.tensor_wait_free(t, [b for b, _ in take])
                    P.wait(t, "sV")
                    inst = None
                    for c in range(NHC):
                        inst = t.matmul(out=take[0][1][:, 0:512],
                                        lhsT=h2qs[:, c, 128 * ii:128 * (ii + 1)],
                                        rhs=w2s[:, c, :],
                                        start=(c == 0), stop=(c == NHC - 1))
                    P.inc(inst, "sT")
                P.sec("tensor", t)

                def v10(v, i=i, seg=seg, take=take):
                    P.wait(v, "sT")
                    sl_ = slice(512 * seg, 512 * (seg + 1))
                    v.tensor_scalar(out=tmp[:], in0=take[0][1][:, 0:512],
                                    scalar1=dsc2[:, i:i + 1], scalar2=None, op0=ALU.mult)
                    v.drain()
                    v.tensor_tensor(out=X[:, i, sl_], in0=X[:, i, sl_], in1=tmp[:],
                                    op=ALU.add)
                    vv = P.incl(v, "sV")
                    banks.set_free_at([take[0][0]], "sV", vv)
                P.sec("vector", v10)
    es.close()



def emit_l3_attn(P, banks, X, W, NL=4, NL_RUN=None):
    T, DM, NTT, NDC = P.T, P.DM, P.NTT, P.NDC
    DI = 2 * DM
    NH = DI // 32
    NG = NH // 4
    DS = 128
    CD = DI + 2 * DS
    NXC = CD // 128
    NIC = DI // 128
    DIP = 2 * DI + 2 * DS + NH
    SC = min(512, T)
    NSC = T // SC
    CPS = SC // 128
    assert NH <= 64
    es = ExitStack()
    sbl = lambda n, sh, dt=F32: es.enter_context(P.nc.sbuf_tensor(P.uid(n), sh, dt))

    h_h = P.dram_tmp(P.uid("l3h"), [128, NTT, DM])
    hn_h = P.dram_tmp(P.uid("l3hn"), [128, NDC, T], BF16)
    win_h = [P.dram_tmp(P.uid(f"l3winb{l}"), [128, NDC, DIP], BF16) for l in range(NL)]
    crow_h = P.dram_tmp(P.uid("l3crow"), [1, NH * 128])

    bnds = list(range(0, DI, 512)) + list(range(DI, DI + CD, 512))
    bnds += [DI + CD, DIP]
    bnds = sorted(set(b for b in bnds if b <= DIP))
    slices = [(a, b) for a, b in zip(bnds[:-1], bnds[1:])]

    triu = sbl("l3_triu", [128, 128], BF16)

    def v0(v):
        P.wait(v, "sG")
        v.tensor_scalar(out=triu[:], in0=P.iota_col[:], scalar1=0, scalar2=None,
                        op0=ALU.is_ge)
        P.incl(v, "sV")
    P.sec("vector", v0)

    bnw = sbl("l3_bnw", [128, NDC, NL])
    gw = sbl("l3_gw", [128, NIC, NL])
    dtb = sbl("l3_dtb", [NH, NL])
    alg = sbl("l3_alg", [NH, NL])
    av = sbl("l3_a", [NH, NL])
    cwt = sbl("l3_cw", [128, NL, NXC, 4])
    cbt = sbl("l3_cb", [128, NL, NXC])
    P.dma(bnw[:, :, :], W["l3_bnw"].rearrange("(c p) l -> p c l", p=128), queue="sync")
    P.dma(gw[:, :, :], W["l3_gw"].rearrange("(c p) l -> p c l", p=128), queue="sync")
    P.dma(dtb[:, :], W["l3_dtb"], queue="sync")
    P.dma(alg[:, :], W["l3_alog"], queue="sync")
    P.dma(cwt[:, :, :, :], W["l3_cw"].rearrange("p (l c k) -> p l c k", l=NL, k=4), queue="sync")
    P.dma(cbt[:, :, :], W["l3_cb"].rearrange("p (l c) -> p l c", l=NL), queue="sync")
    P.dma_wait("scalar", "sD")

    def s0(s):
        s.activation(out=av[:], in_=alg[:], func=ACT.Exp)
        P.incl(s, "sA")
    P.sec("scalar", s0)

    def v0b(v):
        P.wait(v, "sA")
        v.tensor_scalar(out=av[:], in0=av[:], scalar1=-1.0, scalar2=None, op0=ALU.mult)
        P.incl(v, "sV")
    P.sec("vector", v0b)

    ones64 = sbl("l3_ones64", [NH, 128])

    def v0c(v):
        v.memset(ones64[:], 1.0)
        P.incl(v, "sV")
    P.sec("vector", v0c)


    for l in range(NL_RUN if NL_RUN is not None else NL):
        with ExitStack() as ses:
            CS = 512
            stg = P.sb(P.uid("l3ws"), [128, NDC, CS], F32, es=ses)
            stgb = P.sb(P.uid("l3wsb"), [128, NDC, CS], BF16, es=ses)
            hv = W["l3_win"].rearrange("(l c p) o -> p l c o", p=128, l=NL)
            for o0 in range(0, DIP, CS):
                on = min(CS, DIP - o0)
                P.dma(stg[:, :, 0:on], hv[:, l, :, o0:o0 + on], queue="sync",
                      wait_engines=True)
                P.dma_wait("vector")

                def v(v, on=on, l=l):
                    P.wait(v, "sD2")
                    for c in range(NDC):
                        v.tensor_scalar(out=stgb[:, c, 0:on], in0=stg[:, c, 0:on],
                                        scalar1=bnw[:, c, l:l + 1], scalar2=None,
                                        op0=ALU.mult)
                    P.incl(v, "sV")
                P.sec("vector", v)

                def so(e, l=l, o0=o0, on=on):
                    P.wait(e, "sD2")
                    P.wait(e, "sV")
                    P.inc(e.dma_start(out=win_h[l][:, :, o0:o0 + on], in_=stgb[:, :, 0:on]),
                          "sD2", 16)
                P.sec("sync", so)
    P.dma_wait("vector", "sD2")

    with ExitStack() as les:
        cs = min(512, DM)
        nch = DM // cs
        st = P.sb(P.uid("l3_st"), [128, 6 * nch], F32, es=les)
        mv = P.sb(P.uid("l3_mv"), [128, 2], F32, es=les)
        sd = P.sb(P.uid("l3_sd"), [128, 1], F32, es=les)
        rs = P.sb(P.uid("l3_rs"), [128, 1], F32, es=les)
        lnt = P.sb(P.uid("l3_lnb"), [128, DM], F32, es=les)
        for i in range(NTT):
            def v1(v, i=i):
                v.drain()
                P.wait(v, "sD")
                for c in range(nch):
                    v.bn_stats(out=st[:, 6 * c:6 * c + 6], in_=X[:, i, cs * c:cs * (c + 1)])
                v.drain()
                v.bn_aggr(out=mv[:], in_=st[:])
                v.drain()
                v.tensor_scalar(out=sd[:], in0=mv[:, 1:2], scalar1=1e-5,
                                scalar2=None, op0=ALU.add)
                P.incl(v, "sV")
            P.sec("vector", v1)

            def s1(s):
                P.wait(s, "sV")
                s.activation(out=sd[:], in_=sd[:], func=ACT.Sqrt)
                P.incl(s, "sA")
            P.sec("scalar", s1)

            def v2(v, i=i):
                P.wait(v, "sA")
                v.reciprocal(out=rs[:], in_=sd[:])
                v.drain()
                v.tensor_scalar(out=lnt[:], in0=X[:, i, :],
                                scalar1=mv[:, 0:1], scalar2=rs[:, 0:1],
                                op0=ALU.subtract, op1=ALU.mult)
                P.incl(v, "sV")
            P.sec("vector", v2)

            def so(e, i=i):
                P.wait(e, "sD")
                P.wait(e, "sV")
                P.inc(e.dma_start(out=h_h[:, i, :], in_=lnt[:, :]), "sD", 16)
            P.sec("sync", so)

    for l in range(NL_RUN if NL_RUN is not None else NL):
        with ExitStack() as lyes:
            lsb = lambda n, sh, dt=F32: lyes.enter_context(P.nc.sbuf_tensor(P.uid(n), sh, dt))

            with ExitStack() as res_:
                cs = min(512, DM)
                nch = DM // cs
                st = P.sb(P.uid("l3r_st"), [128, 6 * nch], F32, es=res_)
                mv = P.sb(P.uid("l3r_mv"), [128, 2], F32, es=res_)
                sd = P.sb(P.uid("l3r_sd"), [128, 1], F32, es=res_)
                rs = P.sb(P.uid("l3r_rs"), [128, 1], F32, es=res_)
                lnt = P.sb(P.uid("l3r_b"), [128, DM], F32, es=res_)
                hstg = P.sb(P.uid("l3r_h"), [128, DM], F32, es=res_)
                hnstg = P.sb(P.uid("l3r_hn"), [128, NDC, 128], BF16, es=res_)
                for i in range(NTT):
                    def si(e, i=i):
                        P.wait(e, "sD")
                        P.wait(e, "sV")
                        P.inc(e.dma_start(out=hstg[:, :], in_=h_h[:, i, :]), "sD", 16)
                    P.sec("sync", si)

                    def v1(v, i=i):
                        P.wait(v, "sD")
                        for c in range(nch):
                            v.bn_stats(out=st[:, 6 * c:6 * c + 6], in_=hstg[:, cs * c:cs * (c + 1)])
                        v.drain()
                        v.bn_aggr(out=mv[:], in_=st[:])
                        v.drain()
                        v.tensor_scalar(out=sd[:], in0=mv[:, 0:1], scalar1=mv[:, 0:1],
                                        scalar2=None, op0=ALU.mult)
                        v.drain()
                        v.tensor_tensor(out=sd[:], in0=sd[:], in1=mv[:, 1:2], op=ALU.add)
                        v.drain()
                        v.tensor_scalar(out=sd[:], in0=sd[:], scalar1=1e-5, scalar2=None,
                                        op0=ALU.add)
                        P.incl(v, "sV")
                    P.sec("vector", v1)

                    def s1(s):
                        P.wait(s, "sV")
                        s.activation(out=sd[:], in_=sd[:], func=ACT.Sqrt)
                        P.incl(s, "sA")
                    P.sec("scalar", s1)

                    def v2(v, i=i):
                        P.wait(v, "sA")
                        v.reciprocal(out=rs[:], in_=sd[:])
                        v.drain()
                        v.tensor_scalar(out=lnt[:], in0=hstg[:, :],
                                        scalar1=rs[:, 0:1], scalar2=None, op0=ALU.mult)
                        P.incl(v, "sV")
                    P.sec("vector", v2)

                    for c0 in range(0, NDC, 4):
                        grp = list(range(c0, min(c0 + 4, NDC)))
                        taken = banks.take(len(grp))

                        def t(t, grp=grp, taken=taken):
                            banks.tensor_wait_free(t, [b_ for b_, _ in taken])
                            P.wait(t, "sV")
                            inst = None
                            for (bi, bap), c in zip(taken, grp):
                                inst = t.transpose(out=bap[:, 0:128],
                                                   in_=lnt[:, 128 * c:128 * (c + 1)],
                                                   identity=P.ident_f[:])
                            P.inc(inst, "sT")
                        P.sec("tensor", t)

                        def s(s, grp=grp, taken=taken):
                            P.wait(s, "sT")
                            for (bi, bap), c in zip(taken, grp):
                                s.activation(out=hnstg[:, c, :], in_=bap[:, 0:128],
                                             func=ACT.Copy)
                            vv = P.incl(s, "sA")
                            banks.set_free_at([bi for bi, _ in taken], "sA", vv)
                        P.sec("scalar", s)

                    def so(e, i=i):
                        P.wait(e, "sD")
                        P.wait(e, "sA")
                        P.inc(e.dma_start(out=hn_h[:, :, 128 * i:128 * (i + 1)],
                                          in_=hnstg[:, :, :]), "sD", 16)
                    P.sec("sync", so)

                    def v3(v):
                        P.wait(v, "sT")
                        P.incl(v, "sV")
                    P.sec("vector", v3)

            wout_h = P.dram_tmp(P.uid(f"l3woutb{l}"), [128, NIC, DM], BF16)
            with ExitStack() as ses:
                CS = 512
                stg = P.sb(P.uid("l3os"), [128, NIC, CS], F32, es=ses)
                stgb2 = P.sb(P.uid("l3osb"), [128, NIC, CS], BF16, es=ses)
                hv = W["l3_wout"].rearrange("(l c p) o -> p l c o", p=128, l=NL)
                for o0 in range(0, DM, CS):
                    P.dma(stg[:, :, :], hv[:, l, :, o0:o0 + CS], queue="sync",
                          wait_engines=(o0 == 0))
                    P.dma_wait("vector")

                    def v(v, o0=o0, l=l):
                        P.wait(v, "sD2")
                        for c in range(NIC):
                            v.tensor_scalar(out=stgb2[:, c, :], in0=stg[:, c, :],
                                            scalar1=gw[:, c, l:l + 1], scalar2=None,
                                            op0=ALU.mult)
                        P.incl(v, "sV")
                    P.sec("vector", v)

                    def g2(e, o0=o0):
                        P.wait(e, "sD2")
                        P.wait(e, "sV")
                        P.inc(e.dma_start(out=wout_h[:, :, o0:o0 + CS], in_=stgb2[:, :, :]),
                              "sD2", 16)
                    P.sec("sync", g2)
            P.dma_wait("vector", "sD2")
            wouts = lsb("l3_wouts", [128, NIC, 512], BF16)

            S = lsb("l3_S", [128, DI])
            S_bf = lsb("l3_Sbf", [128, DI], BF16)
            cum = lsb("l3_cum", [NH, SC])
            dts = lsb("l3_dts", [NH, SC])
            wsc = lsb("l3_wsc", [NH, SC])
            xext = lsb("l3_xext", [128, NXC, SC + 3], BF16)
            hal3 = lsb("l3_hal3", [128, NXC, 3], BF16)
            zt = lsb("l3_zt", [128, CPS, DI], BF16)
            hnsc = lsb("l3_hnsc", [128, NDC, SC], BF16)
            wins = lsb("l3_wins", [128, NDC, 512], BF16)
            dxb = lsb("l3_dxb", [128, DI], BF16)
            dtx = lsb("l3_dtx", [128, DI], BF16)
            Ysb = lsb("l3_Y", [128, DI])
            ygb = lsb("l3_yg", [128, DI], BF16)
            YGT = lsb("l3_ygt", [128, NIC, 128], BF16)
            TPc = lsb("l3_tp", [128, 64])
            DTt = lsb("l3_dtt", [128, 64])
            ETc = lsb("l3_et", [128, 64])
            RTc = lsb("l3_rt", [128, 64])
            LSc = lsb("l3_ls", [128, 64])
            Mh = lsb("l3_mh", [128, 64])
            cbs = lsb("l3_cbs", [128, 128], BF16)
            Msc = lsb("l3_msc", [128, 2, 512])
            scg = lsb("l3_scg", [128, 512], BF16)
            NCL = min(16, NG)
            CHALF = NH * 128 // NCL
            crow = lsb("l3_crow", [1, CHALF])
            ytmp = lsb("l3_ytmp", [128, DI], BF16)
            ycv = Msc[:, 0, :]
            rsn = lsb("l3_rsn", [128, 1])
            Bt = lsb("l3_bt", [128, 128], BF16)
            hres = lsb("l3_hres", [128, 512])

            def vS(v):
                P.wait(v, "sT")
                v.memset(S[:], 0.0)
                v.memset(S_bf[:], 0.0)
                P.incl(v, "sV")
            P.sec("vector", vS)

            for seg in range(0, DI, 512):
                P.dma(cum[0:1, :], W["l3_dexp"][0:1, DI * l + seg:DI * l + seg + 512],
                      queue="sync", wait_engines=True)
                P.dma_wait("tensor", "sD")
                take = banks.take(1)

                def t(t, seg=seg, take=take, l=l):
                    banks.tensor_wait_free(t, [b_ for b_, _ in take])
                    P.wait(t, "sD")
                    P.wait(t, "sV")
                    inst = t.matmul(out=take[0][1][:, 0:512], lhsT=P.ones_col[:],
                                    rhs=cum[0:1, 0:512], start=True, stop=True)
                    P.inc(inst, "sT")
                P.sec("tensor", t)

                def v(v, seg=seg, take=take):
                    P.wait(v, "sT")
                    v.tensor_copy(out=dxb[:, seg:seg + 512], in_=take[0][1][:, 0:512])
                    vv = P.incl(v, "sV")
                    banks.set_free_at([take[0][0]], "sV", vv)
                P.sec("vector", v)

            for sc in range(NSC):
                t0 = sc * SC

                def vh(v, sc=sc):
                    P.wait(v, "sT")
                    if sc == 0:
                        for j in range(NXC):
                            v.memset(xext[:, j, 0:3], 0.0)
                    else:
                        for j in range(NXC):
                            v.tensor_copy(out=xext[:, j, 0:3], in_=hal3[:, j, :])
                    P.incl(v, "sV")
                P.sec("vector", vh)

                def sl(e, t0=t0):
                    P.wait(e, "sD")
                    P.wait(e, "sT")
                    P.inc(e.dma_start(out=hnsc[:, :, :], in_=hn_h[:, :, t0:t0 + SC]), "sD", 16)
                P.sec("sync", sl)
                P.dma_wait("tensor", "sD")

                for (a, b) in slices:
                    on = b - a

                    def swl(e, l=l, a=a, on=on):
                        P.wait(e, "sD2")
                        P.wait(e, "sT")
                        P.inc(e.dma_start(out=wins[:, :, 0:on], in_=win_h[l][:, :, a:a + on]),
                              "sD2", 16)
                    P.sec("sync", swl)
                    P.dma_wait("tensor", "sD2")

                    if b <= DI:
                        for ii in range(CPS):
                            take = banks.take(1)

                            def t(t, ii=ii, on=on, take=take):
                                banks.tensor_wait_free(t, [b_ for b_, _ in take])
                                P.wait(t, "sD")
                                P.wait(t, "sV")
                                inst = None
                                for c in range(NDC):
                                    inst = t.matmul(out=take[0][1][:, 0:on],
                                                    lhsT=hnsc[:, c, 128 * ii:128 * (ii + 1)],
                                                    rhs=wins[:, c, 0:on],
                                                    start=(c == 0), stop=(c == NDC - 1))
                                P.inc(inst, "sT")
                            P.sec("tensor", t)

                            def s(s, ii=ii, a=a, on=on, take=take):
                                P.wait(s, "sT")
                                s.activation(out=zt[:, ii, a:a + on],
                                             in_=take[0][1][:, 0:on], func=ACT.Silu)
                                vv = P.incl(s, "sA")
                                banks.set_free_at([take[0][0]], "sA", vv)
                            P.sec("scalar", s)
                    elif b <= DI + CD:
                        for j in range(on // 128):
                            m = (a - DI) // 128 + j
                            take = banks.take(1)

                            def t(t, j=j, take=take):
                                banks.tensor_wait_free(t, [b_ for b_, _ in take])
                                P.wait(t, "sD")
                                P.wait(t, "sV")
                                inst = None
                                for c in range(NDC):
                                    inst = t.matmul(out=take[0][1][:, 0:SC],
                                                    lhsT=wins[:, c, 128 * j:128 * (j + 1)],
                                                    rhs=hnsc[:, c, :],
                                                    start=(c == 0), stop=(c == NDC - 1))
                                P.inc(inst, "sT")
                            P.sec("tensor", t)

                            def s(s, m=m, take=take):
                                P.wait(s, "sT")
                                s.activation(out=xext[:, m, 3:3 + SC], in_=take[0][1][:, 0:SC],
                                             func=ACT.Copy)
                                vv = P.incl(s, "sA")
                                banks.set_free_at([take[0][0]], "sA", vv)
                            P.sec("scalar", s)
                    else:
                        take = banks.take(1)

                        def t(t, take=take):
                            banks.tensor_wait_free(t, [b_ for b_, _ in take])
                            P.wait(t, "sD")
                            P.wait(t, "sV")
                            inst = None
                            for c in range(NDC):
                                inst = t.matmul(out=take[0][1][0:NH, 0:SC],
                                                lhsT=wins[:, c, 0:NH],
                                                rhs=hnsc[:, c, :],
                                                start=(c == 0), stop=(c == NDC - 1))
                            P.inc(inst, "sT")
                        P.sec("tensor", t)

                        def s(s, l=l, take=take):
                            P.wait(s, "sT")
                            s.activation(out=dts[:, :], in_=take[0][1][0:NH, 0:SC],
                                         func=ACT.Abs, bias=dtb[:, l:l + 1])
                            vv = P.incl(s, "sA")
                            banks.set_free_at([take[0][0]], "sA", vv)
                        P.sec("scalar", s)

                        def vsp(v, l=l, take=take):
                            P.wait(v, "sA")
                            v.tensor_scalar(out=wsc[:, :], in0=take[0][1][0:NH, 0:SC],
                                            scalar1=dtb[:, l:l + 1], scalar2=None,
                                            op0=ALU.add)
                            P.incl(v, "sV", barrier=False)
                        P.sec("vector", vsp)

                        def ssp(s):
                            s.activation(out=dts[:, :], in_=dts[:, :], func=ACT.Exp,
                                         scale=-1.0)
                            P.incl(s, "sA")
                        P.sec("scalar", ssp)

                        def vsp2(v):
                            P.wait(v, "sA")
                            v.tensor_scalar(out=dts[:, :], in0=dts[:, :], scalar1=1.0,
                                            scalar2=None, op0=ALU.add)
                            P.incl(v, "sV", barrier=False)
                        P.sec("vector", vsp2)

                        def ssp2(s):
                            P.wait(s, "sV")
                            s.activation(out=dts[:, :], in_=dts[:, :], func=ACT.Ln)
                            P.incl(s, "sA")
                        P.sec("scalar", ssp2)

                        def vsp3(v):
                            P.wait(v, "sA")
                            v.tensor_scalar(out=wsc[:, :], in0=wsc[:, :], scalar1=0.0,
                                            scalar2=None, op0=ALU.max)
                            v.drain()
                            v.tensor_tensor(out=dts[:, :], in0=dts[:, :], in1=wsc[:, :],
                                            op=ALU.add)
                            P.incl(v, "sV", barrier=False)
                        P.sec("vector", vsp3)

                for j in range(NXC):
                    def vc(v, l=l, j=j):
                        P.wait(v, "sA")
                        v.tensor_copy(out=hal3[:, j, :], in_=xext[:, j, SC:SC + 3])
                        v.drain()
                        v.tensor_scalar(out=ycv[:, :], in0=xext[:, j, 3:3 + SC],
                                        scalar1=cwt[:, l, j, 3:4], scalar2=cbt[:, l, j:j + 1],
                                        op0=ALU.mult, op1=ALU.add)
                        v.drain()
                        for k in range(3):
                            v.scalar_tensor_tensor(out=ycv[:, :], in0=xext[:, j, k:k + SC],
                                                   scalar=cwt[:, l, j, k:k + 1], in1=ycv[:, :],
                                                   op0=ALU.mult, op1=ALU.add)
                            v.drain()
                        P.incl(v, "sV", barrier=False)
                    P.sec("vector", vc)

                    def sc_(s, j=j):
                        P.wait(s, "sV")
                        s.activation(out=xext[:, j, 3:3 + SC], in_=ycv[:, :], func=ACT.Silu)
                        P.incl(s, "sA", barrier=False)
                    P.sec("scalar", sc_)

                    def vc2(v):
                        P.wait(v, "sA")
                        P.incl(v, "sV")
                    P.sec("vector", vc2)

                def vscan(v, l=l):
                    P.wait(v, "sA")
                    v.tensor_scalar(out=wsc[:, :], in0=dts[:, :], scalar1=av[:, l:l + 1],
                                    scalar2=None, op0=ALU.mult)
                    v.drain()
                    for cc in range(CPS):
                        v.tensor_tensor_scan(out=cum[:, 128 * cc:128 * (cc + 1)],
                                             data0=ones64[:, :],
                                             data1=wsc[:, 128 * cc:128 * (cc + 1)],
                                             initial=0.0, op0=ALU.mult, op1=ALU.add)
                    P.incl(v, "sV")
                P.sec("vector", vscan)

                for cc in range(CPS):
                    gch = sc * CPS + cc

                    def scr(e, cc=cc):
                        P.wait(e, "sD")
                        P.wait(e, "sV")
                        P.inc(e.dma_start(out=crow_h.rearrange("o (h t) -> h (t o)", h=NH),
                                          in_=cum[:, 128 * cc:128 * (cc + 1)]), "sD", 16)
                    P.sec("sync", scr)

                    take = banks.take(2) + banks.take_bf(1)

                    def tt(t, cc=cc, take=take):
                        banks.tensor_wait_free(t, [b_ for b_, _ in take])
                        P.wait(t, "sV")
                        P.wait(t, "sA")
                        t.transpose(out=take[0][1][:, 0:NH], in_=cum[:, 128 * cc:128 * (cc + 1)],
                                    identity=P.ident_f[0:NH, 0:NH])
                        t.transpose(out=take[1][1][:, 0:NH], in_=dts[:, 128 * cc:128 * (cc + 1)],
                                    identity=P.ident_f[0:NH, 0:NH])
                        inst = t.transpose(out=take[2][1][:, 0:128],
                                           in_=xext[:, NIC, 3 + 128 * cc:3 + 128 * (cc + 1)],
                                           identity=P.ident_b[:])
                        P.inc(inst, "sT")
                    P.sec("tensor", tt)

                    def se(s, take=take):
                        P.wait(s, "sT")
                        s.activation(out=TPc[:, 0:NH], in_=take[0][1][:, 0:NH], func=ACT.Copy)
                        s.drain()
                        s.activation(out=ETc[:, 0:NH], in_=take[0][1][:, 0:NH], func=ACT.Exp)
                        s.drain()
                        s.activation(out=DTt[:, 0:NH], in_=take[1][1][:, 0:NH], func=ACT.Copy)
                        s.drain()
                        s.activation(out=Bt[:, :], in_=take[2][1][:, 0:128], func=ACT.Copy)
                        vv = P.incl(s, "sA")
                        banks.set_free_at([b_ for b_, _ in take], "sA", vv)
                    P.sec("scalar", se)


                    for q0 in range(0, NIC, 2):
                        grp = list(range(q0, min(q0 + 2, NIC)))
                        taken = banks.take_bf(len(grp))

                        def t(t, grp=grp, taken=taken, cc=cc):
                            banks.tensor_wait_free(t, [b_ for b_, _ in taken])
                            P.wait(t, "sV")
                            P.wait(t, "sA")
                            inst = None
                            for (bi, bap), q in zip(taken, grp):
                                inst = t.transpose(out=bap[:, 0:128],
                                                   in_=xext[:, q, 3 + 128 * cc:3 + 128 * (cc + 1)],
                                                   identity=P.ident_b[:])
                            P.inc(inst, "sT")
                        P.sec("tensor", t)

                        def vdx(v, grp=grp, taken=taken):
                            P.wait(v, "sT")
                            for (bi, bap), q in zip(taken, grp):
                                v.tensor_tensor(out=r3(dtx[:, 128 * q:128 * (q + 1)], 4),
                                                in0=r3(bap[:, 0:128], 4),
                                                in1=bch(DTt[:, 4 * q:4 * q + 4], 4, 32),
                                                op=ALU.mult)
                                v.drain()
                                v.tensor_tensor(out=Ysb[:, 128 * q:128 * (q + 1)],
                                                in0=bap[:, 0:128],
                                                in1=dxb[:, 128 * q:128 * (q + 1)], op=ALU.mult)
                                v.drain()
                            vv = P.incl(v, "sV")
                            banks.set_free_at([bi for bi, _ in taken], "sV", vv)
                        P.sec("vector", vdx)

                    takeY = banks.take(DI // 512)

                    def ti(t, cc=cc, takeY=takeY):
                        banks.tensor_wait_free(t, [b_ for b_, _ in takeY])
                        P.wait(t, "sV")
                        inst = None
                        for k, (bi, bap) in enumerate(takeY):
                            inst = t.matmul(out=bap[:, 0:512],
                                            lhsT=xext[:, NIC + 1, 3 + 128 * cc:3 + 128 * (cc + 1)],
                                            rhs=S_bf[:, 512 * k:512 * (k + 1)],
                                            start=True, stop=True)
                        P.inc(inst, "sT")
                    P.sec("tensor", ti)

                    def vi(v, takeY=takeY):
                        P.wait(v, "sT")
                        for k, (bi, bap) in enumerate(takeY):
                            hs = 16 * k
                            v.tensor_tensor(out=r3(ytmp[:, 512 * k:512 * (k + 1)], 16),
                                            in0=r3(bap[:, 0:512], 16),
                                            in1=bch(ETc[:, hs:hs + 16], 16, 32),
                                            op=ALU.mult)
                        v.drain()
                        for k in range(len(takeY)):
                            v.tensor_tensor(out=Ysb[:, 512 * k:512 * (k + 1)],
                                            in0=Ysb[:, 512 * k:512 * (k + 1)],
                                            in1=ytmp[:, 512 * k:512 * (k + 1)], op=ALU.add)
                        vv = P.incl(v, "sV")
                        banks.set_free_at([b_ for b_, _ in takeY], "sV", vv)
                    P.sec("vector", vi)

                    take = banks.take(1)

                    def tc(t, cc=cc, take=take):
                        banks.tensor_wait_free(t, [b_ for b_, _ in take])
                        P.wait(t, "sA")
                        inst = t.matmul(out=take[0][1][:, 0:128],
                                        lhsT=xext[:, NIC, 3 + 128 * cc:3 + 128 * (cc + 1)],
                                        rhs=xext[:, NIC + 1, 3 + 128 * cc:3 + 128 * (cc + 1)],
                                        start=True, stop=True)
                        P.inc(inst, "sT")
                    P.sec("tensor", tc)

                    def vcb(v, take=take):
                        P.wait(v, "sT")
                        v.tensor_tensor(out=cbs[:, :], in0=take[0][1][:, 0:128], in1=triu[:, :],
                                        op=ALU.mult)
                        vv = P.incl(v, "sV")
                        banks.set_free_at([take[0][0]], "sV", vv)
                    P.sec("vector", vcb)

                    takeY2 = banks.take(DI // 512, hold=True)
                    NH2 = NH // NCL

                    H2G = max(1, NG // NCL)
                    tp1_val = {}
                    vgc_val = {}
                    vg2_val = {}
                    tin_val = {}
                    takes_g = {}

                    def emit_tp1(g4):
                        takes_g[g4] = banks.take(1)

                        def tp1(t, g4=g4, take=takes_g[g4]):
                            banks.tensor_wait_free(t, [b_ for b_, _ in take])
                            P.wait(t, "sD")
                            g4l = g4 % H2G
                            inst = t.matmul(out=take[0][1][:, 0:512], lhsT=P.ones_col[:],
                                            rhs=crow[:, 512 * g4l:512 * (g4l + 1)],
                                            start=True, stop=True)
                            tp1_val[g4] = P.inc(inst, "sT")
                        P.sec("tensor", tp1)

                    def emit_vgvgc(g4):
                        pg = g4 % 2

                        def vg(v, g4=g4, pg=pg, take=takes_g[g4]):
                            P.wait(v, "sT", tp1_val[g4])
                            v.tensor_tensor(out=r3(Msc[:, pg, :], 4),
                                            in0=r3(take[0][1][:, 0:512], 4),
                                            in1=bch(TPc[:, 4 * g4:4 * g4 + 4], 4, 128),
                                            op=ALU.subtract)
                            vv = P.incl(v, "sV")
                            banks.set_free_at([take[0][0]], "sV", vv)
                        P.sec("vector", vg)

                        def vgc(v, g4=g4, pg=pg):
                            v.tensor_scalar(out=Msc[:, pg, :], in0=Msc[:, pg, :],
                                            scalar1=0.0, scalar2=None, op0=ALU.min)
                            vgc_val[g4] = P.incl(v, "sV", barrier=False)
                        P.sec("vector", vgc)

                    for g4 in range(NG):
                        if g4 % max(1, NG // NCL) == 0:
                            half = g4 // max(1, NG // NCL)

                            def scl_(e, half=half):
                                P.wait(e, "sD")
                                P.wait(e, "sV")
                                P.inc(e.dma_start(out=crow[:, :],
                                                  in_=crow_h[:, CHALF * half:CHALF * (half + 1)]),
                                      "sD", 16)
                            P.sec("sync", scl_)

                            takeL = banks.take(1)

                            def tl(t, takeL=takeL, half=half):
                                banks.tensor_wait_free(t, [b_ for b_, _ in takeL])
                                P.wait(t, "sD")
                                cend = crow[:, :].rearrange("o (h t) -> o h t", h=NH2)[:, :, 127:128]
                                inst = t.matmul(out=takeL[0][1][:, 0:NH2], lhsT=P.ones_col[:],
                                                rhs=cend.rearrange("o h t -> o (h t)"),
                                                start=True, stop=True)
                                P.inc(inst, "sT")
                            P.sec("tensor", tl)

                            def sl2(s, takeL=takeL, half=half):
                                P.wait(s, "sT")
                                s.activation(out=LSc[:, NH2 * half:NH2 * (half + 1)],
                                             in_=takeL[0][1][:, 0:NH2], func=ACT.Exp)
                                P.incl(s, "sA", barrier=False)
                            P.sec("scalar", sl2)

                            def vm(v, takeL=takeL, half=half):
                                P.wait(v, "sA")
                                v.tensor_tensor(out=Mh[:, NH2 * half:NH2 * (half + 1)],
                                                in0=takeL[0][1][:, 0:NH2],
                                                in1=TPc[:, NH2 * half:NH2 * (half + 1)],
                                                op=ALU.subtract)
                                vv = P.incl(v, "sV", barrier=False)
                                banks.set_free_at([takeL[0][0]], "sV", vv)
                            P.sec("vector", vm)

                            def sr3(s, half=half):
                                P.wait(s, "sV")
                                s.activation(out=RTc[:, NH2 * half:NH2 * (half + 1)],
                                             in_=Mh[:, NH2 * half:NH2 * (half + 1)], func=ACT.Exp)
                                P.incl(s, "sA", barrier=False)
                            P.sec("scalar", sr3)

                            emit_tp1(g4)
                            emit_vgvgc(g4)
                        pg = g4 % 2

                        def sg(s, g4=g4, pg=pg):
                            P.wait(s, "sV", vgc_val[g4])
                            s.activation(out=Msc[:, pg, :], in_=Msc[:, pg, :], func=ACT.Exp)
                            P.incl(s, "sA", barrier=False)
                        P.sec("scalar", sg)

                        if g4 + 1 < NG and (g4 + 1) % H2G != 0:
                            emit_tp1(g4 + 1)
                            emit_vgvgc(g4 + 1)

                        def vg2(v, g4=g4, pg=pg):
                            P.wait(v, "sA")
                            if g4 >= 1:
                                P.wait(v, "sT", tin_val[g4 - 1])
                            v.tensor_tensor(out=r3(scg[:, :], 4), in0=r3(Msc[:, pg, :], 4),
                                            in1=cbs[:, :].rearrange("p (o w) -> p o w", o=1).to_broadcast([128, 4, 128]),
                                            op=ALU.mult)
                            vg2_val[g4] = P.incl(v, "sV", barrier=False)
                        P.sec("vector", vg2)

                        def tin(t, g4=g4, takeY2=takeY2):
                            if g4 == 0:
                                banks.tensor_wait_free(t, [b_ for b_, _ in takeY2])
                            P.wait(t, "sV", vg2_val[g4])
                            inst = None
                            for j in range(4):
                                h = 4 * g4 + j
                                bk, co = takeY2[(32 * h) // 512], (32 * h) % 512
                                inst = t.matmul(out=bk[1][:, co:co + 32],
                                                lhsT=scg[:, 128 * j:128 * (j + 1)],
                                                rhs=dtx[:, 32 * h:32 * h + 32],
                                                start=True, stop=True)
                            tin_val[g4] = P.inc(inst, "sT")
                        P.sec("tensor", tin)

                    def vy(v, takeY2=takeY2):
                        P.wait(v, "sT")
                        for k, (bi, bap) in enumerate(takeY2):
                            v.tensor_tensor(out=Ysb[:, 512 * k:512 * (k + 1)],
                                            in0=Ysb[:, 512 * k:512 * (k + 1)],
                                            in1=bap[:, 0:512], op=ALU.add)
                        vv = P.incl(v, "sV")
                        banks.set_free_at([b_ for b_, _ in takeY2], "sV", vv)
                    P.sec("vector", vy)
                    banks.release(takeY2)

                    def vgt(v, cc=cc):
                        v.tensor_tensor(out=ytmp[:, :], in0=Ysb[:, :], in1=zt[:, cc, :],
                                        op=ALU.mult)
                        P.incl(v, "sV", barrier=False)
                    P.sec("vector", vgt)

                    def sqg(s):
                        P.wait(s, "sV")
                        s.activation(out=Ysb[:, :], in_=ytmp[:, :], func=ACT.Square,
                                     accum_out=rsn[:, 0:1])
                        P.incl(s, "sA", barrier=False)
                    P.sec("scalar", sqg)

                    def vgt2(v):
                        P.wait(v, "sA")
                        v.tensor_scalar(out=rsn[:], in0=rsn[:], scalar1=1.0 / DI,
                                        scalar2=1e-5, op0=ALU.mult, op1=ALU.add)
                        P.incl(v, "sV", barrier=False)
                    P.sec("vector", vgt2)

                    def sgt(s):
                        P.wait(s, "sV")
                        s.activation(out=rsn[:], in_=rsn[:], func=ACT.Sqrt)
                        P.incl(s, "sA", barrier=False)
                    P.sec("scalar", sgt)

                    def vgt2(v):
                        P.wait(v, "sA")
                        v.reciprocal(out=rsn[:], in_=rsn[:])
                        v.drain()
                        v.tensor_scalar(out=ygb[:, :], in0=ytmp[:, :], scalar1=rsn[:, 0:1],
                                        scalar2=None, op0=ALU.mult)
                        P.incl(v, "sV", barrier=False)
                    P.sec("vector", vgt2)

                    for q0 in range(0, NIC, 2):
                        grp = list(range(q0, min(q0 + 2, NIC)))
                        taken = banks.take_bf(len(grp))

                        def t(t, grp=grp, taken=taken):
                            banks.tensor_wait_free(t, [b_ for b_, _ in taken])
                            P.wait(t, "sV")
                            inst = None
                            for (bi, bap), q in zip(taken, grp):
                                inst = t.transpose(out=bap[:, 0:128],
                                                   in_=ygb[:, 128 * q:128 * (q + 1)],
                                                   identity=P.ident_b[:])
                            P.inc(inst, "sT")
                        P.sec("tensor", t)

                        def s(s, grp=grp, taken=taken):
                            P.wait(s, "sT")
                            for (bi, bap), q in zip(taken, grp):
                                s.activation(out=YGT[:, q, :], in_=bap[:, 0:128], func=ACT.Copy)
                            vv = P.incl(s, "sA", barrier=False)
                            banks.set_free_at([bi for bi, _ in taken], "sA", vv)
                        P.sec("scalar", s)

                    for seg in range(0, DM, 512):
                        def shl(e, gch=gch, seg=seg):
                            P.wait(e, "sD")
                            P.wait(e, "sV")
                            P.inc(e.dma_start(out=hres[:, :],
                                              in_=h_h[:, gch, seg:seg + 512]), "sD", 16)
                        P.sec("sync", shl)
                        P.dma_wait("vector", "sD")

                        def sw3(e, seg=seg):
                            P.wait(e, "sD2")
                            P.wait(e, "sT")
                            P.inc(e.dma_start(out=wouts[:, :, :],
                                              in_=wout_h[:, :, seg:seg + 512]), "sD2", 16)
                        P.sec("sync", sw3)
                        P.dma_wait("tensor", "sD2")
                        take = banks.take(1)

                        def t(t, seg=seg, take=take):
                            banks.tensor_wait_free(t, [b_ for b_, _ in take])
                            P.wait(t, "sA")
                            P.wait(t, "sD2")
                            inst = None
                            for c in range(NIC):
                                inst = t.matmul(out=take[0][1][:, 0:512],
                                                lhsT=YGT[:, c, :],
                                                rhs=wouts[:, c, :],
                                                start=(c == 0), stop=(c == NIC - 1))
                            P.inc(inst, "sT")
                        P.sec("tensor", t)

                        def v(v, seg=seg, take=take):
                            P.wait(v, "sT")
                            P.wait(v, "sD")
                            v.tensor_tensor(out=hres[:, :], in0=hres[:, :],
                                            in1=take[0][1][:, 0:512], op=ALU.add)
                            vv = P.incl(v, "sV")
                            banks.set_free_at([take[0][0]], "sV", vv)
                        P.sec("vector", v)

                        def shs(e, gch=gch, seg=seg):
                            P.wait(e, "sD")
                            P.wait(e, "sV")
                            P.inc(e.dma_start(out=h_h[:, gch, seg:seg + 512],
                                              in_=hres[:, :]), "sD", 16)
                        P.sec("sync", shs)

                    def vsd(v):
                        for seg in range(0, DI, 512):
                            hs = seg // 32
                            v.tensor_tensor(out=r3(dtx[:, seg:seg + 512], 16),
                                            in0=r3(dtx[:, seg:seg + 512], 16),
                                            in1=bch(RTc[:, hs:hs + 16], 16, 32),
                                            op=ALU.mult)
                        P.incl(v, "sV")
                    P.sec("vector", vsd)

                    takeS = banks.take(DI // 512)

                    def ts_(t, takeS=takeS):
                        banks.tensor_wait_free(t, [b_ for b_, _ in takeS])
                        P.wait(t, "sV")
                        inst = None
                        for k, (bi, bap) in enumerate(takeS):
                            inst = t.matmul(out=bap[:, 0:512], lhsT=Bt[:, :],
                                            rhs=dtx[:, 512 * k:512 * (k + 1)],
                                            start=True, stop=True)
                        P.inc(inst, "sT")
                    P.sec("tensor", ts_)

                    def vst(v, takeS=takeS):
                        P.wait(v, "sT")
                        for seg in range(0, DI, 512):
                            hs = seg // 32
                            v.tensor_tensor(out=r3(S[:, seg:seg + 512], 16),
                                            in0=r3(S[:, seg:seg + 512], 16),
                                            in1=bch(LSc[:, hs:hs + 16], 16, 32),
                                            op=ALU.mult)
                        v.drain()
                        for k, (bi, bap) in enumerate(takeS):
                            v.tensor_tensor(out=S[:, 512 * k:512 * (k + 1)],
                                            in0=S[:, 512 * k:512 * (k + 1)],
                                            in1=bap[:, 0:512], op=ALU.add)
                        v.drain()
                        v.tensor_copy(out=S_bf[:, :], in_=S[:, :])
                        vv = P.incl(v, "sV")
                        banks.set_free_at([b_ for b_, _ in takeS], "sV", vv)
                    P.sec("vector", vst)

    with ExitStack() as fes:
        fnw = P.sb(P.uid("l3_fnw"), [1, DM], F32, es=fes)
        FNWB = P.sb(P.uid("l3_fnwb"), [128, DM], F32, es=fes)
        P.dma(fnw[:, :], W["l3_fnw"], queue="sync", wait_engines=True)
        P.dma_wait("tensor", "sD")
        for seg in range(0, DM, 512):
            take = banks.take(1)

            def t(t, seg=seg, take=take):
                banks.tensor_wait_free(t, [b for b, _ in take])
                P.wait(t, "sD")
                P.wait(t, "sV")
                inst = t.matmul(out=take[0][1][:, 0:512], lhsT=P.ones_col[:],
                                rhs=fnw[:, seg:seg + 512], start=True, stop=True)
                P.inc(inst, "sT")
            P.sec("tensor", t)

            def v(v, seg=seg, take=take):
                P.wait(v, "sT")
                v.tensor_copy(out=FNWB[:, seg:seg + 512], in_=take[0][1][:, 0:512])
                vv = P.incl(v, "sV")
                banks.set_free_at([take[0][0]], "sV", vv)
            P.sec("vector", v)
        hstg = P.sb(P.uid("l3f_h"), [128, DM], F32, es=fes)
        st = P.sb(P.uid("l3f_st"), [128, 6 * max(1, DM // 512)], F32, es=fes)
        mv = P.sb(P.uid("l3f_mv"), [128, 2], F32, es=fes)
        sd = P.sb(P.uid("l3f_sd"), [128, 1], F32, es=fes)
        rs = P.sb(P.uid("l3f_rs"), [128, 1], F32, es=fes)
        cs = min(512, DM)
        nch = DM // cs
        for i in range(NTT):
            def si(e, i=i):
                P.wait(e, "sD")
                P.wait(e, "sV")
                P.inc(e.dma_start(out=hstg[:, :], in_=h_h[:, i, :]), "sD", 16)
            P.sec("sync", si)

            def v1(v, i=i):
                P.wait(v, "sD")
                for c in range(nch):
                    v.bn_stats(out=st[:, 6 * c:6 * c + 6], in_=hstg[:, cs * c:cs * (c + 1)])
                v.drain()
                v.bn_aggr(out=mv[:], in_=st[:])
                v.drain()
                v.tensor_scalar(out=sd[:], in0=mv[:, 0:1], scalar1=mv[:, 0:1],
                                scalar2=None, op0=ALU.mult)
                v.drain()
                v.tensor_tensor(out=sd[:], in0=sd[:], in1=mv[:, 1:2], op=ALU.add)
                v.drain()
                v.tensor_scalar(out=sd[:], in0=sd[:], scalar1=1e-5, scalar2=None,
                                op0=ALU.add)
                P.incl(v, "sV")
            P.sec("vector", v1)

            def s1(s):
                P.wait(s, "sV")
                s.activation(out=sd[:], in_=sd[:], func=ACT.Sqrt)
                P.incl(s, "sA")
            P.sec("scalar", s1)

            def v2(v, i=i):
                P.wait(v, "sA")
                v.reciprocal(out=rs[:], in_=sd[:])
                v.drain()
                v.tensor_scalar(out=hstg[:, :], in0=hstg[:, :],
                                scalar1=rs[:, 0:1], scalar2=None, op0=ALU.mult)
                v.drain()
                v.tensor_tensor(out=hstg[:, :], in0=hstg[:, :], in1=FNWB[:, :], op=ALU.mult)
                v.drain()
                v.tensor_tensor(out=X[:, i, :], in0=X[:, i, :], in1=hstg[:, :], op=ALU.add)
                P.incl(v, "sV")
            P.sec("vector", v2)
    es.close()



def emit_allgather_max(P, banks, send, recv, nmax, ncores):
    P._xid = getattr(P, "_xid", -1) + 1
    cin_h = P.dram_tmp(f"ccin{P._xid}", [1, nmax])
    cout_h = P.dram_tmp(f"ccout{P._xid}", [1, nmax])
    P.dma(cin_h[:, :], send[0:1, 0:nmax], queue="sync", wait_engines=True)

    def g(gp):
        P.wait(gp, "sD")
        P.inc(gp.collective_compute(
            "AllReduce", ALU.max, replica_groups=[list(range(ncores))],
            ins=[cin_h[:, :]], outs=[cout_h[:, :]]), "sG")
    P.sec("gpsimd", g)

    def so(e):
        P.wait(e, "sG")
        P.wait(e, "sD")
        P.wait(e, "sT")
        P.inc(e.dma_start(out=recv[0:1, 0:nmax], in_=cout_h[:, :]), "sD", 16)
    P.sec("sync", so)

    take = banks.take(1)

    def t(t, take=take):
        banks.tensor_wait_free(t, [b for b, _ in take])
        P.wait(t, "sD")
        inst = t.matmul(out=take[0][1][:, 0:nmax], lhsT=P.ones_col[:],
                        rhs=recv[0:1, 0:nmax], start=True, stop=True)
        P.inc(inst, "sT")
    P.sec("tensor", t)

    def v(v, take=take):
        P.wait(v, "sT")
        v.tensor_copy(out=recv[:, 0:nmax], in_=take[0][1][:, 0:nmax])
        vv = P.incl(v, "sV")
        banks.set_free_at([take[0][0]], "sV", vv)
    P.sec("vector", v)


def emit_full_absmax(P, banks, acc_col, out11):
    take = banks.take(2)

    def t(t, take=take):
        banks.tensor_wait_free(t, [b for b, _ in take])
        P.wait(t, "sV")
        inst = t.transpose(out=take[0][1][0:1, 0:128], in_=acc_col[:, 0:1],
                           identity=P.ident_f[:])
        P.inc(inst, "sT")
    P.sec("tensor", t)

    def v(v, take=take):
        P.wait(v, "sT")
        v.tensor_reduce(out=out11[0:1, 0:1], in_=take[0][1][0:1, 0:128], axis=AXX,
                        op=ALU.max)
        vv = P.incl(v, "sV")
        banks.set_free_at([take[0][0]], "sV", vv)
    P.sec("vector", v)

    def t2(t, take=take):
        banks.tensor_wait_free(t, [take[1][0]])
        P.wait(t, "sV")
        inst = t.matmul(out=take[1][1][:, 0:1], lhsT=P.ones_col[:], rhs=out11[0:1, 0:1],
                        start=True, stop=True)
        P.inc(inst, "sT")
    P.sec("tensor", t2)

    def v2(v, take=take):
        P.wait(v, "sT")
        v.tensor_copy(out=out11[:, 0:1], in_=take[1][1][:, 0:1])
        vv = P.incl(v, "sV")
        banks.set_free_at([take[1][0]], "sV", vv)
    P.sec("vector", v2)


def emit_lpbq_transposed(P, banks, w_hbm, name, dout, din, wT_h):
    nb = din // 32
    with ExitStack() as ses:
        wrow = P.sb(P.uid(name + "_w"), [128, din], F32, es=ses)
        wi = P.sb(P.uid(name + "_i"), [128, din], F32, es=ses)
        wqb = P.sb(P.uid(name + "_q"), [128, din], BF16, es=ses)
        sblk = P.sb(P.uid(name + "_s"), [128, nb], F32, es=ses)
        sinv = P.sb(P.uid(name + "_v"), [128, nb], F32, es=ses)
        tstg = P.sb(P.uid(name + "_t"), [128, din // 128, 128], BF16, es=ses)
        for m in range(dout // 128):
            P.dma(wrow[:, :], w_hbm[128 * m:128 * (m + 1), :], queue="sync",
                  wait_engines=True)
            P.dma_wait("vector", "sD")

            _QC = int(_os.environ.get("L1F_QCUT", "99"))

            def v(v):
                v.drain()
                if _QC >= 1:
                    v.tensor_reduce(out=sblk[:, :],
                                    in_=wrow[:, :].rearrange("p (b k) -> p b k", k=32),
                                    axis=AXX, op=ALU.max, apply_absolute_value=True)
                    v.drain()
                if _QC >= 2:
                    v.tensor_scalar(out=sblk[:, :], in0=sblk[:, :], scalar1=1.0 / 7.0,
                                    scalar2=1e-12, op0=ALU.mult, op1=ALU.max)
                    v.drain()
                if _QC >= 3:
                    v.reciprocal(out=sinv[:, :], in_=sblk[:, :])
                    v.drain()
                if _QC >= 4:
                    v.tensor_tensor(out=wrow[:, :].rearrange("p (b k) -> p b k", k=32),
                                    in0=wrow[:, :].rearrange("p (b k) -> p b k", k=32),
                                    in1=sinv[:, :].rearrange("p (b o) -> p b o", o=1).to_broadcast([128, nb, 32]),
                                    op=ALU.mult)
                P.incl(v, "sV")
            P.sec("vector", v)

            def v2(v):
                if _QC >= 5:
                    v.tensor_scalar(out=wi[:, :], in0=wrow[:, :], scalar1=RND_C,
                                    scalar2=None, op0=ALU.add)
                    v.drain()
                if _QC >= 6:
                    v.tensor_scalar(out=wrow[:, :], in0=wi[:, :], scalar1=RND_C,
                                    scalar2=7.0, op0=ALU.subtract, op1=ALU.min)
                    v.drain()
                if _QC >= 7:
                    v.tensor_scalar(out=wrow[:, :], in0=wrow[:, :], scalar1=-8.0,
                                    scalar2=None, op0=ALU.max)
                    v.drain()
                if _QC >= 8:
                    v.tensor_tensor(out=wqb[:, :].rearrange("p (b k) -> p b k", k=32),
                                    in0=wrow[:, :].rearrange("p (b k) -> p b k", k=32),
                                    in1=sblk[:, :].rearrange("p (b o) -> p b o", o=1).to_broadcast([128, nb, 32]),
                                    op=ALU.mult)
                v.drain()
                P.incl(v, "sV")
            P.sec("vector", v2)

            for c0 in range(0, din // 128, 2):
                sub = list(range(c0, min(c0 + 2, din // 128)))
                tk = banks.take_bf(len(sub))

                def t(t, sub=sub, tk=tk):
                    banks.tensor_wait_free(t, [b for b, _ in tk])
                    P.wait(t, "sV")
                    inst = None
                    for (bi, bap), c in zip(tk, sub):
                        inst = t.transpose(out=bap[:, 0:128],
                                           in_=wqb[:, 128 * c:128 * (c + 1)],
                                           identity=P.ident_b[:])
                    P.inc(inst, "sT")
                P.sec("tensor", t)

                def s(s, sub=sub, tk=tk):
                    P.wait(s, "sT")
                    for (bi, bap), c in zip(tk, sub):
                        s.activation(out=tstg[:, c, :], in_=bap[:, 0:128], func=ACT.Copy)
                    vv = P.incl(s, "sA")
                    banks.set_free_at([bi for bi, _ in tk], "sA", vv)
                P.sec("scalar", s)

            def so(e, m=m):
                P.wait(e, "sD")
                P.wait(e, "sA")
                P.inc(e.dma_start(out=wT_h[:, :, 128 * m:128 * (m + 1)], in_=tstg[:, :, :]),
                      "sD", 16)
            P.sec("sync", so)
    P.dma_wait("vector", "sD")


def emit_l1_ffn(P, banks, X, W, ncores=8):
    T, DM, NTT, NDC = P.T, P.DM, P.NTT, P.NDC

    DH = 3 * DM
    NHC = DH // 128
    es = ExitStack()
    sbl = lambda n, sh, dt=F32: es.enter_context(P.nc.sbuf_tensor(P.uid(n), sh, dt))

    wgT_h = P.dram_tmp(P.uid("l1fwg"), [128, NDC, DH], BF16)
    wuT_h = P.dram_tmp(P.uid("l1fwu"), [128, NDC, DH], BF16)
    wdT_h = P.dram_tmp(P.uid("l1fwd"), [128, NHC, DM], BF16)
    up_h = P.dram_tmp(P.uid("l1fup"), [128, NTT, DH], BF16)
    gt_h = P.dram_tmp(P.uid("l1fgt"), [128, NTT, DH], BF16)

    _CUT = int(_os.environ.get("L1F_CUT", "99"))
    emit_lpbq_transposed(P, banks, W["l1f_wg"], "l1fwg", DH, DM, wgT_h)
    if _CUT <= 1: return
    emit_lpbq_transposed(P, banks, W["l1f_wu"], "l1fwu", DH, DM, wuT_h)
    emit_lpbq_transposed(P, banks, W["l1f_wd"], "l1fwd", DM, DH, wdT_h)
    if _CUT <= 2: return

    send4 = sbl("l1f_send", [128, 4, 2])
    recv4 = sbl("l1f_recv", [128, 4, 2])
    gmax = sbl("l1f_gmax", [128, 4])
    acc = sbl("l1f_acc", [128, 1])
    m11 = sbl("l1f_m11", [128, 1])
    sx = sbl("l1f_sx", [128, 2])

    with ExitStack() as les:
        cs = min(512, DM)
        nch = DM // cs
        st = P.sb(P.uid("lf_st"), [128, 6 * nch], F32, es=les)
        mv = P.sb(P.uid("lf_mv"), [128, 2], F32, es=les)
        sd = P.sb(P.uid("lf_sd"), [128, 1], F32, es=les)
        rs = P.sb(P.uid("lf_rs"), [128, 1], F32, es=les)
        lnt = P.sb(P.uid("lf_b"), [128, DM], F32, es=les)
        tmx = P.sb(P.uid("lf_mx"), [128, 1], F32, es=les)

        def emit_ln_tile(i, post):
            def v1(v, i=i):
                v.drain()
                for c in range(nch):
                    v.bn_stats(out=st[:, 6 * c:6 * c + 6], in_=X[:, i, cs * c:cs * (c + 1)])
                v.drain()
                v.bn_aggr(out=mv[:], in_=st[:])
                v.drain()
                v.tensor_scalar(out=sd[:], in0=mv[:, 1:2], scalar1=1e-5,
                                scalar2=None, op0=ALU.add)
                P.incl(v, "sV")
            P.sec("vector", v1)

            def s1(s):
                P.wait(s, "sV")
                s.activation(out=sd[:], in_=sd[:], func=ACT.Sqrt)
                P.incl(s, "sA")
            P.sec("scalar", s1)

            def v2(v, i=i):
                P.wait(v, "sA")
                v.reciprocal(out=rs[:], in_=sd[:])
                v.drain()
                v.tensor_scalar(out=lnt[:], in0=X[:, i, :],
                                scalar1=mv[:, 0:1], scalar2=rs[:, 0:1],
                                op0=ALU.subtract, op1=ALU.mult)
                P.incl(v, "sV")
            P.sec("vector", v2)
            post(i)

        def postA(i):
            def v3(v, i=i):
                v.tensor_reduce(out=tmx[:], in_=lnt[:], axis=AXX, op=ALU.max,
                                apply_absolute_value=True)
                v.drain()
                if i == 0:
                    v.tensor_copy(out=acc[:], in_=tmx[:])
                else:
                    v.tensor_tensor(out=acc[:], in0=acc[:], in1=tmx[:], op=ALU.max)
                P.incl(v, "sV")
            P.sec("vector", v3)

        for i in range(NTT):
            emit_ln_tile(i, postA)
        emit_full_absmax(P, banks, acc, m11)

        def vsend(v):
            v.tensor_copy(out=send4[:, 0, 0:1], in_=m11[:])
            v.drain()
            v.tensor_copy(out=send4[:, 0, 1:2], in_=m11[:])
            P.incl(v, "sV")
        P.sec("vector", vsend)
        if _CUT == 3: return
        emit_allgather_max(P, banks, send4[:, 0, :], recv4[:, 0, :], 2, ncores)
        if _CUT == 35: return

        def vscale(v):
            v.tensor_copy(out=gmax[:, 0:1], in_=recv4[:, 0, 0:1])
            v.drain()
            v.tensor_scalar(out=sx[:, 1:2], in0=gmax[:, 0:1], scalar1=1.0 / 32767.0,
                            scalar2=1e-12 / 32767.0, op0=ALU.mult, op1=ALU.max)
            v.drain()
            v.reciprocal(out=sx[:, 0:1], in_=sx[:, 1:2])
            P.incl(v, "sV")
        P.sec("vector", vscale)

        rmx = P.sb(P.uid("l1f_rmx"), [128, 2], F32, es=les)
        qscope = ExitStack()
        XQT = P.sb(P.uid("l1f_xqt"), [128, NDC, T], BF16, es=qscope)
        lni = P.sb(P.uid("lf_i"), [128, DM], F32, es=qscope)

        def postB(i):
            def v4(v, i=i):
                v.tensor_scalar(out=lni[:], in0=lnt[:], scalar1=sx[:, 0:1],
                                scalar2=RND_C, op0=ALU.mult, op1=ALU.add)
                v.drain()
                v.tensor_scalar(out=lnt[:], in0=lni[:], scalar1=RND_C,
                                scalar2=sx[:, 1:2], op0=ALU.subtract, op1=ALU.mult)
                P.incl(v, "sV")
            P.sec("vector", v4)

            for c0 in range(0, NDC, 4):
                grp = list(range(c0, min(c0 + 4, NDC)))
                taken = banks.take(len(grp))

                def t(t, grp=grp, taken=taken):
                    banks.tensor_wait_free(t, [b for b, _ in taken])
                    P.wait(t, "sV")
                    inst = None
                    for (bi, bap), c in zip(taken, grp):
                        inst = t.transpose(out=bap[:, 0:128],
                                           in_=lnt[:, 128 * c:128 * (c + 1)],
                                           identity=P.ident_f[:])
                    P.inc(inst, "sT")
                P.sec("tensor", t)

                def s(s, grp=grp, taken=taken, i=i):
                    P.wait(s, "sT")
                    for (bi, bap), c in zip(taken, grp):
                        s.activation(out=XQT[:, c, 128 * i:128 * (i + 1)], in_=bap[:, 0:128],
                                     func=ACT.Copy)
                    vv = P.incl(s, "sA")
                    banks.set_free_at([bi for bi, _ in taken], "sA", vv)
                P.sec("scalar", s)

            def v5(v):
                P.wait(v, "sT")
                P.incl(v, "sV")
            P.sec("vector", v5)

        for i in range(NTT):
            emit_ln_tile(i, postB)
        if _CUT <= 4: return

        wslc = P.sb(P.uid("l1f_ws"), [128, NDC, 512], BF16, es=qscope)
        ostg = P.sb(P.uid("l1f_os"), [128, 512], BF16, es=qscope)

        for wi_, (wTh, dst_h) in enumerate(((wuT_h, up_h), (wgT_h, gt_h))):
            for seg in range(DH // 512):
                def swl(e, wTh=wTh, seg=seg):
                    P.wait(e, "sD2")
                    P.wait(e, "sT")
                    P.inc(e.dma_start(out=wslc[:, :, :], in_=wTh[:, :, 512 * seg:512 * (seg + 1)]),
                          "sD2", 16)
                P.sec("sync", swl)
                P.dma_wait("tensor", "sD2")
                for i in range(NTT):
                    take = banks.take(1)

                    def t(t, i=i, take=take):
                        banks.tensor_wait_free(t, [b for b, _ in take])
                        P.wait(t, "sV")
                        P.wait(t, "sA")
                        inst = None
                        for c in range(NDC):
                            inst = t.matmul(out=take[0][1][:, 0:512],
                                            lhsT=XQT[:, c, 128 * i:128 * (i + 1)],
                                            rhs=wslc[:, c, :],
                                            start=(c == 0), stop=(c == NDC - 1))
                        P.inc(inst, "sT")
                    P.sec("tensor", t)

                    def v6(v, i=i, seg=seg, wi_=wi_, take=take):
                        P.wait(v, "sT")
                        P.wait(v, "sD")
                        v.tensor_copy(out=ostg[:, :], in_=take[0][1][:, 0:512])
                        v.drain()
                        v.tensor_reduce(out=acc[:], in_=take[0][1][:, 0:512], axis=AXX,
                                        op=ALU.max, apply_absolute_value=True)
                        v.drain()
                        if i == 0 and seg == 0:
                            v.tensor_copy(out=rmx[:, wi_:wi_ + 1], in_=acc[:])
                        else:
                            v.tensor_tensor(out=rmx[:, wi_:wi_ + 1], in0=rmx[:, wi_:wi_ + 1],
                                            in1=acc[:], op=ALU.max)
                        vv = P.incl(v, "sV")
                        banks.set_free_at([take[0][0]], "sV", vv)
                    P.sec("vector", v6)

                    def so(e, i=i, seg=seg, dst_h=dst_h):
                        P.wait(e, "sD")
                        P.wait(e, "sV")
                        P.inc(e.dma_start(out=dst_h[:, i, 512 * seg:512 * (seg + 1)],
                                          in_=ostg[:, :]), "sD", 16)
                    P.sec("sync", so)

        def vq_close(e):
            P.wait(e, "sD")
            for sname in ("sT", "sV", "sA"):
                P.wait(e, sname)
        P.sec("sync", vq_close)
        qscope.close()
        if _CUT <= 5: return

        def vA(v):
            v.tensor_copy(out=acc[:], in_=rmx[:, 0:1])
            P.incl(v, "sV")
        P.sec("vector", vA)
        emit_full_absmax(P, banks, acc, m11)

        def vA2(v):
            v.tensor_copy(out=send4[:, 1, 0:1], in_=m11[:])
            v.drain()
            v.tensor_copy(out=acc[:], in_=rmx[:, 1:2])
            P.incl(v, "sV")
        P.sec("vector", vA2)
        emit_full_absmax(P, banks, acc, m11)

        def vA3(v):
            v.tensor_copy(out=send4[:, 1, 1:2], in_=m11[:])
            P.incl(v, "sV")
        P.sec("vector", vA3)
        emit_allgather_max(P, banks, send4[:, 1, :], recv4[:, 1, :], 2, ncores)

        def vA4(v):
            v.tensor_copy(out=gmax[:, 1:2], in_=recv4[:, 1, 0:1])
            v.drain()
            v.tensor_copy(out=gmax[:, 2:3], in_=recv4[:, 1, 1:2])
            P.incl(v, "sV")
        P.sec("vector", vA4)

        sx2 = P.sb(P.uid("l1f_sx2"), [128, 8], F32, es=les)

        def emit_scales(mcol, dstcol):
            def v(v):
                v.tensor_scalar(out=sx2[:, 2 * dstcol + 1:2 * dstcol + 2], in0=mcol,
                                scalar1=1.0 / 32767.0, scalar2=1e-12 / 32767.0,
                                op0=ALU.mult, op1=ALU.max)
                v.drain()
                v.reciprocal(out=sx2[:, 2 * dstcol:2 * dstcol + 1],
                             in_=sx2[:, 2 * dstcol + 1:2 * dstcol + 2])
                P.incl(v, "sV")
            P.sec("vector", v)

        emit_scales(gmax[:, 1:2], 0)
        emit_scales(gmax[:, 2:3], 1)
        if _CUT <= 6: return

        ub = P.sb(P.uid("l1f_ub"), [128, DH], BF16, es=les)
        gb = P.sb(P.uid("l1f_gb"), [128, DH], BF16, es=les)
        f1 = P.sb(P.uid("l1f_f1"), [128, DH], F32, es=les)
        f2 = P.sb(P.uid("l1f_f2"), [128, DH], F32, es=les)
        fi = P.sb(P.uid("l1f_fi"), [128, DH], F32, es=les)

        def qdq_inplace(v, buf, ibuf, invcol, sclcol):
            v.tensor_scalar(out=ibuf, in0=buf, scalar1=invcol, scalar2=RND_C,
                            op0=ALU.mult, op1=ALU.add)
            v.drain()
            v.tensor_scalar(out=buf, in0=ibuf, scalar1=RND_C, scalar2=sclcol,
                            op0=ALU.subtract, op1=ALU.mult)
            v.drain()

        def emit_t2_tile(i, track):
            def sl(e, i=i):
                P.wait(e, "sD")
                P.wait(e, "sV")
                P.wait(e, "sA")
                P.inc(e.dma_start(out=gb[:, :], in_=gt_h[:, i, :]), "sD", 16)
            P.sec("sync", sl)

            def v(v, i=i):
                P.wait(v, "sD")
                v.tensor_copy(out=f1[:, :], in_=gb[:, :])
                v.drain()
                qdq_inplace(v, f1[:, :], fi[:, :], sx2[:, 2:3], sx2[:, 3:4])
                P.incl(v, "sV")
            P.sec("vector", v)

            def s(s):
                P.wait(s, "sV")
                s.activation(out=f2[:, :], in_=f1[:, :], func=ACT.Sigmoid)
                P.incl(s, "sA")
            P.sec("scalar", s)

            def v2(v):
                P.wait(v, "sA")
                v.tensor_scalar(out=fi[:, :], in0=f2[:, :], scalar1=65536.0,
                                scalar2=RND_C, op0=ALU.mult, op1=ALU.add)
                v.drain()
                v.tensor_scalar(out=f2[:, :], in0=fi[:, :], scalar1=RND_C + 65535.0,
                                scalar2=RND_C, op0=ALU.min, op1=ALU.subtract)
                v.drain()
                v.scalar_tensor_tensor(out=f1[:, :], in0=f2[:, :], scalar=1.0 / 65536.0,
                                       in1=f1[:, :], op0=ALU.mult, op1=ALU.mult)
                P.incl(v, "sV")
            P.sec("vector", v2)
            track(i)

        def trackT2(i):
            def v3(v, i=i):
                v.tensor_reduce(out=acc[:], in_=f1[:, :], axis=AXX, op=ALU.max,
                                apply_absolute_value=True)
                v.drain()
                if i == 0:
                    v.tensor_copy(out=rmx[:, 0:1], in_=acc[:])
                else:
                    v.tensor_tensor(out=rmx[:, 0:1], in0=rmx[:, 0:1], in1=acc[:],
                                    op=ALU.max)
                P.incl(v, "sV")
            P.sec("vector", v3)

        for i in range(NTT):
            emit_t2_tile(i, trackT2)

        def vB(v):
            v.tensor_copy(out=acc[:], in_=rmx[:, 0:1])
            P.incl(v, "sV")
        P.sec("vector", vB)
        emit_full_absmax(P, banks, acc, m11)

        def vB2(v):
            v.tensor_copy(out=send4[:, 2, 0:1], in_=m11[:])
            v.drain()
            v.tensor_copy(out=send4[:, 2, 1:2], in_=m11[:])
            P.incl(v, "sV")
        P.sec("vector", vB2)
        emit_allgather_max(P, banks, send4[:, 2, :], recv4[:, 2, :], 2, ncores)

        def vB3(v):
            v.tensor_copy(out=gmax[:, 3:4], in_=recv4[:, 2, 0:1])
            P.incl(v, "sV")
        P.sec("vector", vB3)
        emit_scales(gmax[:, 3:4], 2)
        if _CUT <= 7: return

        def emit_p_tile(i, track):
            emit_t2_tile(i, lambda _i: None)

            def sl(e, i=i):
                P.wait(e, "sD")
                P.wait(e, "sV")
                P.inc(e.dma_start(out=ub[:, :], in_=up_h[:, i, :]), "sD", 16)
            P.sec("sync", sl)

            def v(v, i=i):
                P.wait(v, "sD")
                qdq_inplace(v, f1[:, :], fi[:, :], sx2[:, 4:5], sx2[:, 5:6])
                v.tensor_copy(out=f2[:, :], in_=ub[:, :])
                v.drain()
                qdq_inplace(v, f2[:, :], fi[:, :], sx2[:, 0:1], sx2[:, 1:2])
                v.tensor_tensor(out=f1[:, :], in0=f1[:, :], in1=f2[:, :], op=ALU.mult)
                P.incl(v, "sV")
            P.sec("vector", v)
            track(i)

        for i in range(NTT):
            emit_p_tile(i, trackT2)

        def vC(v):
            v.tensor_copy(out=acc[:], in_=rmx[:, 0:1])
            P.incl(v, "sV")
        P.sec("vector", vC)
        emit_full_absmax(P, banks, acc, m11)

        def vC2(v):
            v.tensor_copy(out=send4[:, 3, 0:1], in_=m11[:])
            v.drain()
            v.tensor_copy(out=send4[:, 3, 1:2], in_=m11[:])
            P.incl(v, "sV")
        P.sec("vector", vC2)
        emit_allgather_max(P, banks, send4[:, 3, :], recv4[:, 3, :], 2, ncores)

        def vC3(v):
            v.tensor_copy(out=gmax[:, 0:1], in_=recv4[:, 3, 0:1])
            P.incl(v, "sV")
        P.sec("vector", vC3)
        emit_scales(gmax[:, 0:1], 3)
        if _CUT <= 8: return

        OQT = P.sb(P.uid("l1f_oqt"), [128, NHC, 128], BF16, es=les)
        wds = P.sb(P.uid("l1f_wds"), [128, NHC, 512], BF16, es=les)

        for i in range(NTT):
            emit_p_tile(i, lambda _i: None)

            def vq(v):
                qdq_inplace(v, f1[:, :], fi[:, :], sx2[:, 6:7], sx2[:, 7:8])
                v.tensor_copy(out=gb[:, :], in_=f1[:, :])
                P.incl(v, "sV")
            P.sec("vector", vq)

            for c0 in range(0, NHC, 2):
                grp = list(range(c0, min(c0 + 2, NHC)))
                taken = banks.take_bf(len(grp))

                def t(t, grp=grp, taken=taken):
                    banks.tensor_wait_free(t, [b for b, _ in taken])
                    P.wait(t, "sV")
                    inst = None
                    for (bi, bap), c in zip(taken, grp):
                        inst = t.transpose(out=bap[:, 0:128],
                                           in_=gb[:, 128 * c:128 * (c + 1)],
                                           identity=P.ident_b[:])
                    P.inc(inst, "sT")
                P.sec("tensor", t)

                def s(s, grp=grp, taken=taken):
                    P.wait(s, "sT")
                    for (bi, bap), c in zip(taken, grp):
                        s.activation(out=OQT[:, c, :], in_=bap[:, 0:128], func=ACT.Copy)
                    vv = P.incl(s, "sA")
                    banks.set_free_at([bi for bi, _ in taken], "sA", vv)
                P.sec("scalar", s)

            for seg in range(DM // 512):
                def swl(e, seg=seg):
                    P.wait(e, "sD2")
                    P.wait(e, "sT")
                    P.inc(e.dma_start(out=wds[:, :, :], in_=wdT_h[:, :, 512 * seg:512 * (seg + 1)]),
                          "sD2", 16)
                P.sec("sync", swl)
                P.dma_wait("tensor", "sD2")
                take = banks.take(1)

                def t(t, seg=seg, take=take):
                    banks.tensor_wait_free(t, [b for b, _ in take])
                    P.wait(t, "sA")
                    inst = None
                    for c in range(NHC):
                        inst = t.matmul(out=take[0][1][:, 0:512],
                                        lhsT=OQT[:, c, :],
                                        rhs=wds[:, c, :],
                                        start=(c == 0), stop=(c == NHC - 1))
                    P.inc(inst, "sT")
                P.sec("tensor", t)

                def v(v, i=i, seg=seg, take=take):
                    P.wait(v, "sT")
                    sl_ = slice(512 * seg, 512 * (seg + 1))
                    v.tensor_tensor(out=X[:, i, sl_], in0=X[:, i, sl_],
                                    in1=take[0][1][:, 0:512], op=ALU.add)
                    vv = P.incl(v, "sV")
                    banks.set_free_at([take[0][0]], "sV", vv)
                P.sec("vector", v)
    es.close()



WEIGHT_SPECS = {
    "l0_wq": (1024, 1024), "l0_wk": (1024, 1024), "l0_wv": (1024, 1024),
    "l0_wo": (1024, 1024), "l0_bo": (1, 1024),
    "l0f_wr": (1024, 1024), "l0f_wk": (1024, 1024), "l0f_wv": (1024, 1024),
    "l0f_mur": (128, 8), "l0f_muk": (128, 8),
    "l2f_w1": (1024, 4096), "l2f_cw": (128, 96), "l2f_cb": (128, 32),
    "l2f_w2": (4096, 1024),
    "l1f_wg": (3072, 1024), "l1f_wu": (3072, 1024), "l1f_wd": (1024, 3072),
    "l1_wiq": (1024, 128), "l1_wik": (1024, 128), "l1_wiw": (1024, 4),
    "l1_gb": (1, 4), "l1_hib": (1, 16), "l1_wq": (1024, 1024),
    "l1_wk": (1024, 1024), "l1_wv": (1024, 1024), "l1_wgv": (1024, 1024),
    "l1_wgo": (1024, 1024), "l1_wo": (1024, 1024),
    "l3_win": (4096, 4416), "l3_wout": (8192, 1024), "l3_cw": (128, 288),
    "l3_cb": (128, 72), "l3_dtb": (64, 4), "l3_alog": (64, 4),
    "l3_dexp": (1, 8192), "l3_gw": (2048, 4), "l3_bnw": (1024, 4),
    "l3_fnw": (1, 1024),
    "l2_wq": (1024, 1024), "l2_wk": (1024, 1024), "l2_wv": (1024, 1024),
    "l2_proj": (64, 256), "l2_wo": (1024, 1024), "l2_bo": (1, 1024),
    "l3f_win": (1024, 64), "l3f_bin": (128, 1), "l3f_w1": (128, 16),
    "l3f_b1": (128, 16), "l3f_w2": (128, 16), "l3f_b2": (128, 1),
    "l3f_wout": (128, 1024), "l3f_bout": (1, 1024),
}

BLOCK_WEIGHTS = {
    "l0a": ["l0_wq", "l0_wk", "l0_wv", "l0_wo", "l0_bo"],
    "l0f": ["l0f_wr", "l0f_wk", "l0f_wv", "l0f_mur", "l0f_muk"],
    "l2a": ["l2_wq", "l2_wk", "l2_wv", "l2_proj", "l2_wo", "l2_bo"],
    "l2f": ["l2f_w1", "l2f_cw", "l2f_cb", "l2f_w2"],
    "l3a": ["l3_win", "l3_wout", "l3_cw", "l3_cb", "l3_dtb", "l3_alog",
            "l3_dexp", "l3_gw", "l3_bnw", "l3_fnw"],
    "l1f": ["l1f_wg", "l1f_wu", "l1f_wd"],
    "l1a": ["l1_wiq", "l1_wik", "l1_wiw", "l1_gb", "l1_hib", "l1_wq", "l1_wk",
            "l1_wv", "l1_wgv", "l1_wgo", "l1_wo"],
    "l3f": ["l3f_win", "l3f_bin", "l3f_w1", "l3f_b1", "l3f_w2", "l3f_b2",
            "l3f_wout", "l3f_bout"],
}

BLOCK_EMITTERS = {}


def _reg():
    BLOCK_EMITTERS["l0a"] = emit_l0_attn
    BLOCK_EMITTERS["l0f"] = emit_l0_ffn
    BLOCK_EMITTERS["l3f"] = emit_l3_ffn
    BLOCK_EMITTERS["l2a"] = emit_l2_attn
    BLOCK_EMITTERS["l2f"] = emit_l2_ffn
    BLOCK_EMITTERS["l3a"] = emit_l3_attn
    BLOCK_EMITTERS["l1f"] = emit_l1_ffn


_reg()


def build_program(cfg, blocks, wspecs=None):
    wspecs = wspecs or WEIGHT_SPECS
    P = Prog(cfg)
    T, DM = P.T, P.DM
    x_h = P.dram_in("x", [T, DM])
    W = {}
    for b in blocks:
        for wn in BLOCK_WEIGHTS[b]:
            W[wn] = P.dram_in(wn, wspecs[wn])
    y_h = P.dram_out("y", [T, DM])

    P.begin()
    emit_const_setup(P)
    X = P.sb("X", [128, P.NTT, DM])
    banks = Banks(P)
    P.dma(X[:, :, :], x_h.rearrange("(i p) d -> p i d", p=128), queue="sync")
    P.dma_wait("vector", "sD")

    def v(v):
        P.incl(v, "sV")
    P.sec("vector", v)

    for b in blocks:
        BLOCK_EMITTERS[b](P, banks, X, W)

    def sync_out(e):
        P.wait(e, "sD")
        P.wait(e, "sV")
        P.wait(e, "sA")
        P.wait(e, "sT")
        P.inc(e.dma_start(out=y_h.rearrange("(i p) d -> p i d", p=128), in_=X[:, :, :]), "sD", 16)
        P.wait(e, "sD")
    P.sec("sync", sync_out)
    P.finish()
    return P


def kernel(**inputs):
    g = host_layout_full(inputs)
    x = np.ascontiguousarray(np.asarray(inputs["L0_attn_x"], dtype=np.float32))
    blocks = ["l0a", "l0f", "l1a", "l1f", "l2a", "l2f", "l3a", "l3f"]
    wspecs = {k: v.shape for k, v in g.items()}
    P = build_program(FULL_CFG, blocks, wspecs=wspecs)
    in_maps = []
    for core in range(8):
        m = {"x": x[core]}
        for b in blocks:
            for wn in BLOCK_WEIGHTS[b]:
                m[wn] = g[wn]
        in_maps.append(m)
    res = run_bass_kernel_spmd(P.nc, in_maps, core_ids=list(range(8)))
    return np.stack([np.asarray(r["y"], dtype=np.float32) for r in res.results], axis=0)


def host_layout_full(inputs):
    g = {}
    f32 = lambda a: np.ascontiguousarray(np.asarray(a), dtype=np.float32)
    DMv = 1024
    Wqkv = np.asarray(inputs["L0_attn_Wqkv"])
    hd = 64
    H = 16
    perm0 = []
    for g4 in range(H // 4):
        for h in range(4 * g4, 4 * g4 + 4):
            perm0 += list(range(h * hd, h * hd + 32))
        for h in range(4 * g4, 4 * g4 + 4):
            perm0 += list(range(h * hd + 32, h * hd + 64))
    perm0 = np.asarray(perm0)
    g["l0_wq"] = f32(Wqkv[:, 0:DMv][:, perm0])
    g["l0_wk"] = f32(Wqkv[:, DMv:2 * DMv][:, perm0])
    g["l0_wv"] = f32(Wqkv[:, 2 * DMv:3 * DMv])
    g["l0_wo"] = f32(inputs["L0_attn_Wo"])
    g["l0_bo"] = f32(inputs["L0_attn_bo"]).reshape(1, DMv)
    g["l0f_wr"] = f32(np.asarray(inputs["L0_ffn_W_r"]).T)
    g["l0f_wk"] = f32(np.asarray(inputs["L0_ffn_W_k"]).T)
    g["l0f_wv"] = f32(np.asarray(inputs["L0_ffn_W_v"]).T)
    g["l0f_mur"] = f32(np.asarray(inputs["L0_ffn_mu_r"]).reshape(DMv // 128, 128).T)
    g["l0f_muk"] = f32(np.asarray(inputs["L0_ffn_mu_k"]).reshape(DMv // 128, 128).T)
    H1 = 16
    permA = []
    for g4 in range(H1 // 4):
        for h in range(4 * g4, 4 * g4 + 4):
            permA += [h * 64 + 2 * i for i in range(32)]
        for h in range(4 * g4, 4 * g4 + 4):
            permA += [h * 64 + 2 * i + 1 for i in range(32)]
    permA = np.asarray(permA)
    g["l1_wiq"] = f32(np.asarray(inputs["L1_attn_W_Iq"]).T)
    g["l1_wik"] = f32(np.asarray(inputs["L1_attn_W_Ik"]).T)
    g["l1_wiw"] = f32(np.asarray(inputs["L1_attn_W_Iw"]).T)
    g["l1_gb"] = f32(np.asarray(inputs["L1_attn_gate_bias"]).reshape(1, -1))
    g["l1_hib"] = f32(np.asarray(inputs["L1_attn_head_importance_bias"]).reshape(1, -1))
    g["l1_wq"] = f32(np.asarray(inputs["L1_attn_W_q"]).T[:, permA])
    g["l1_wk"] = f32(np.asarray(inputs["L1_attn_W_k"]).T[:, permA])
    g["l1_wv"] = f32(np.asarray(inputs["L1_attn_W_v"]).T)
    g["l1_wgv"] = f32(np.asarray(inputs["L1_attn_W_gv"]).T)
    g["l1_wgo"] = f32(np.asarray(inputs["L1_attn_W_go"]).T)
    g["l1_wo"] = f32(np.asarray(inputs["L1_attn_W_o"]).T)
    g["l1f_wg"] = f32(inputs["L1_ffn_w_gate"])
    g["l1f_wu"] = f32(inputs["L1_ffn_w_up"])
    g["l1f_wd"] = f32(inputs["L1_ffn_w_down"])
    g["l2_wq"] = f32(inputs["L2_attn_Wq"])
    g["l2_wk"] = f32(inputs["L2_attn_Wk"])
    g["l2_wv"] = f32(inputs["L2_attn_Wv"])
    g["l2_proj"] = f32(inputs["L2_attn_proj"])
    g["l2_wo"] = f32(inputs["L2_attn_Wo"])
    g["l2_bo"] = f32(np.asarray(inputs["L2_attn_bo"]).reshape(1, DMv))
    g["l2f_w1"] = f32(np.asarray(inputs["L2_ffn_w1"]).T)
    g["l2f_w2"] = f32(np.asarray(inputs["L2_ffn_w2"]).T)
    cw_ = np.asarray(inputs["L2_ffn_conv_w"]).reshape(-1, 3)
    g["l2f_cw"] = f32(cw_.reshape(-1, 128, 3).transpose(1, 0, 2).reshape(128, -1))
    g["l2f_cb"] = f32(np.asarray(inputs["L2_ffn_conv_b"]).reshape(-1, 128).T)
    NLv = 4
    DIv = 2 * DMv
    CDv = DIv + 256
    NXCv = CDv // 128
    g["l3_win"] = f32(np.asarray(inputs["L3_attn_W_in"]).reshape(NLv * DMv, -1))
    g["l3_wout"] = f32(np.asarray(inputs["L3_attn_W_out"]).reshape(NLv * DIv, DMv))
    cw3 = np.asarray(inputs["L3_attn_conv_w"])
    g["l3_cw"] = f32(np.stack([cw3[l].reshape(NXCv, 128, 4).transpose(1, 0, 2)
                               for l in range(NLv)], axis=1).reshape(128, -1))
    cb3 = np.asarray(inputs["L3_attn_conv_b"])
    g["l3_cb"] = f32(np.stack([cb3[l].reshape(NXCv, 128).T for l in range(NLv)],
                              axis=1).reshape(128, -1))
    g["l3_dtb"] = f32(np.asarray(inputs["L3_attn_dt_bias"]).T)
    g["l3_alog"] = f32(np.asarray(inputs["L3_attn_A_log"]).T)
    g["l3_dexp"] = f32(np.repeat(np.asarray(inputs["L3_attn_D"]), 32, axis=1).reshape(1, -1))
    g["l3_gw"] = f32(np.asarray(inputs["L3_attn_gate_norm_w"]).T)
    g["l3_bnw"] = f32(np.asarray(inputs["L3_attn_block_norm_w"]).T)
    g["l3_fnw"] = f32(np.asarray(inputs["L3_attn_final_norm_w"]).reshape(1, DMv))
    g["l3f_win"] = f32(inputs["L3_ffn_Win"])
    g["l3f_bin"] = f32(np.tile(np.asarray(inputs["L3_ffn_bin_"]).reshape(64, 1), (2, 1)))
    g["l3f_w1"] = f32(np.tile(np.asarray(inputs["L3_ffn_W1"]), (2, 1)))
    g["l3f_b1"] = f32(np.tile(np.asarray(inputs["L3_ffn_b1"]), (2, 1)))
    g["l3f_w2"] = f32(np.tile(np.asarray(inputs["L3_ffn_W2"]), (2, 1)))
    g["l3f_b2"] = f32(np.tile(np.asarray(inputs["L3_ffn_b2"]).reshape(64, 1), (2, 1)))
    g["l3f_wout"] = f32(np.tile(np.asarray(inputs["L3_ffn_Wout"]), (2, 1)))
    g["l3f_bout"] = f32(np.asarray(inputs["L3_ffn_bout"]).reshape(1, DMv))
    return g



def emit_l1_attn(P, banks, X, W, n_iter=13):
    T, DM, NTT, NDC = P.T, P.DM, P.NTT, P.NDC
    H = DM // 64
    HI = H // 4
    DIX = 32 * HI
    QB = 512
    NQB = T // QB
    es = ExitStack()
    sbl = lambda n, sh, dt=F32: es.enter_context(P.nc.sbuf_tensor(P.uid(n), sh, dt))

    q_h = P.dram_tmp(P.uid("l1q"), [128, NDC, T], BF16)
    k_h = P.dram_tmp(P.uid("l1k"), [128, NDC, T], BF16)
    v_h = P.dram_tmp(P.uid("l1v"), [128, NTT, DM], BF16)
    row_h = P.dram_tmp(P.uid("l1row"), [1, T])

    LNXT = sbl("l1_lnxt", [128, NDC, T], BF16)
    aes = ExitStack()
    asl_ = lambda n, sh, dt=F32: aes.enter_context(P.nc.sbuf_tensor(P.uid(n), sh, dt))
    dtri = asl_("l1_dtri", [128, 4, 512], BF16)
    tpd_ = asl_("l1_tpd", [128, 512])

    def vdt(v):
        v.drain()
        v.tensor_copy(out=tpd_[:], in_=P.tramp128[:, 0:512])
        v.drain()
        v.tensor_scalar(out=tpd_[:], in0=tpd_[:], scalar1=P.prow_f[:, 0:1],
                        scalar2=None, op0=ALU.subtract)
        v.drain()
        for d in range(4):
            v.tensor_scalar(out=dtri[:, d, :], in0=tpd_[:], scalar1=float(128 * d),
                            scalar2=None, op0=ALU.is_ge)
        P.incl(v, "sV")
    P.sec("vector", vdt)
    qI = asl_("l1_qi", [128, T])
    kI = asl_("l1_ki", [128, T])
    wT = asl_("l1_wt", [128, NTT, HI])
    gbb = asl_("l1_gbb", [128, HI])
    kfacb = asl_("l1_kfac", [128, H])

    with ExitStack() as pes:
        wiq = emit_weight_load_cast(P, W["l1_wiq"], "l1wiq", DM, DIX, to_bf16=False, es=pes)
        wik = emit_weight_load_cast(P, W["l1_wik"], "l1wik", DM, DIX, to_bf16=False, es=pes)
        wiw = emit_weight_load_cast(P, W["l1_wiw"], "l1wiw", DM, HI, to_bf16=False, es=pes)
        gbf = pes.enter_context(P.nc.sbuf_tensor(P.uid("l1_gbf"), [1, HI], F32))
        hibf = pes.enter_context(P.nc.sbuf_tensor(P.uid("l1_hibf"), [1, H], F32))
        P.dma(gbf[:], W["l1_gb"], queue="sync")
        P.dma(hibf[:], W["l1_hib"], queue="sync")
        P.dma_wait("scalar", "sD")
        wsg = pes.enter_context(P.nc.sbuf_tensor(P.uid("l1_wsg"), [HI, T], F32))
        xtf = pes.enter_context(P.nc.sbuf_tensor(P.uid("l1_xtf"), [128, NDC, 128], F32))
        cs = min(512, DM)
        nch = DM // cs
        st = pes.enter_context(P.nc.sbuf_tensor(P.uid("l1_st"), [128, 6 * nch], F32))
        mv = pes.enter_context(P.nc.sbuf_tensor(P.uid("l1_mv"), [128, 2], F32))
        sd = pes.enter_context(P.nc.sbuf_tensor(P.uid("l1_sd"), [128, 1], F32))
        rs = pes.enter_context(P.nc.sbuf_tensor(P.uid("l1_rs"), [128, 1], F32))
        lnt = pes.enter_context(P.nc.sbuf_tensor(P.uid("l1_lnb"), [128, DM], F32))

        for i in range(NTT):
            def v1(v, i=i):
                v.drain()
                for c in range(nch):
                    v.bn_stats(out=st[:, 6 * c:6 * c + 6], in_=X[:, i, cs * c:cs * (c + 1)])
                v.drain()
                v.bn_aggr(out=mv[:], in_=st[:])
                v.drain()
                v.tensor_scalar(out=sd[:], in0=mv[:, 1:2], scalar1=1e-5,
                                scalar2=None, op0=ALU.add)
                P.incl(v, "sV")
            P.sec("vector", v1)

            def s1(s):
                P.wait(s, "sV")
                s.activation(out=sd[:], in_=sd[:], func=ACT.Sqrt)
                P.incl(s, "sA")
            P.sec("scalar", s1)

            def v2(v, i=i):
                P.wait(v, "sA")
                v.reciprocal(out=rs[:], in_=sd[:])
                v.drain()
                v.tensor_scalar(out=lnt[:], in0=X[:, i, :],
                                scalar1=mv[:, 0:1], scalar2=rs[:, 0:1],
                                op0=ALU.subtract, op1=ALU.mult)
                P.incl(v, "sV")
            P.sec("vector", v2)

            for c0 in range(0, NDC, 4):
                grp = list(range(c0, min(c0 + 4, NDC)))
                taken = banks.take(len(grp))

                def t(t, grp=grp, taken=taken):
                    banks.tensor_wait_free(t, [b for b, _ in taken])
                    P.wait(t, "sV")
                    inst = None
                    for (bi, bap), c in zip(taken, grp):
                        inst = t.transpose(out=bap[:, 0:128],
                                           in_=lnt[:, 128 * c:128 * (c + 1)],
                                           identity=P.ident_f[:])
                    P.inc(inst, "sT")
                P.sec("tensor", t)

                def s(s, grp=grp, taken=taken, i=i):
                    P.wait(s, "sT")
                    for (bi, bap), c in zip(taken, grp):
                        s.activation(out=LNXT[:, c, 128 * i:128 * (i + 1)], in_=bap[:, 0:128],
                                     func=ACT.Copy)
                        s.activation(out=xtf[:, c, :], in_=bap[:, 0:128], func=ACT.Copy)
                    vv = P.incl(s, "sA")
                    banks.set_free_at([bi for bi, _ in taken], "sA", vv)
                P.sec("scalar", s)

            take = banks.take(3)

            def t2(t, i=i, take=take):
                banks.tensor_wait_free(t, [b for b, _ in take])
                P.wait(t, "sA")
                inst = None
                for c in range(NDC):
                    t.matmul(out=take[0][1][0:DIX, 0:128], lhsT=wiq[:, c, :],
                             rhs=xtf[:, c, :], start=(c == 0), stop=(c == NDC - 1))
                for c in range(NDC):
                    t.matmul(out=take[1][1][0:DIX, 0:128], lhsT=wik[:, c, :],
                             rhs=xtf[:, c, :], start=(c == 0), stop=(c == NDC - 1))
                for c in range(NDC):
                    inst = t.matmul(out=take[2][1][0:HI, 0:128], lhsT=wiw[:, c, :],
                                    rhs=xtf[:, c, :], start=(c == 0), stop=(c == NDC - 1))
                P.inc(inst, "sT")
            P.sec("tensor", t2)

            def s2(s, i=i, take=take):
                P.wait(s, "sT")
                s.activation(out=qI[0:DIX, 128 * i:128 * (i + 1)], in_=take[0][1][0:DIX, 0:128],
                             func=ACT.Copy)
                s.drain()
                s.activation(out=kI[0:DIX, 128 * i:128 * (i + 1)], in_=take[1][1][0:DIX, 0:128],
                             func=ACT.Copy)
                s.drain()
                s.activation(out=wsg[:, 128 * i:128 * (i + 1)], in_=take[2][1][0:HI, 0:128],
                             func=ACT.Sigmoid)
                vv = P.incl(s, "sA")
                banks.set_free_at([b for b, _ in take], "sA", vv)
            P.sec("scalar", s2)

            def v3(v):
                P.wait(v, "sT")
                P.incl(v, "sV")
            P.sec("vector", v3)

        for i in range(NTT):
            take = banks.take(1)

            def t(t, i=i, take=take):
                banks.tensor_wait_free(t, [b for b, _ in take])
                P.wait(t, "sA")
                inst = t.transpose(out=take[0][1][:, 0:HI],
                                   in_=wsg[:, 128 * i:128 * (i + 1)],
                                   identity=P.ident_f[0:HI, 0:HI])
                P.inc(inst, "sT")
            P.sec("tensor", t)

            def v(v, i=i, take=take):
                P.wait(v, "sT")
                v.tensor_copy(out=wT[:, i, :], in_=take[0][1][:, 0:HI])
                vv = P.incl(v, "sV")
                banks.set_free_at([take[0][0]], "sV", vv)
            P.sec("vector", v)

        take = banks.take(2)

        def tb(t, take=take):
            banks.tensor_wait_free(t, [b for b, _ in take])
            P.wait(t, "sD")
            P.wait(t, "sV")
            t.matmul(out=take[0][1][:, 0:HI], lhsT=P.ones_col[:], rhs=gbf[:],
                     start=True, stop=True)
            inst = t.matmul(out=take[1][1][:, 0:H], lhsT=P.ones_col[:], rhs=hibf[:],
                            start=True, stop=True)
            P.inc(inst, "sT")
        P.sec("tensor", tb)

        def sb_(s, take=take):
            P.wait(s, "sT")
            s.activation(out=gbb[:, :], in_=take[0][1][:, 0:HI], func=ACT.Copy)
            s.drain()
            s.activation(out=kfacb[:, :], in_=take[1][1][:, 0:H], func=ACT.Sigmoid)
            vv = P.incl(s, "sA")
            banks.set_free_at([b for b, _ in take], "sA", vv)
        P.sec("scalar", sb_)

        def vk(v):
            P.wait(v, "sA")
            v.tensor_tensor(out=kfacb[:, :], in0=kfacb[:, :], in1=kfacb[:, :], op=ALU.mult)
            P.incl(v, "sV")
        P.sec("vector", vk)

    with ExitStack() as pes2:
        psl = lambda n, sh, dt=F32: pes2.enter_context(P.nc.sbuf_tensor(P.uid(n), sh, dt))
        wv = emit_weight_load_cast(P, W["l1_wv"], "l1wv", DM, DM, es=pes2)
        wgv = emit_weight_load_cast(P, W["l1_wgv"], "l1wgv", DM, DM, es=pes2)
        vstg = psl("l1_vstg", [128, 4, DM], BF16)
        gtmp = psl("l1_gtmp", [128, 512])
        for i in range(NTT):
            ii = i % 4
            for seg in range(DM // 512):
                take = banks.take(2)

                def t(t, i=i, seg=seg, take=take):
                    banks.tensor_wait_free(t, [b for b, _ in take])
                    P.wait(t, "sV")
                    P.wait(t, "sA")
                    if i >= 4:
                        P.wait(t, "sD")
                    for c in range(NDC):
                        t.matmul(out=take[0][1][:, 0:512],
                                 lhsT=LNXT[:, c, 128 * i:128 * (i + 1)],
                                 rhs=wv[:, c, 512 * seg:512 * (seg + 1)],
                                 start=(c == 0), stop=(c == NDC - 1))
                    inst = None
                    for c in range(NDC):
                        inst = t.matmul(out=take[1][1][:, 0:512],
                                        lhsT=LNXT[:, c, 128 * i:128 * (i + 1)],
                                        rhs=wgv[:, c, 512 * seg:512 * (seg + 1)],
                                        start=(c == 0), stop=(c == NDC - 1))
                    P.inc(inst, "sT")
                P.sec("tensor", t)

                def s(s, take=take):
                    P.wait(s, "sT")
                    s.activation(out=gtmp[:, :], in_=take[1][1][:, 0:512], func=ACT.Sigmoid)
                    P.incl(s, "sA")
                P.sec("scalar", s)

                def v(v, ii=ii, seg=seg, take=take):
                    P.wait(v, "sA")
                    v.tensor_tensor(out=vstg[:, ii, 512 * seg:512 * (seg + 1)],
                                    in0=take[0][1][:, 0:512], in1=gtmp[:, :], op=ALU.mult)
                    vv = P.incl(v, "sV")
                    banks.set_free_at([b for b, _ in take], "sV", vv)
                P.sec("vector", v)
            if ii == 3 or i == NTT - 1:
                def so(e, i=i, ii=ii):
                    P.wait(e, "sD")
                    P.wait(e, "sV")
                    P.inc(e.dma_start(out=v_h[:, i - ii:i + 1, :], in_=vstg[:, 0:ii + 1, :]),
                          "sD", 16)
                P.sec("sync", so)
        P.dma_wait("vector", "sD")

    with ExitStack() as pes3:
        psl = lambda n, sh, dt=F32: pes3.enter_context(P.nc.sbuf_tensor(P.uid(n), sh, dt))
        COS, SIN = emit_rope_tables(P, 16, np.log(10000.0) / 16.0, "l1rope", es=pes3)
        wq = emit_weight_load_cast(P, W["l1_wq"], "l1wq", DM, DM, es=pes3)
        wk = emit_weight_load_cast(P, W["l1_wk"], "l1wk", DM, DM, es=pes3)
        TSEG = min(1024, T)
        raw = psl("l1_raw", [128, 2, TSEG])
        rot = psl("l1_rot", [128, 2, TSEG])
        gout = psl("l1_gout", [128, 2, TSEG], BF16)
        for (wmat, dsth) in ((wq, q_h), (wk, k_h)):
            for g2 in range(H // 4):
                for t0 in range(0, T, TSEG):
                    for half in range(2):
                        m = 2 * g2 + half
                        for q0 in range(0, TSEG, 512):
                            take = banks.take(1)

                            def t(t, m=m, q0=q0, t0=t0, take=take, wmat=wmat):
                                banks.tensor_wait_free(t, [b for b, _ in take])
                                P.wait(t, "sV")
                                P.wait(t, "sD")
                                inst = None
                                for c in range(NDC):
                                    inst = t.matmul(out=take[0][1][:, 0:512],
                                                    lhsT=wmat[:, c, 128 * m:128 * (m + 1)],
                                                    rhs=LNXT[:, c, t0 + q0:t0 + q0 + 512],
                                                    start=(c == 0), stop=(c == NDC - 1))
                                P.inc(inst, "sT")
                            P.sec("tensor", t)

                            def s(s, half=half, q0=q0, take=take):
                                P.wait(s, "sT")
                                s.activation(out=raw[:, half, q0:q0 + 512],
                                             in_=take[0][1][:, 0:512], func=ACT.Copy)
                                vv = P.incl(s, "sA")
                                banks.set_free_at([take[0][0]], "sA", vv)
                            P.sec("scalar", s)

                    def v(v, t0=t0):
                        P.wait(v, "sA")
                        cs_ = COS[:, t0:t0 + TSEG]
                        sn_ = SIN[:, t0:t0 + TSEG]
                        v.tensor_tensor(out=rot[:, 0, :], in0=raw[:, 0, :], in1=cs_, op=ALU.mult)
                        v.tensor_tensor(out=rot[:, 1, :], in0=raw[:, 0, :], in1=sn_, op=ALU.mult)
                        v.drain()
                        v.tensor_tensor(out=raw[:, 0, :], in0=raw[:, 1, :], in1=sn_, op=ALU.mult)
                        v.tensor_tensor(out=raw[:, 1, :], in0=raw[:, 1, :], in1=cs_, op=ALU.mult)
                        v.drain()
                        v.tensor_tensor(out=gout[:, 0, :], in0=rot[:, 0, :], in1=raw[:, 0, :], op=ALU.subtract)
                        v.tensor_tensor(out=gout[:, 1, :], in0=rot[:, 1, :], in1=raw[:, 1, :], op=ALU.add)
                        P.incl(v, "sV")
                    P.sec("vector", v)

                    def so2(e, g2=g2, t0=t0, dsth=dsth):
                        P.wait(e, "sD")
                        P.wait(e, "sV")
                        P.inc(e.dma_start(out=dsth[:, 2 * g2:2 * g2 + 2, t0:t0 + TSEG],
                                          in_=gout[:, :, :]), "sD", 16)
                    P.sec("sync", so2)
                    P.dma_wait("vector", "sD")
                    P.dma_wait("scalar", "sD")

    of_h = P.dram_tmp(P.uid("l1of"), [128, NDC, T], BF16)
    ofst = asl_("l1_ofst", [128, 2, 512], BF16)
    qg = asl_("l1_qg", [128, 2, T], BF16)
    kg = asl_("l1_kg", [128, 2, T], BF16)
    vext = asl_("l1_vext", [128, NTT, 65], BF16)
    mqk = asl_("l1_m", [128, 4, T])
    junk = asl_("l1_junk", [128, T], BF16)
    junkA = junk
    mT = asl_("l1_mt", [128, 2, 512])
    tb2 = asl_("l1_tb", [128, 512])
    geb = asl_("l1_ge", [128, 512], BF16)
    epb = asl_("l1_ep", [128, 2, 512], BF16)
    kt = asl_("l1_kt", [128, 4, 4])
    kti32 = asl_("l1_kti", [128, 16], F32)
    tau = asl_("l1_tau", [128, 4, 4])
    lo = asl_("l1_lo", [128, 16])
    hib2 = asl_("l1_hi", [128, 16])
    cnt = asl_("l1_cnt", [128, 4, 4])
    ge16 = asl_("l1_ge16", [128, 16])
    sums = asl_("l1_sums", [128, 4, 2])
    var0 = asl_("l1_var", [128, 4])
    w2c = asl_("l1_w2", [128, 4])
    taur = asl_("l1_taur", [1, 512])
    pvs = asl_("l1_pvs", [65, 512])
    rsum = taur
    inv8 = 1.0 / np.sqrt(64.0)

    for hi in range(HI):
        r0 = 32 * hi
        def li(e, hi=hi):
            P.wait(e, "sD")
            for sname in ("sT", "sV", "sA"):
                P.wait(e, sname)
            P.inc(e.dma_start(out=qg[:, :, :], in_=q_h[:, 2 * hi:2 * hi + 2, :]), "sD", 16)
            P.wait(e, "sD")
            P.inc(e.dma_start(out=kg[:, :, :], in_=k_h[:, 2 * hi:2 * hi + 2, :]), "sD", 16)
        P.sec("sync", li)
        P.dma_wait("tensor", "sD")
        P.dma_wait("vector", "sD")

        for qb in range(NQB):
            jmax = (qb + 1) * QB
            for qt in range(4):
                i = 4 * qb + qt
                jlen = 128 * (i + 1)
                for j0 in range(0, jlen, 512):
                    jn = min(512, jlen - j0)
                    take = banks.take(1)

                    def t(t, i=i, j0=j0, jn=jn, take=take, hi=hi):
                        banks.tensor_wait_free(t, [b for b, _ in take])
                        P.wait(t, "sV")
                        inst = t.matmul(out=take[0][1][:, 0:jn],
                                        lhsT=qI[r0:r0 + 32, 128 * i:128 * (i + 1)],
                                        rhs=kI[r0:r0 + 32, j0:j0 + jn],
                                        start=True, stop=True, tile_position=(r0, 0))
                        P.inc(inst, "sT")
                    P.sec("tensor", t)

                    def s(s, qt=qt, j0=j0, jn=jn, take=take, hi=hi):
                        P.wait(s, "sT")
                        s.activation(out=mqk[:, qt, j0:j0 + jn], in_=take[0][1][:, 0:jn],
                                     func=ACT.Sigmoid, bias=gbb[:, hi:hi + 1])
                        vv = P.incl(s, "sA")
                        banks.set_free_at([take[0][0]], "sA", vv)
                    P.sec("scalar", s)

                def vm(v, qt=qt, i=i, jlen=jlen):
                    P.wait(v, "sA")
                    v.tensor_tensor(out=mqk[:, qt, jlen - 128:jlen],
                                    in0=mqk[:, qt, jlen - 128:jlen], in1=P.tri_f[:, :],
                                    op=ALU.mult)
                    P.incl(v, "sV")
                P.sec("vector", vm)

            for qt in range(4):
                i = 4 * qb + qt
                jlen = 128 * (i + 1)

                def v1(v, qt=qt, jlen=jlen):
                    v.tensor_scalar(out=junk[:, 0:jlen], in0=mqk[:, qt, 0:jlen],
                                    scalar1=1.0, scalar2=0.0, op0=ALU.mult, op1=ALU.add,
                                    accum_out=sums[:, qt, 0:1])
                    P.incl(v, "sV")
                P.sec("vector", v1)

                def s1(s, qt=qt, jlen=jlen):
                    P.wait(s, "sV")
                    s.activation(out=junkA[:, 0:jlen], in_=mqk[:, qt, 0:jlen],
                                 func=ACT.Square, accum_out=sums[:, qt, 1:2])
                    P.incl(s, "sA")
                P.sec("scalar", s1)

            def v2(v, qb=qb, hi=hi):
                P.wait(v, "sA")
                for qt in range(4):
                    v.tensor_scalar(out=var0[:, qt:qt + 1], in0=sums[:, qt, 0:1],
                                    scalar1=1.0 / T, scalar2=None, op0=ALU.mult)
                v.drain()
                v.tensor_tensor(out=var0[:, :], in0=var0[:, :], in1=var0[:, :], op=ALU.mult)
                v.drain()
                for qt in range(4):
                    v.scalar_tensor_tensor(out=var0[:, qt:qt + 1], in0=sums[:, qt, 1:2],
                                           scalar=1.0 / T, in1=var0[:, qt:qt + 1],
                                           op0=ALU.mult, op1=ALU.subtract)
                v.drain()
                for qt in range(4):
                    v.tensor_tensor(out=w2c[:, qt:qt + 1], in0=wT[:, 4 * qb + qt, hi:hi + 1],
                                    in1=wT[:, 4 * qb + qt, hi:hi + 1], op=ALU.mult)
                v.drain()
                v.tensor_tensor(out=w2c[:, :], in0=w2c[:, :], in1=var0[:, :], op=ALU.mult)
                v.drain()
                for h4 in range(4):
                    v.tensor_scalar(out=kt[:, :, h4], in0=w2c[:, :],
                                    scalar1=kfacb[:, 4 * hi + h4:4 * hi + h4 + 1],
                                    scalar2=512.0, op0=ALU.mult, op1=ALU.mult)
                P.incl(v, "sV")
            P.sec("vector", v2)

            def v4(v):
                ktflat = kt[:, :, :].rearrange("p a b -> p (a b)")
                v.tensor_scalar(out=kti32[:, :], in0=ktflat, scalar1=RND_C,
                                scalar2=None, op0=ALU.add)
                v.drain()
                v.tensor_scalar(out=kti32[:, :], in0=kti32[:, :], scalar1=RND_C,
                                scalar2=None, op0=ALU.subtract)
                v.drain()
                v.tensor_tensor(out=ge16[:, :], in0=kti32[:, :], in1=ktflat,
                                op=ALU.is_gt)
                v.drain()
                v.tensor_tensor(out=kti32[:, :], in0=kti32[:, :], in1=ge16[:, :],
                                op=ALU.subtract)
                v.drain()
                v.tensor_scalar(out=ktflat, in0=kti32[:, :],
                                scalar1=32.0, scalar2=1024.0, op0=ALU.max, op1=ALU.min)
                P.incl(v, "sV")
            P.sec("vector", v4)

            def v5(v):
                for qt in range(4):
                    v.memset(mqk[:, qt, 0:4], 1.0)
                v.drain()
                v.memset(lo[:, :], 0.0)
                v.memset(hib2[:, :], 1.0001)
                P.incl(v, "sV")
            P.sec("vector", v5)

            for it in range(n_iter):
                def vt(v):
                    v.tensor_tensor(out=tau[:, :, :].rearrange("p a b -> p (a b)"),
                                    in0=lo[:, :], in1=hib2[:, :], op=ALU.add)
                    v.drain()
                    v.tensor_scalar(out=tau[:, :, :].rearrange("p a b -> p (a b)"),
                                    in0=tau[:, :, :].rearrange("p a b -> p (a b)"),
                                    scalar1=0.5, scalar2=None, op0=ALU.mult)
                    P.incl(v, "sV")
                P.sec("vector", vt)

                for qt in range(4):
                    jlen = 128 * (4 * qb + qt + 1)
                    for h4 in range(4):
                        def vc(v, qt=qt, h4=h4, jlen=jlen):
                            v.tensor_scalar(out=junk[:, 0:jlen], in0=mqk[:, qt, 0:jlen],
                                            scalar1=tau[:, qt, h4:h4 + 1], scalar2=0.0,
                                            op0=ALU.is_ge, op1=ALU.add,
                                            accum_out=cnt[:, qt, h4:h4 + 1])
                            P.incl(v, "sV", barrier=False)
                        P.sec("vector", vc)

                def vu(v):
                    v.drain()
                    v.tensor_tensor(out=ge16[:, :], in0=cnt[:, :, :].rearrange("p a b -> p (a b)"),
                                    in1=kt[:, :, :].rearrange("p a b -> p (a b)"), op=ALU.is_ge)
                    v.drain()
                    v.tensor_tensor(out=cnt[:, :, :].rearrange("p a b -> p (a b)"),
                                    in0=tau[:, :, :].rearrange("p a b -> p (a b)"),
                                    in1=lo[:, :], op=ALU.subtract)
                    v.drain()
                    v.tensor_tensor(out=cnt[:, :, :].rearrange("p a b -> p (a b)"),
                                    in0=cnt[:, :, :].rearrange("p a b -> p (a b)"),
                                    in1=ge16[:, :], op=ALU.mult)
                    v.drain()
                    v.tensor_tensor(out=lo[:, :], in0=lo[:, :],
                                    in1=cnt[:, :, :].rearrange("p a b -> p (a b)"), op=ALU.add)
                    v.drain()
                    v.tensor_tensor(out=cnt[:, :, :].rearrange("p a b -> p (a b)"),
                                    in0=tau[:, :, :].rearrange("p a b -> p (a b)"),
                                    in1=hib2[:, :], op=ALU.subtract)
                    v.drain()
                    v.tensor_scalar(out=ge16[:, :], in0=ge16[:, :], scalar1=-1.0,
                                    scalar2=-1.0, op0=ALU.mult, op1=ALU.subtract)
                    v.drain()
                    v.tensor_tensor(out=cnt[:, :, :].rearrange("p a b -> p (a b)"),
                                    in0=cnt[:, :, :].rearrange("p a b -> p (a b)"),
                                    in1=ge16[:, :], op=ALU.mult)
                    v.drain()
                    v.tensor_tensor(out=hib2[:, :], in0=hib2[:, :],
                                    in1=cnt[:, :, :].rearrange("p a b -> p (a b)"), op=ALU.add)
                    P.incl(v, "sV")
                P.sec("vector", vu)

            NKC = jmax // 128
            for h4 in range(4):
                h = 4 * hi + h4
                def sr(e, qb=qb, h4=h4):
                    P.wait(e, "sD")
                    P.wait(e, "sV")
                    P.inc(e.dma_start(out=row_h[:, 0:QB].rearrange("o (a p) -> p (a o)", p=128),
                                      in_=tau[:, :, h4], allow_slow_non_contiguous=True), "sD", 16)
                    P.wait(e, "sD")
                    P.inc(e.dma_start(out=taur[:, :], in_=row_h[:, 0:QB]), "sD", 16)
                P.sec("sync", sr)
                P.dma_wait("tensor", "sD")
                take = banks.take(1)

                def tt(t, take=take):
                    banks.tensor_wait_free(t, [b for b, _ in take])
                    P.wait(t, "sD")
                    inst = t.matmul(out=take[0][1][:, 0:512], lhsT=P.ones_col[:],
                                    rhs=taur[:, :], start=True, stop=True)
                    P.inc(inst, "sT")
                P.sec("tensor", tt)

                def vb(v, take=take):
                    P.wait(v, "sT")
                    v.tensor_copy(out=tb2[:, :], in_=take[0][1][:, 0:512])
                    vv = P.incl(v, "sV")
                    banks.set_free_at([take[0][0]], "sV", vv)
                P.sec("vector", vb)

                def li2(e, h=h):
                    P.wait(e, "sD")
                    P.wait(e, "sT")
                    P.inc(e.dma_start(out=vext[:, :, 0:64], in_=v_h[:, :, 64 * h:64 * (h + 1)]),
                          "sD", 16)
                P.sec("sync", li2)

                def vv0(v):
                    P.wait(v, "sD")
                    for i_ in range(NTT):
                        v.tensor_copy(out=vext[:, i_, 64:65], in_=P.ones128_b[:])
                    P.incl(v, "sV")
                P.sec("vector", vv0)

                takePV = banks.take(1, hold=True)
                vg_val = {}
                vp_val = {}
                tm_val = {}
                ts_val = {}
                takes = {}
                takes2 = {}

                def emit_tmts(kc, qb=qb, hi=hi, h4=h4, r0=r0):
                    takes[kc] = banks.take(1)

                    def tm(t, kc=kc, take=takes[kc]):
                        banks.tensor_wait_free(t, [b for b, _ in take])
                        inst = t.matmul(out=take[0][1][:, 0:512],
                                        lhsT=kI[r0:r0 + 32, 128 * kc:128 * (kc + 1)],
                                        rhs=qI[r0:r0 + 32, QB * qb:QB * (qb + 1)],
                                        start=True, stop=True, tile_position=(r0, 0))
                        tm_val[kc] = P.inc(inst, "sT")
                    P.sec("tensor", tm)
                    takes2[kc] = banks.take(1)

                    def ts_(t, kc=kc, take2=takes2[kc]):
                        banks.tensor_wait_free(t, [b for b, _ in take2])
                        rr = 32 * h4
                        t.matmul(out=take2[0][1][:, 0:512],
                                 lhsT=kg[rr:rr + 32, 0, 128 * kc:128 * (kc + 1)],
                                 rhs=qg[rr:rr + 32, 0, QB * qb:QB * (qb + 1)],
                                 start=True, stop=False, tile_position=(rr, 0))
                        inst = t.matmul(out=take2[0][1][:, 0:512],
                                        lhsT=kg[rr:rr + 32, 1, 128 * kc:128 * (kc + 1)],
                                        rhs=qg[rr:rr + 32, 1, QB * qb:QB * (qb + 1)],
                                        start=False, stop=True, tile_position=(rr, 0))
                        ts_val[kc] = P.inc(inst, "sT")
                    P.sec("tensor", ts_)

                emit_tmts(0)
                for kc in range(NKC):
                    pp = kc % 2
                    take = takes[kc]
                    take2 = takes2[kc]

                    def sm(s, kc=kc, pp=pp, take=take, hi=hi):
                        P.wait(s, "sT", tm_val[kc])
                        if kc >= 2:
                            P.wait(s, "sV", vg_val[kc - 2])
                        s.activation(out=mT[:, pp, :], in_=take[0][1][:, 0:512],
                                     func=ACT.Sigmoid, bias=gbb[:, hi:hi + 1])
                        vv = P.incl(s, "sA", barrier=False)
                        banks.set_free_at([take[0][0]], "sA", vv)
                    P.sec("scalar", sm)

                    def vg(v, kc=kc, pp=pp, qb=qb):
                        P.wait(v, "sA")
                        if kc == 0:
                            v.memset(mT[0:4, pp, :], 1.0)
                            v.drain()
                        v.tensor_tensor(out=geb[:, :], in0=mT[:, pp, :], in1=tb2[:, :],
                                        op=ALU.is_ge)
                        v.drain()
                        d = kc - 4 * qb
                        if d >= 0:
                            v.tensor_tensor(out=geb[:, :], in0=geb[:, :],
                                            in1=dtri[:, d, :], op=ALU.mult)
                            v.drain()
                        vg_val[kc] = P.incl(v, "sV")
                    P.sec("vector", vg)

                    def se(s, kc=kc, pp=pp, take2=take2):
                        P.wait(s, "sT", ts_val[kc])
                        if kc >= 2:
                            P.wait(s, "sV", vp_val[kc - 2])
                        s.activation(out=epb[:, pp, :], in_=take2[0][1][:, 0:512],
                                     func=ACT.Exp, scale=inv8)
                        vv = P.incl(s, "sA", barrier=False)
                        banks.set_free_at([take2[0][0]], "sA", vv)
                    P.sec("scalar", se)

                    def vp(v, kc=kc, pp=pp):
                        P.wait(v, "sA")
                        v.tensor_tensor(out=epb[:, pp, :], in0=epb[:, pp, :],
                                        in1=geb[:, :], op=ALU.mult)
                        vp_val[kc] = P.incl(v, "sV", barrier=False)
                    P.sec("vector", vp)

                    if kc + 1 < NKC:
                        emit_tmts(kc + 1)

                    def tpv(t, kc=kc, pp=pp, takePV=takePV, NKC=NKC):
                        if kc == 0:
                            banks.tensor_wait_free(t, [b for b, _ in takePV])
                        P.wait(t, "sV", vp_val[kc])
                        inst = t.matmul(out=takePV[0][1][0:65, 0:512], lhsT=vext[:, kc, :],
                                        rhs=epb[:, pp, :], start=(kc == 0), stop=(kc == NKC - 1))
                        P.inc(inst, "sT")
                    P.sec("tensor", tpv)

                def v5b(v, takePV=takePV):
                    P.wait(v, "sT")
                    v.tensor_copy(out=pvs[:], in_=takePV[0][1][0:65, 0:512])
                    v.drain()
                    v.reciprocal(out=rsum[:], in_=pvs[64:65, :])
                    vv = P.incl(v, "sV")
                    banks.set_free_at([b for b, _ in takePV], "sV", vv)
                P.sec("vector", v5b)
                banks.release(takePV)

                take3 = banks.take(1)

                def t3(t, take3=take3):
                    banks.tensor_wait_free(t, [b for b, _ in take3])
                    P.wait(t, "sV")
                    inst = t.matmul(out=take3[0][1][0:64, 0:512], lhsT=P.ones_col[:, 0:64],
                                    rhs=rsum[:], start=True, stop=True)
                    P.inc(inst, "sT")
                P.sec("tensor", t3)

                def v6(v, h=h, hi=hi, qb=qb, take3=take3):
                    P.wait(v, "sT")
                    lc, mr = (64 * h) // 128 - 2 * hi, (64 * h) % 128
                    v.tensor_tensor(out=ofst[mr:mr + 64, lc, :],
                                    in0=pvs[0:64, :], in1=take3[0][1][0:64, 0:512],
                                    op=ALU.mult)
                    vv = P.incl(v, "sV")
                    banks.set_free_at([take3[0][0]], "sV", vv)
                P.sec("vector", v6)

            def sof(e, hi=hi, qb=qb):
                P.wait(e, "sD")
                P.wait(e, "sV")
                P.inc(e.dma_start(out=of_h[:, 2 * hi:2 * hi + 2, QB * qb:QB * (qb + 1)],
                                  in_=ofst[:, :, :]), "sD", 16)
            P.sec("sync", sof)
            P.dma_wait("vector", "sD")

    aes.close()

    with ExitStack() as wes2:
        wgo = emit_weight_load_cast(P, W["l1_wgo"], "l1wgo", DM, DM, es=wes2)
        wo = emit_weight_load_cast(P, W["l1_wo"], "l1wo", DM, DM, es=wes2)
        gof = wes2.enter_context(P.nc.sbuf_tensor(P.uid("l1_gof"), [128, 512], BF16))
        ofs = wes2.enter_context(P.nc.sbuf_tensor(P.uid("l1_ofs"), [128, NDC, T], BF16))
        for c in range(NDC):
            def slo(e, c=c):
                P.wait(e, "sD")
                P.wait(e, "sT")
                P.wait(e, "sV")
                P.inc(e.dma_start(out=ofs[:, c, :], in_=of_h[:, c, :]), "sD", 16)
            P.sec("sync", slo)
            P.dma_wait("vector", "sD")
            for q0 in range(0, T, 512):
                take = banks.take(1)

                def t(t, c=c, q0=q0, take=take):
                    banks.tensor_wait_free(t, [b for b, _ in take])
                    P.wait(t, "sV")
                    P.wait(t, "sA")
                    inst = None
                    for c2 in range(NDC):
                        inst = t.matmul(out=take[0][1][:, 0:512],
                                        lhsT=wgo[:, c2, 128 * c:128 * (c + 1)],
                                        rhs=LNXT[:, c2, q0:q0 + 512],
                                        start=(c2 == 0), stop=(c2 == NDC - 1))
                    P.inc(inst, "sT")
                P.sec("tensor", t)

                def s(s, take=take):
                    P.wait(s, "sT")
                    s.activation(out=gof[:, :], in_=take[0][1][:, 0:512], func=ACT.Sigmoid)
                    vv = P.incl(s, "sA")
                    banks.set_free_at([take[0][0]], "sA", vv)
                P.sec("scalar", s)

                def v(v, c=c, q0=q0):
                    P.wait(v, "sA")
                    v.tensor_tensor(out=ofs[:, c, q0:q0 + 512], in0=ofs[:, c, q0:q0 + 512],
                                    in1=gof[:, :], op=ALU.mult)
                    P.incl(v, "sV")
                P.sec("vector", v)

        for i in range(NTT):
            for seg in range(DM // 512):
                take = banks.take(1)

                def t(t, i=i, seg=seg, take=take):
                    banks.tensor_wait_free(t, [b for b, _ in take])
                    P.wait(t, "sV")
                    inst = None
                    for c in range(NDC):
                        inst = t.matmul(out=take[0][1][:, 0:512],
                                        lhsT=ofs[:, c, 128 * i:128 * (i + 1)],
                                        rhs=wo[:, c, 512 * seg:512 * (seg + 1)],
                                        start=(c == 0), stop=(c == NDC - 1))
                    P.inc(inst, "sT")
                P.sec("tensor", t)

                def v7(v, i=i, seg=seg, take=take):
                    P.wait(v, "sT")
                    sl_ = slice(512 * seg, 512 * (seg + 1))
                    v.tensor_tensor(out=X[:, i, sl_], in0=X[:, i, sl_],
                                    in1=take[0][1][:, 0:512], op=ALU.add)
                    vv = P.incl(v, "sV")
                    banks.set_free_at([take[0][0]], "sV", vv)
                P.sec("vector", v7)
    es.close()


BLOCK_EMITTERS["l1a"] = emit_l1_attn
